# version 07
# baseline (speedup 1.0000x reference)
import contextlib
import math
import numpy as np
import concourse.bass as bass
import concourse.mybir as mybir
from concourse.bass import IndirectOffsetOnAxis
from concourse.bass_utils import run_bass_kernel_spmd

F32 = mybir.dt.float32
BF16 = mybir.dt.bfloat16
F8 = mybir.dt.float8e4
I32 = mybir.dt.int32
AF = mybir.ActivationFunctionType
ALU = mybir.AluOpType
AXX = mybir.AxisListType.X

S, D, NH, HD, II = 8192, 2048, 16, 128, 5632
TOK = 1024
ISH = II // 8
OSH = D // 8
N_CORES = 8
SCALE = 1.0 / math.sqrt(HD)
EXP_BIAS = -20.0
QD = 224.0
IQD = 1.0 / QD
DQF = 4.0 / (448.0 * 448.0)
EPS = 1e-5
IND = 1.0 / D

ENGS = ("sync", "vector", "scalar", "tensor", "gpsimd")


class Ctr:
    def __init__(self, sem):
        self.sem, self.v = sem, 0


class Sched:
    def __init__(self, nc):
        self.nc = nc
        self.ops = {k: [] for k in ENGS}

    def wait(self, eng, ctr, val=None):
        v = ctr.v if val is None else val
        if not v or v <= 0:
            return
        self.ops[eng].append(lambda e, s=ctr.sem, v=v: e.wait_ge(s, v))

    def op(self, eng, build, ctr=None, n=1):
        if ctr is None:
            self.ops[eng].append(lambda e, b=build: b(e))
            return None
        ctr.v += n
        self.ops[eng].append(lambda e, b=build, s=ctr.sem, m=n: b(e).then_inc(s, m))
        return ctr.v

    def dma(self, eng, build, ctr):
        return self.op(eng, build, ctr, 16)

    def dma_nc(self, eng, out, in_, ctr):
        nc = self.nc
        def b(e):
            with nc.allow_non_contiguous_dma(reason="tiny"):
                return e.dma_start(out=out, in_=in_)
        return self.op(eng, b, ctr, 16)

    def flush(self, blk):
        for name in ENGS:
            fns = self.ops[name]
            if not fns:
                continue
            getattr(blk, name)(lambda e, fns=fns: [f(e) for f in fns])
        self.ops = {k: [] for k in ENGS}


def build_nc(dbg_phase=0):
    nc = bass.Bass("TRN2", target_bir_lowering=False)
    dt = nc.dram_tensor

    xt = dt("xt", [D, S], F32, kind="ExternalInput").ap()
    x_own = dt("x_own", [TOK, D], F32, kind="ExternalInput").ap()
    wq = dt("wq", [OSH, D], F32, kind="ExternalInput").ap()
    wk = dt("wk", [OSH, D], F32, kind="ExternalInput").ap()
    wv = dt("wv", [OSH, D], F32, kind="ExternalInput").ap()
    b3 = dt("b3", [128, 3, 2], F32, kind="ExternalInput").ap()
    wo_t = dt("wo_t", [OSH, D], F32, kind="ExternalInput").ap()
    bo_b = dt("bo_b", [128, D], F32, kind="ExternalInput").ap()
    wg_in = dt("wg_in", [II, D], F32, kind="ExternalInput").ap()
    wu_in = dt("wu_in", [II, D], F32, kind="ExternalInput").ap()
    wd_rows = dt("wd_rows", [OSH, II], F32, kind="ExternalInput").ap()
    dn_shard = dt("dn_shard", [ISH], I32, kind="ExternalInput").ap()
    masks = dt("masks", [4, 128, 512], F32, kind="ExternalInput").ap()
    identb = dt("identb", [128, 128], BF16, kind="ExternalInput").ap()

    out_own = dt("out_own", [TOK, D], F32, kind="ExternalOutput").ap()

    q_scr = dt("q_scr", [2, 2, 128, S], BF16).ap()
    k_scr = dt("k_scr", [2, 2, 128, S], BF16).ap()
    v_scr = dt("v_scr", [2, 2, S, 128], BF16).ap()
    o_part0 = dt("o_part0", [S, D // 2], F32).ap()
    o_part1 = dt("o_part1", [S, D // 2], F32).ap()
    o_red0 = dt("o_red0", [TOK, D // 2], F32).ap()
    o_red1 = dt("o_red1", [TOK, D // 2], F32).ap()
    x1_scr = dt("x1_scr", [TOK, D], F32).ap()
    t_scr = dt("t_scr", [TOK, II], F32).ap()
    w8gu_in = dt("w8gu_in", [2, 16, 128, ISH], F8).ap()
    w8gu_full = dt("w8gu_full", [16, 16, 128, ISH], F8).ap()
    w8d_in = dt("w8d_in", [44, 128, OSH], F8).ap()
    w8d_full = dt("w8d_full", [352, 128, OSH], F8).ap()
    wsc_in = dt("wsc_in", [3, ISH], F32).ap()
    wsc_full = dt("wsc_full", [24, ISH], F32).ap()

    dbg = {}
    if dbg_phase == 1:
        for nm, shp, dty in (("d_qsc", [2, 2, 128, S], BF16), ("d_ksc", [2, 2, 128, S], BF16),
                             ("d_vsc", [2, 2, S, 128], BF16), ("d_w8g", [16, 16, 128, ISH], F8),
                             ("d_w8d", [352, 128, OSH], F8), ("d_wsc", [24, ISH], F32)):
            dbg[nm] = dt(nm, shp, dty, kind="ExternalOutput").ap()
    if dbg_phase == 2:
        dbg["d_ahi"] = dt("d_ahi", [128, 64, 256], BF16, kind="ExternalOutput").ap()
        dbg["d_alo"] = dt("d_alo", [128, 64, 256], BF16, kind="ExternalOutput").ap()
        dbg["d_opart"] = dt("d_opart", [S, D], F32, kind="ExternalOutput").ap()
        dbg["d_ored"] = dt("d_ored", [TOK, D], F32, kind="ExternalOutput").ap()
        dbg["d_x1"] = dt("d_x1", [TOK, D], F32, kind="ExternalOutput").ap()
        dbg["d_t"] = dt("d_t", [TOK, II], F32, kind="ExternalOutput").ap()
        dbg["d_mt"] = dt("d_mt", [128, 8], F32, kind="ExternalOutput").ap()
        dbg["d_qxT"] = dt("d_qxT", [128, 16, TOK], F8, kind="ExternalOutput").ap()
        dbg["d_aT"] = dt("d_aT", [128, 2, 2, 128], BF16, kind="ExternalOutput").ap()
        dbg["d_aTl"] = dt("d_aTl", [128, 2, 2, 128], BF16, kind="ExternalOutput").ap()

    st = contextlib.ExitStack()
    with st:
        cpool = st.enter_context(nc.sbuf_tensor("cpool", [128, 12], F32))
        ident_sb = st.enter_context(nc.sbuf_tensor("ident_sb", [128, 128], BF16))
        woT_h = st.enter_context(nc.sbuf_tensor("woT_h", [128, 2, D], BF16))
        woT_l = st.enter_context(nc.sbuf_tensor("woT_l", [128, 2, D], BF16))
        qxT = st.enter_context(nc.sbuf_tensor("qxT", [128, 16, TOK], F8))
        mxd_sb = st.enter_context(nc.sbuf_tensor("mxd_sb", [128, 8], F32))
        mt_sb = st.enter_context(nc.sbuf_tensor("mt_sb", [128, 8], F32))

        ps = [st.enter_context(nc.psum_tensor(f"ps{i}", [128, 512], F32)) for i in range(6)]
        pT = [st.enter_context(nc.psum_tensor(f"pT{i}", [128, 1024], BF16)) for i in range(2)]

        CONSTS = (0.0, 1.0, EXP_BIAS, SCALE, IQD, IND, EPS, -1e9, DQF)
        for i, cv in enumerate(CONSTS):
            nc.const_aps.register(F32, float(cv), cpool[:, i:i + 1])

        semnames = ("sd", "se", "sv", "sg", "sp", "sq", "cc",
                    "swa", "swb", "swd", "sga", "sgb",
                    "sxa", "sxb", "sb3", "skv", "sqa", "sqb", "sms",
                    "sw6a", "sw6b", "st6a", "st6b", "sto", "sout", "sx6a", "sx6b",
                    "sxc", "sxd", "sgc", "sgd")
        sems = {n: Ctr(st.enter_context(nc.semaphore(n))) for n in semnames}
        sd, se, sv, sg = sems["sd"], sems["se"], sems["sv"], sems["sg"]
        sp, sq, cc = sems["sp"], sems["sq"], sems["cc"]
        swa, swb, swd = sems["swa"], sems["swb"], sems["swd"]
        sga, sgb = sems["sga"], sems["sgb"]

        marks_b0 = {}
        stA = contextlib.ExitStack()
        st.enter_context(stA)
        wqkvT_h = stA.enter_context(nc.sbuf_tensor("wqkvT_hS", [128, 3, 16, OSH], BF16))
        wqkvT_l = stA.enter_context(nc.sbuf_tensor("wqkvT_lS", [128, 3, 16, OSH], BF16))
        idx_sb = stA.enter_context(nc.sbuf_tensor("idx_sb", [128, 6], I32))
        gw = stA.enter_context(nc.sbuf_tensor("gw", [128, 4, D], F32))
        gm = stA.enter_context(nc.sbuf_tensor("gm", [128, 6], F32))
        g8 = stA.enter_context(nc.sbuf_tensor("g8", [128, 2, D], F8))
        g8b = stA.enter_context(nc.sbuf_tensor("g8b", [128, 2, D], BF16))
        gev = stA.enter_context(nc.sbuf_tensor("gev", [128, 4, 128], F8))
        wsc_sb = stA.enter_context(nc.sbuf_tensor("wsc_sb", [128, 3, 6], F32))
        zpad = stA.enter_context(nc.sbuf_tensor("zpad", [1, 448], F32))
        wd_sb = stA.enter_context(nc.sbuf_tensor("wd_sb", [128, 1, II], F32))
        wd8 = stA.enter_context(nc.sbuf_tensor("wd8", [128, 1, II], F8))
        wd8b = stA.enter_context(nc.sbuf_tensor("wd8b", [128, 1, II], BF16))
        with (
            nc.sbuf_tensor("wrow", [128, 2, D], F32) as wrow,
            nc.sbuf_tensor("wsp_h", [128, 2, D], BF16) as wsp_h,
            nc.sbuf_tensor("wsp_l", [128, 2, D], BF16) as wsp_l,
            nc.Block("b0") as blk,
        ):
            sch = Sched(nc)
            sch.dma("sync", lambda e: e.dma_start(out=ident_sb[:, :], in_=identb), sd)
            sch.dma_nc("sync", idx_sb[:, 0:5],
                       dn_shard[0:640].rearrange("(t p) -> p t", p=128), sd)
            sch.dma_nc("sync", idx_sb[0:64, 5:6],
                       dn_shard[640:704].rearrange("(p o) -> p o", o=1), sd)
            m_ident = m_idx = sd.v
            for i, cv in enumerate(CONSTS):
                sch.op("vector", lambda e, i=i, cv=cv: e.memset(cpool[:, i:i + 1], float(cv)))
            m_const = sch.op("vector", lambda e: e.engine_nop(), sv)

            evac_mark = {}
            n = 0
            prev_split = {0: None, 1: None}
            prev_last_tr = None
            for wi, wap in enumerate((wq, wk, wv, wo_t)):
                him, lom, wload = {}, {}, {}
                for t in range(2):
                    if prev_split[t] is not None:
                        sch.wait("sync", se, prev_split[t][0])
                        sch.wait("sync", sv, prev_split[t][1])
                    wload[t] = sch.dma(
                        "sync", lambda e, t=t, wap=wap: e.dma_start(
                            out=wrow[:, t:t + 1, :],
                            in_=wap[128 * t:128 * (t + 1), :].rearrange("p d -> p () d")),
                        (swa, swb)[t])
                for t in range(2):
                    dsth = woT_h[:, t, :] if wi == 3 else wsp_h[:, t, :]
                    dstl = woT_l[:, t, :] if wi == 3 else wsp_l[:, t, :]
                    sch.wait("scalar", (swa, swb)[t], wload[t])
                    if wi > 0 and wi < 4 and prev_last_tr is not None and wi != 3:
                        sch.wait("scalar", sg, prev_last_tr)
                    him[t] = sch.op("scalar", lambda e, d=dsth, t=t: e.copy(d, wrow[:, t, :]), se)
                    sch.wait("vector", se, him[t])
                    lom[t] = sch.op("vector", lambda e, dl=dstl, dh=dsth, t=t:
                                    e.scalar_tensor_tensor(dl, wrow[:, t, :], 1.0, dh,
                                                           ALU.mult, ALU.subtract), sv)
                    prev_split[t] = (him[t], lom[t])
                if wi == 3:
                    break
                for hl in range(2):
                    srcT = (wsp_h, wsp_l)[hl]
                    dst = (wqkvT_h, wqkvT_l)[hl]
                    for t in range(2):
                        for f in range(16):
                            if n == 0:
                                sch.wait("tensor", sd, m_ident)
                            sch.wait("tensor", (se, sv)[hl], (him, lom)[hl][t])
                            if n >= 2:
                                sch.wait("tensor", sv, evac_mark[n - 2])
                            mtr = sch.op("tensor", lambda e, srcT=srcT, t=t, f=f, n=n:
                                         e.transpose(pT[n % 2][:, 0:128],
                                                     srcT[:, t, 128 * f:128 * (f + 1)],
                                                     ident_sb[:, :]), sg)
                            sch.wait("vector", sg, mtr)
                            evac_mark[n] = sch.op(
                                "vector", lambda e, dst=dst, wi=wi, f=f, t=t, n=n:
                                e.tensor_copy(dst[:, wi, f, 128 * t:128 * (t + 1)],
                                              pT[n % 2][:, 0:128]), sv)
                            n += 1
                    prev_last_tr = sg.v
            m_awT = evac_mark[n - 1]
            marks_b0["awT"] = m_awT

            marks_b0["const"] = m_const
            marks_b0["idx"] = m_idx
            sch.flush(blk)

        with (
            nc.sbuf_tensor("b3_sb", [128, 3, 2], F32) as b3_sb,
            nc.sbuf_tensor("xf", [128, 4, 512], F32) as xf,
            nc.sbuf_tensor("xh", [128, 2, 512], BF16) as xh,
            nc.sbuf_tensor("xl", [128, 2, 512], BF16) as xl,
            nc.sbuf_tensor("qkst", [128, 6, 512], F32) as qkst,
            nc.sbuf_tensor("qkh", [128, 6, 512], BF16) as qkh,
            nc.sbuf_tensor("qkl", [128, 6, 512], BF16) as qkl,
            nc.sbuf_tensor("vev", [128, 4, 128], BF16) as vev,
            nc.Block("b1") as blk,
        ):
            sch = Sched(nc)
            sch.wait("tensor", sv, marks_b0["awT"])
            m_b3 = sch.dma("sync", lambda e: e.dma_start(out=b3_sb[:, :, :], in_=b3), sems["sb3"])
            xh_m, xl_m, mm_fc = {}, {}, {}
            prev = {}
            ntp = 0
            tevac2, tdma2 = {}, {}
            tile_rows = [128] * 5 + [64]
            tile_off = [0, 128, 256, 384, 512, 640]
            q8m, rowm, b16m, gatherm, tevac, tdma = {}, {}, {}, {}, {}, {}
            ntp_h = [0]

            def emit_prep(it):
                ntp = ntp_h[0]
                gi, ti = it // 6, it % 6
                nr = tile_rows[ti]
                wfull = (wg_in, wu_in)[gi]
                g4 = (sga, sgb, sems["sgc"], sems["sgd"])[it % 4]
                if it == 0:
                    sch.wait("gpsimd", sd, marks_b0["idx"])
                if it >= 4:
                    sch.wait("gpsimd", se, q8m[it - 4])
                    sch.wait("gpsimd", sv, rowm[it - 4])
                gatherm[it] = sch.dma(
                    "gpsimd", lambda e, nr=nr, it=it, ti=ti, wfull=wfull: e.indirect_dma_start(
                        out=gw[0:nr, it % 4, :], out_offset=None,
                        in_=wfull[:, :],
                        in_offset=IndirectOffsetOnAxis(ap=idx_sb[0:nr, ti:ti + 1], axis=0)),
                    g4)
                sch.wait("vector", g4, gatherm[it])
                if it >= 2:
                    sch.wait("vector", se, q8m[it - 2])
                rowm[it] = sch.op("vector", lambda e, nr=nr, it=it: e.reduce_max(
                    gm[0:nr, it % 2:it % 2 + 1], gw[0:nr, it % 4, :], axis=AXX,
                    apply_absolute_value=True), sv)
                sch.wait("vector", sv, rowm[it])
                sch.op("vector", lambda e, nr=nr, gi=gi, ti=ti, it=it: e.tensor_copy(
                    wsc_sb[0:nr, gi, ti:ti + 1], gm[0:nr, it % 2:it % 2 + 1]), sv)
                sch.wait("vector", sv, rowm[it])
                m1 = sch.op("vector", lambda e, nr=nr, it=it: e.tensor_scalar_mul(
                    gm[0:nr, 4 + it % 2:5 + it % 2], gm[0:nr, it % 2:it % 2 + 1], IQD), sv)
                sch.wait("vector", sv, m1)
                mrs = sch.op("vector", lambda e, nr=nr, it=it: e.reciprocal(
                    gm[0:nr, 2 + it % 2:3 + it % 2], gm[0:nr, 4 + it % 2:5 + it % 2]), sv)
                if it == 0:
                    sch.wait("scalar", sv, marks_b0["const"])
                sch.wait("scalar", sv, mrs)
                q8m[it] = sch.op("scalar", lambda e, nr=nr, it=it: e.activation(
                    g8[0:nr, it % 2, :], gw[0:nr, it % 4, :], AF.Copy, bias=0.0,
                    scale=gm[0:nr, 2 + it % 2:3 + it % 2]), se)
                sch.wait("vector", se, q8m[it])
                b16m[it] = sch.op("vector", lambda e, nr=nr, it=it: e.tensor_copy(
                    g8b[0:nr, it % 2, :], g8[0:nr, it % 2, :]), sv)
                for f in range(16):
                    sch.wait("tensor", sv, b16m[it])
                    if ntp < 2:
                        sch.wait("tensor", sv, sv.v)
                    else:
                        sch.wait("tensor", se, tevac[ntp - 2])
                    mtr = sch.op("tensor", lambda e, nr=nr, it=it, f=f, ntp=ntp:
                                 e.transpose(pT[ntp % 2][:, 0:nr],
                                             g8b[0:nr, it % 2, 128 * f:128 * (f + 1)],
                                             ident_sb[0:nr, 0:nr]), sg)
                    sch.wait("scalar", sg, mtr)
                    if ntp >= 4:
                        sch.wait("scalar", sq, sq.v)
                    tevac[ntp] = sch.op("scalar", lambda e, nr=nr, ntp=ntp: e.copy(
                        gev[:, ntp % 4, 0:nr], pT[ntp % 2][:, 0:nr]), se)
                    sch.wait("gpsimd", se, tevac[ntp])
                    tdma[ntp] = sch.dma("gpsimd", lambda e, gi=gi, f=f, ti=ti, nr=nr, ntp=ntp:
                                        e.dma_start(
                                            out=w8gu_in[gi, f, :, tile_off[ti]:tile_off[ti] + nr],
                                            in_=gev[:, ntp % 4, 0:nr]), sq)
                    ntp += 1
                ntp_h[0] = ntp

            for tc in range(16):
                for fc in range(16):
                    i = tc * 16 + fc
                    q4 = (sems["sxa"], sems["sxb"], sems["sxc"], sems["sxd"])[i % 4]
                    if i >= 4:
                        ptc, pfc = (i - 4) // 16, (i - 4) % 16
                        sch.wait("sync", sv, xl_m[(ptc, pfc)])
                    mload = sch.dma("sync", lambda e, tc=tc, fc=fc, i=i: e.dma_start(
                        out=xf[:, i % 4, :],
                        in_=xt[128 * fc:128 * (fc + 1), 512 * tc:512 * (tc + 1)]), q4)
                    sch.wait("scalar", q4, mload)
                    if i >= 2:
                        sch.wait("scalar", sg, mm_fc[i - 2])
                    xh_m[(tc, fc)] = sch.op("scalar", lambda e, fc=fc, i=i: e.copy(
                        xh[:, fc % 2, :], xf[:, i % 4, :]), se)
                    sch.wait("vector", se, xh_m[(tc, fc)])
                    if i >= 2:
                        sch.wait("vector", sg, mm_fc[i - 2])
                    xl_m[(tc, fc)] = sch.op("vector", lambda e, fc=fc, i=i: e.scalar_tensor_tensor(
                        xl[:, fc % 2, :], xf[:, i % 4, :], 1.0, xh[:, fc % 2, :],
                        ALU.mult, ALU.subtract), sv)
                    sch.wait("tensor", se, xh_m[(tc, fc)])
                    sch.wait("tensor", sv, xl_m[(tc, fc)])
                    for oi in range(6):
                        wi, hd = oi // 2, oi % 2
                        if fc == 0 and tc > 0:
                            sch.wait("tensor", se, prev[oi]["stage"])
                        for pi in range(3):
                            lw = (wqkvT_h, wqkvT_h, wqkvT_l)[pi]
                            rx = (xh, xl, xh)[pi]
                            strt = (fc == 0 and pi == 0)
                            stop = (fc == 15 and pi == 2)
                            last_of_fc = (oi == 5 and pi == 2)
                            sch.op("tensor", lambda e, lw=lw, rx=rx, wi=wi, hd=hd, fc=fc, oi=oi,
                                   strt=strt, stop=stop:
                                   e.matmul(ps[oi][:, :], lw[:, wi, fc, 128 * hd:128 * (hd + 1)],
                                            rx[:, fc % 2, :], start=strt, stop=stop),
                                   sg if (stop or last_of_fc) else None)
                            if last_of_fc:
                                mm_fc[i] = sg.v
                cur = {}
                for oi in range(6):
                    wi, hd = oi // 2, oi % 2
                    mm_stop = sg.v - (5 - oi)
                    cur[oi] = {}
                    sch.wait("scalar", sg, mm_stop)
                    if tc == 0:
                        sch.wait("scalar", sems["sb3"], m_b3)
                    if tc > 0:
                        p = prev[oi]
                        sch.wait("scalar", sv, p["lo"])
                        if wi < 2:
                            sch.wait("scalar", sq, sq.v)
                        else:
                            sch.wait("scalar", sg, p["lasttr"])
                    mstg = sch.op("scalar", lambda e, oi=oi, wi=wi, hd=hd: e.activation(
                        qkst[:, oi, :], ps[oi][:, :], AF.Identity,
                        bias=b3_sb[:, wi, hd:hd + 1], scale=1.0), se)
                    cur[oi]["stage"] = mstg
                    sch.wait("scalar", se, mstg)
                    mhi = sch.op("scalar", lambda e, oi=oi: e.copy(
                        qkh[:, oi, :], qkst[:, oi, :]), se)
                    cur[oi]["hi"] = mhi
                    sch.wait("vector", se, mhi)
                    mlo = sch.op("vector", lambda e, oi=oi: e.scalar_tensor_tensor(
                        qkl[:, oi, :], qkst[:, oi, :], 1.0, qkh[:, oi, :],
                        ALU.mult, ALU.subtract), sv)
                    cur[oi]["lo"] = mlo
                    if wi < 2:
                        scr = (q_scr, k_scr)[wi]
                        sch.wait("gpsimd", se, mhi)
                        cur[oi]["hidma"] = sch.dma(
                            "gpsimd", lambda e, scr=scr, hd=hd, tc=tc, oi=oi: e.dma_start(
                                out=scr[hd, 0, :, 512 * tc:512 * (tc + 1)], in_=qkh[:, oi, :]), sq)
                        sch.wait("gpsimd", sv, mlo)
                        cur[oi]["lodma"] = sch.dma(
                            "gpsimd", lambda e, scr=scr, hd=hd, tc=tc, oi=oi: e.dma_start(
                                out=scr[hd, 1, :, 512 * tc:512 * (tc + 1)], in_=qkl[:, oi, :]), sq)
                    else:
                        for hl in range(2):
                            mm = (mhi, mlo)[hl]
                            srcv = (qkh, qkl)[hl]
                            for j in range(4):
                                sch.wait("tensor", (se, sv)[hl], mm)
                                if tc == 0 and ntp < 2:
                                    pass
                                elif ntp >= 2:
                                    sch.wait("tensor", sv, tevac2[ntp - 2])
                                mtr = sch.op("tensor", lambda e, srcv=srcv, oi=oi, j=j, ntp=ntp:
                                             e.transpose(pT[ntp % 2][:, 0:128],
                                                         srcv[:, oi, 128 * j:128 * (j + 1)],
                                                         ident_sb[:, :]), sg)
                                cur[oi]["lasttr"] = mtr
                                sch.wait("vector", sg, mtr)
                                if ntp >= 4:
                                    sch.wait("vector", sq, sq.v)
                                tevac2[ntp] = sch.op("vector", lambda e, ntp=ntp: e.tensor_copy(
                                    vev[:, ntp % 4, :], pT[ntp % 2][:, 0:128]), sv)
                                sch.wait("gpsimd", sv, tevac2[ntp])
                                tdma2[ntp] = sch.dma(
                                    "gpsimd", lambda e, hd=hd, hl=hl, tc=tc, j=j, ntp=ntp: e.dma_start(
                                        out=v_scr[hd, hl, 512 * tc + 128 * j:512 * tc + 128 * (j + 1), :],
                                        in_=vev[:, ntp % 4, :]), sq)
                                ntp += 1
                prev = cur
                if 2 <= tc < 14:
                    emit_prep(tc - 2)


            ntp = ntp_h[0]
            prev_tr_end = None
            prev_b16 = None
            for t in range(2):
                if prev_b16 is not None:
                    sch.wait("sync", sv, prev_b16)
                    sch.wait("sync", se, se.v)
                mwl = sch.dma("sync", lambda e, t=t: e.dma_start(
                    out=wd_sb[:, 0:1, :],
                    in_=wd_rows[128 * t:128 * (t + 1), :].rearrange("p d -> p () d")), swd)
                sch.wait("vector", swd, mwl)
                mr = sch.op("vector", lambda e, t=t: e.reduce_max(
                    gm[:, 4:5], wd_sb[:, 0, :], axis=AXX, apply_absolute_value=True), sv)
                sch.wait("vector", sv, mr)
                sch.op("vector", lambda e, t=t: e.tensor_copy(
                    wsc_sb[:, 2, t:t + 1], gm[:, 4:5]), sv)
                sch.wait("vector", sv, mr)
                m1 = sch.op("vector", lambda e: e.tensor_scalar_mul(gm[:, 3:4], gm[:, 4:5], IQD), sv)
                sch.wait("vector", sv, m1)
                mrs = sch.op("vector", lambda e: e.reciprocal(gm[:, 5:6], gm[:, 3:4]), sv)
                sch.wait("scalar", sv, mrs)
                if prev_tr_end is not None:
                    sch.wait("scalar", sg, prev_tr_end)
                m8 = sch.op("scalar", lambda e, t=t: e.activation(
                    wd8[:, 0, :], wd_sb[:, 0, :], AF.Copy, bias=0.0, scale=gm[:, 5:6]), se)
                sch.wait("vector", se, m8)
                mb16 = sch.op("vector", lambda e, t=t: e.tensor_copy(wd8b[:, 0, :], wd8[:, 0, :]), sv)
                prev_b16 = mb16
                for ic in range(44):
                    sch.wait("tensor", sv, mb16)
                    if ntp >= 2:
                        sch.wait("tensor", se, tevac[ntp - 2])
                    mtr = sch.op("tensor", lambda e, ic=ic, ntp=ntp:
                                 e.transpose(pT[ntp % 2][:, 0:128],
                                             wd8b[:, 0, 128 * ic:128 * (ic + 1)],
                                             ident_sb[:, :]), sg)
                    sch.wait("scalar", sg, mtr)
                    if ntp >= 4:
                        sch.wait("scalar", sq, sq.v)
                    tevac[ntp] = sch.op("scalar", lambda e, ntp=ntp: e.copy(
                        gev[:, ntp % 4, 0:128], pT[ntp % 2][:, 0:128]), se)
                    sch.wait("gpsimd", se, tevac[ntp])
                    tdma[ntp] = sch.dma("gpsimd", lambda e, ic=ic, t=t, ntp=ntp: e.dma_start(
                        out=w8d_in[ic, :, 128 * t:128 * (t + 1)], in_=gev[:, ntp % 4, 0:128]), sq)
                    ntp += 1
                prev_tr_end = sg.v

            sch.wait("gpsimd", sv, sv.v)
            for g in range(2):
                sch.dma_nc("gpsimd", wsc_in[g, 0:640].rearrange("(t p) -> p t", p=128),
                           wsc_sb[:, g, 0:5], sq)
                sch.dma_nc("gpsimd", wsc_in[g, 640:704].rearrange("(p o) -> p o", o=1),
                           wsc_sb[0:64, g, 5:6], sq)
            sch.dma_nc("gpsimd", wsc_in[2, 0:256].rearrange("(t p) -> p t", p=128),
                       wsc_sb[:, 2, 0:2], sq)
            mz = sch.op("vector", lambda e: e.memset(zpad[:, :], 0.0), sv)
            sch.wait("gpsimd", sv, mz)
            sch.dma("gpsimd", lambda e: e.dma_start(
                out=wsc_in[2, 256:704].rearrange("(o t) -> o t", o=1), in_=zpad[:, :]), sq)
            sch.wait("gpsimd", sq, sq.v)
            rg = [list(range(N_CORES))]
            sch.op("gpsimd", lambda e: e.collective_compute(
                "AllGather", ALU.bypass, replica_groups=rg,
                ins=[w8gu_in.opt()], outs=[w8gu_full.opt()]), cc)
            sch.op("gpsimd", lambda e: e.collective_compute(
                "AllGather", ALU.bypass, replica_groups=rg,
                ins=[w8d_in.opt()], outs=[w8d_full.opt()]), cc)
            sch.op("gpsimd", lambda e: e.collective_compute(
                "AllGather", ALU.bypass, replica_groups=rg,
                ins=[wsc_in.opt()], outs=[wsc_full.opt()]), cc)

            if dbg_phase == 1:
                sch.wait("gpsimd", sq, sq.v)
                sch.wait("gpsimd", cc, 3)
                for a, b in (("d_qsc", q_scr), ("d_ksc", k_scr), ("d_vsc", v_scr),
                             ("d_w8g", w8gu_full), ("d_w8d", w8d_full), ("d_wsc", wsc_full)):
                    sch.dma("gpsimd", lambda e, a=a, b=b: e.dma_start(out=dbg[a], in_=b), sq)
                sch.wait("gpsimd", sq, sq.v)
            sch.flush(blk)

        stA.close()

        if dbg_phase == 1:
            return nc

        sqa, sqb, skv, sms = sems["sqa"], sems["sqb"], sems["skv"], sems["sms"]
        marks_b0["b1_se"] = se.v

        stB = contextlib.ExitStack()
        st.enter_context(stB)
        attn_hi = stB.enter_context(nc.sbuf_tensor("attn_hi", [128, 64, 256], BF16))
        attn_lo = stB.enter_context(nc.sbuf_tensor("attn_lo", [128, 64, 256], BF16))
        with (
            nc.sbuf_tensor("masks_sb", [128, 4, 512], F32) as masks_sb,
            nc.sbuf_tensor("kh_sb", [128, S], BF16) as kh_sb,
            nc.sbuf_tensor("kl_sb", [128, S], BF16) as kl_sb,
            nc.sbuf_tensor("vh_sb", [128, 64, 130], BF16) as vh_sb,
            nc.sbuf_tensor("vl_sb", [128, 64, 130], BF16) as vl_sb,
            nc.sbuf_tensor("qh_sb", [128, 2, 512], BF16) as qh_sb,
            nc.sbuf_tensor("ql_sb", [128, 2, 512], BF16) as ql_sb,
            nc.sbuf_tensor("p32", [128, 2, 512], F32) as p32,
            nc.sbuf_tensor("ph_sb", [128, 2, 512], BF16) as ph_sb,
            nc.sbuf_tensor("pl_sb", [128, 2, 512], BF16) as pl_sb,
            nc.sbuf_tensor("rcp", [128, 4], F32) as rcp,
            nc.Block("b2") as blk,
        ):
            sch = Sched(nc)
            sch.wait("sync", sq, sq.v)
            sch.wait("gpsimd", sq, sq.v)
            m_msk = sch.dma("sync", lambda e: e.dma_start(
                out=masks_sb[:, :, :], in_=masks.rearrange("m p q -> p m q")), sms)
            m_ones = sch.op("vector", lambda e: e.memset(vh_sb[:, :, 128:129], 1.0), sv)
            m_zcol = sch.op("vector", lambda e: e.memset(vl_sb[:, :, 128:129], 0.0), sv)

            expm, splitm, av_of_kc = {}, {}, {}
            evacm = {}
            prevmhi = {}
            smark_g = {}
            prev_head_last = None
            prev_av_end = None
            for h in range(2):
                if prev_head_last is not None:
                    sch.wait("sync", sg, prev_head_last)
                    sch.wait("gpsimd", sg, prev_head_last)
                for hl, dstk in ((0, kh_sb), (1, kl_sb)):
                    sch.dma("sync", lambda e, h=h, hl=hl, dstk=dstk: e.dma_start(
                        out=dstk[:, :], in_=k_scr[h, hl, :, :]), skv)
                for hl, dstv in ((0, vh_sb), (1, vl_sb)):
                    for ch in range(64):
                        sch.dma("gpsimd", lambda e, h=h, hl=hl, dstv=dstv, ch=ch: e.dma_start(
                            out=dstv[:, ch, 0:128],
                            in_=v_scr[h, hl, 128 * ch:128 * (ch + 1), :]), skv)
                m_kv = skv.v
                qldm = {}
                for qg in range(16):
                    sqp = (sqa, sqb)[qg % 2]
                    if qg >= 2:
                        sch.wait("sync", sg, smark_g[(h, qg - 2)])
                    elif h == 1:
                        sch.wait("sync", sg, smark_g[(0, 14 + qg)])
                    for hl, dstq in ((0, qh_sb), (1, ql_sb)):
                        qldm[(qg, hl)] = sch.dma(
                            "sync", lambda e, h=h, hl=hl, qg=qg, dstq=dstq: e.dma_start(
                                out=dstq[:, qg % 2, :],
                                in_=q_scr[h, hl, :, 512 * qg:512 * (qg + 1)]), sqp)
                    L = 4 * qg + 4
                    smark = {}

                    def emit_scores(kc, qg=qg, h=h, L=L, qldm=qldm, smark=smark):
                        if kc >= 2:
                            sch.wait("tensor", se, expm[(h, qg, kc - 2)])
                        else:
                            sch.wait("tensor", se, se.v)
                        if kc == 0:
                            sch.wait("tensor", (sqa, sqb)[qg % 2], qldm[(qg, 1)])
                            sch.wait("tensor", skv, m_kv)
                        for pi in range(3):
                            lk = (kh_sb, kh_sb, kl_sb)[pi]
                            rq = (qh_sb, ql_sb, qh_sb)[pi]
                            sch.op("tensor", lambda e, lk=lk, rq=rq, kc=kc, qg=qg, pi=pi:
                                   e.matmul(ps[kc % 2][:, :],
                                            lk[:, 128 * kc:128 * (kc + 1)],
                                            rq[:, qg % 2, :],
                                            start=(pi == 0), stop=(pi == 2)),
                                   sg if pi == 2 else None)
                        mk = ("g", sg.v)
                        if kc >= L - 4:
                            p = kc - (L - 4)
                            sch.wait("vector", sg, sg.v)
                            if qg == 0 and kc == 0 and h == 0:
                                sch.wait("vector", sms, m_msk)
                            mm = sch.op("vector", lambda e, p=p, kc=kc: e.tensor_tensor(
                                ps[kc % 2][:, :], ps[kc % 2][:, :], masks_sb[:, p, :],
                                ALU.add), sv)
                            mk = ("v", mm)
                        smark[kc] = mk

                    def emit_pstage(kc, qg=qg, h=h, smark=smark):
                        kind, mk = smark[kc]
                        sch.wait("scalar", sv if kind == "v" else sg, mk)
                        if kc >= 2:
                            sch.wait("scalar", sp, splitm[(h, qg, kc - 2)])
                        else:
                            sch.wait("scalar", sp, sp.v)
                        expm[(h, qg, kc)] = sch.op("scalar", lambda e, kc=kc: e.activation(
                            p32[:, kc % 2, :], ps[kc % 2][:, :], AF.Exp,
                            bias=EXP_BIAS, scale=SCALE), se)
                        sch.wait("vector", se, expm[(h, qg, kc)])
                        if kc >= 2:
                            sch.wait("vector", sg, av_of_kc[(h, qg, kc - 2)])
                        elif prev_av_end is not None:
                            sch.wait("vector", sg, prev_av_end)
                        mph = sch.op("vector", lambda e, kc=kc: e.tensor_copy(
                            ph_sb[:, kc % 2, :], p32[:, kc % 2, :]), sv)
                        sch.wait("vector", sv, mph)
                        splitm[(h, qg, kc)] = sch.op(
                            "vector", lambda e, kc=kc: e.scalar_tensor_tensor(
                                pl_sb[:, kc % 2, :], p32[:, kc % 2, :], 1.0,
                                ph_sb[:, kc % 2, :], ALU.mult, ALU.subtract), sp)

                    def emit_av(kc, qg=qg, h=h, L=L):
                        sch.wait("tensor", sp, splitm[(h, qg, kc)])
                        if kc == 0:
                            if evacm:
                                sch.wait("tensor", sv, max(evacm.values()))
                            else:
                                sch.wait("tensor", sv, m_zcol)
                        for qt in range(4):
                            for pi in range(3):
                                lp = (ph_sb, ph_sb, pl_sb)[pi]
                                rv = (vh_sb, vl_sb, vh_sb)[pi]
                                sch.op("tensor",
                                       lambda e, lp=lp, rv=rv, kc=kc, qt=qt, pi=pi, L=L:
                                       e.matmul(ps[2 + qt][:, 0:129],
                                                lp[:, kc % 2, 128 * qt:128 * (qt + 1)],
                                                rv[:, kc, 0:129],
                                                start=(kc == 0 and pi == 0),
                                                stop=(kc == L - 1 and pi == 2)),
                                       sg if (qt == 3 and pi == 2) else None)
                        av_of_kc[(h, qg, kc)] = sg.v

                    for kc in range(L + 2):
                        if kc < L:
                            emit_scores(kc)
                        if 1 <= kc <= L:
                            emit_pstage(kc - 1)
                        if kc >= 2:
                            emit_av(kc - 2)
                    m_av_end = av_of_kc[(h, qg, L - 1)]
                    prev_av_end = m_av_end
                    for qt in range(4):
                        ch = 4 * qg + qt
                        sch.wait("vector", sg, m_av_end)
                        if qt in prevmhi:
                            sch.wait("vector", se, prevmhi[qt])
                        mrc = sch.op("vector", lambda e, qt=qt: e.reciprocal(
                            rcp[:, qt:qt + 1], ps[2 + qt][:, 128:129]), sv)
                        sch.wait("scalar", sv, mrc)
                        mhi = sch.op("scalar", lambda e, qt=qt, ch=ch, h=h: e.activation(
                            attn_hi[:, ch, 128 * h:128 * (h + 1)], ps[2 + qt][:, 0:128],
                            AF.Copy, bias=0.0, scale=rcp[:, qt:qt + 1]), se)
                        prevmhi[qt] = mhi
                        sch.wait("vector", se, mhi)
                        evacm[qt] = sch.op(
                            "vector", lambda e, qt=qt, ch=ch, h=h: e.scalar_tensor_tensor(
                                attn_lo[:, ch, 128 * h:128 * (h + 1)], ps[2 + qt][:, 0:128],
                                rcp[:, qt:qt + 1], attn_hi[:, ch, 128 * h:128 * (h + 1)],
                                ALU.mult, ALU.subtract), sv)
                    smark_g[(h, qg)] = sg.v
                prev_head_last = sg.v
            marks_b0["attn_done"] = max(evacm.values())
            sch.flush(blk)

        with (
            nc.sbuf_tensor("aTh", [128, 64, 2, 128], BF16) as aTh,
            nc.sbuf_tensor("aTl", [128, 64, 2, 128], BF16) as aTl,
            nc.sbuf_tensor("ost", [128, 2, 512], F32) as ost,
            nc.Block("b3") as blk,
        ):
            sch = Sched(nc)
            sch.wait("tensor", sv, marks_b0["attn_done"])
            sch.wait("tensor", se, se.v)
            trm = {}
            tmk = {}
            for nn in range(257):
                if nn < 256:
                    ch, r = nn // 4, nn % 4
                    hl, jj = r // 2, r % 2
                    srca = (attn_hi, attn_lo)[hl]
                    if nn >= 2:
                        sch.wait("tensor", se, trm[nn - 2])
                    tmk[nn] = sch.op("tensor", lambda e, srca=srca, ch=ch, jj=jj, nn=nn:
                                     e.transpose(pT[nn % 2][:, 0:128],
                                                 srca[:, ch, 128 * jj:128 * (jj + 1)],
                                                 ident_sb[:, :]), sg)
                else:
                    sch.wait("tensor", se, trm[nn - 2])
                    tmk[nn] = sch.op("tensor", lambda e, nn=nn: e.transpose(
                        pT[nn % 2][:, 0:128], ident_sb[:, :], ident_sb[:, :]), sg)
                if nn >= 1:
                    k = nn - 1
                    ch, r = k // 4, k % 4
                    hl, jj = r // 2, r % 2
                    dsta = (aTh, aTl)[hl]
                    sch.wait("scalar", sg, tmk[nn])
                    trm[k] = sch.op("scalar", lambda e, dsta=dsta, ch=ch, jj=jj, k=k:
                                    e.copy(dsta[:, ch, jj, :], pT[k % 2][:, 0:128]), se)
            m_aT = trm[255]
            stm = {}
            for half in range(2):
                opart = (o_part0, o_part1)[half]
                for ch in range(64):
                    for oci in range(2):
                        oc = 2 * half + oci
                        if ch == 0 and half == 0 and oci == 0:
                            sch.wait("tensor", se, m_aT)
                        if ch >= 1:
                            sch.wait("tensor", se, stm[(ch - 1, oc)])
                        elif half == 1:
                            sch.wait("tensor", se, stm[(63, oc - 2)])
                        for jj in range(2):
                            for pi in range(3):
                                la = (aTh, aTh, aTl)[pi]
                                rw = (woT_h, woT_l, woT_h)[pi]
                                sch.op("tensor",
                                       lambda e, la=la, rw=rw, ch=ch, jj=jj, oc=oc, pi=pi:
                                       e.matmul(ps[oc][:, :], la[:, ch, jj, :],
                                                rw[:, jj, 512 * oc:512 * (oc + 1)],
                                                start=(jj == 0 and pi == 0),
                                                stop=(jj == 1 and pi == 2)),
                                       sg if (jj == 1 and pi == 2) else None)
                        mmk = sg.v
                        sch.wait("scalar", sg, mmk)
                        if ch >= 1 or half == 1:
                            sch.wait("scalar", sq, sq.v)
                        stm[(ch, oc)] = sch.op("scalar", lambda e, oc=oc: e.copy(
                            ost[:, oc % 2, :], ps[oc][:, :]), se)
                        sch.wait("gpsimd", se, stm[(ch, oc)])
                        sch.dma("gpsimd", lambda e, ch=ch, oci=oci, opart=opart: e.dma_start(
                            out=opart[128 * ch:128 * (ch + 1), 512 * oci:512 * (oci + 1)],
                            in_=ost[:, (2 * (0) + oci) % 2 if False else (2 * 0 + oci), :])
                            if False else e.dma_start(
                            out=opart[128 * ch:128 * (ch + 1), 512 * oci:512 * (oci + 1)],
                            in_=ost[:, (2 * half + oci) % 2, :]), sq)
                sch.wait("gpsimd", sq, sq.v)
                ored = (o_red0, o_red1)[half]
                sch.op("gpsimd", lambda e, opart=opart, ored=ored: e.collective_compute(
                    "ReduceScatter", ALU.add, replica_groups=[list(range(N_CORES))],
                    ins=[opart.opt()], outs=[ored.opt()]), cc)
            sch.flush(blk)

        stB.close()

        with (
            nc.sbuf_tensor("xo5", [128, 2, D], F32) as xo5,
            nc.sbuf_tensor("oo5", [128, 2, D], F32) as oo5,
            nc.sbuf_tensor("bob", [128, D], F32) as bob,
            nc.sbuf_tensor("z5", [128, 2, D], F32) as z5,
            nc.sbuf_tensor("sqd", [128, D], F32) as sqd,
            nc.sbuf_tensor("st5", [128, 2, 8], F32) as st5,
            nc.sbuf_tensor("x15", [128, 2, D], F32) as x15,
            nc.sbuf_tensor("qx8", [128, 2, D], F8) as qx8,
            nc.sbuf_tensor("qxb", [128, 2, D], BF16) as qxb,
            nc.Block("b5") as blk,
        ):
            sch = Sched(nc)
            m_bob = sch.dma("sync", lambda e: e.dma_start(out=bob[:, :], in_=bo_b), sms)
            x1m, qxbm, qxtr = {}, {}, {}
            ntp = 0
            tevac5 = {}
            for tt in range(8):
                pp = tt % 2
                sxp = (sems["sxa"], sems["sxb"])[pp]
                if tt >= 2:
                    sch.wait("sync", se, x1m[tt - 2])
                    sch.wait("sync", sv, qxbm[tt - 2])
                ldm = sch.dma("sync", lambda e, tt=tt, pp=pp: e.dma_start(
                    out=xo5[:, pp, :], in_=x_own[128 * tt:128 * (tt + 1), :]), sxp)
                if tt == 0:
                    sch.wait("sync", cc, 5)
                sch.dma("sync", lambda e, tt=tt, pp=pp: e.dma_start(
                    out=oo5[:, pp, 0:1024], in_=o_red0[128 * tt:128 * (tt + 1), :]), sxp)
                ldo = sch.dma("sync", lambda e, tt=tt, pp=pp: e.dma_start(
                    out=oo5[:, pp, 1024:2048], in_=o_red1[128 * tt:128 * (tt + 1), :]), sxp)
                sch.wait("vector", sxp, ldo)
                if tt == 0:
                    sch.wait("vector", sms, m_bob)
                if tt >= 2:
                    sch.wait("vector", se, x1m[tt - 2])
                m1 = sch.op("vector", lambda e, pp=pp: e.tensor_add(
                    z5[:, pp, :], oo5[:, pp, :], bob[:, :]), sv)
                sch.wait("vector", sv, m1)
                m2 = sch.op("vector", lambda e, pp=pp: e.scalar_tensor_tensor(
                    z5[:, pp, :], z5[:, pp, :], 1.0, xo5[:, pp, :], ALU.mult, ALU.add,
                    accum_out=st5[:, pp, 0:1]), sv)
                sch.wait("vector", sv, m2)
                m3 = sch.op("vector", lambda e, pp=pp: e.tensor_scalar_mul(
                    st5[:, pp, 1:2], st5[:, pp, 0:1], -IND), sv)
                sch.wait("scalar", sv, m3)
                m4 = sch.op("scalar", lambda e, pp=pp: e.activation(
                    sqd[:, :], z5[:, pp, :], AF.Square, bias=st5[:, pp, 1:2], scale=1.0,
                    accum_out=st5[:, pp, 2:3]), se)
                sch.wait("scalar", se, m4)
                m5 = sch.op("scalar", lambda e, pp=pp: e.activation(
                    st5[:, pp, 3:4], st5[:, pp, 2:3], AF.Sqrt, bias=EPS, scale=IND), se)
                sch.wait("vector", se, m5)
                m6 = sch.op("vector", lambda e, pp=pp: e.reciprocal(
                    st5[:, pp, 4:5], st5[:, pp, 3:4]), sv)
                sch.wait("vector", sv, m6)
                m7 = sch.op("vector", lambda e, pp=pp: e.tensor_mul(
                    st5[:, pp, 5:6], st5[:, pp, 1:2], st5[:, pp, 4:5]), sv)
                sch.wait("scalar", sv, m7)
                x1m[tt] = sch.op("scalar", lambda e, pp=pp: e.activation(
                    x15[:, pp, :], z5[:, pp, :], AF.Identity,
                    bias=st5[:, pp, 5:6], scale=st5[:, pp, 4:5]), se)
                sch.wait("gpsimd", se, x1m[tt])
                sch.dma("gpsimd", lambda e, tt=tt, pp=pp: e.dma_start(
                    out=x1_scr[128 * tt:128 * (tt + 1), :], in_=x15[:, pp, :]), sq)
                sch.wait("vector", se, x1m[tt])
                m8 = sch.op("vector", lambda e, tt=tt, pp=pp: e.reduce_max(
                    st5[:, pp, 6:7], x15[:, pp, :], axis=AXX, apply_absolute_value=True), sv)
                sch.wait("vector", sv, m8)
                m9 = sch.op("vector", lambda e, tt=tt, pp=pp: e.tensor_scalar_mul(
                    mxd_sb[:, tt:tt + 1], st5[:, pp, 6:7], DQF), sv)
                m10 = sch.op("vector", lambda e, pp=pp: e.tensor_scalar_mul(
                    st5[:, pp, 7:8], st5[:, pp, 6:7], IQD), sv)
                sch.wait("vector", sv, m10)
                m11 = sch.op("vector", lambda e, pp=pp: e.reciprocal(
                    st5[:, pp, 6:7], st5[:, pp, 7:8]), sv)
                sch.wait("scalar", sv, m11)
                m12 = sch.op("scalar", lambda e, pp=pp: e.activation(
                    qx8[:, pp, :], x15[:, pp, :], AF.Copy, bias=0.0,
                    scale=st5[:, pp, 6:7]), se)
                sch.wait("vector", se, m12)
                if tt >= 2:
                    sch.wait("vector", sg, qxtr[tt - 2])
                qxbm[tt] = sch.op("vector", lambda e, pp=pp: e.tensor_copy(
                    qxb[:, pp, :], qx8[:, pp, :]), sv)
                for f in range(16):
                    sch.wait("tensor", sv, qxbm[tt])
                    if ntp >= 2:
                        sch.wait("tensor", se, tevac5[ntp - 2])
                    elif tt == 0 and f == 0:
                        sch.wait("tensor", se, se.v)
                    mtr = sch.op("tensor", lambda e, pp=pp, f=f, ntp=ntp:
                                 e.transpose(pT[ntp % 2][:, 0:128],
                                             qxb[:, pp, 128 * f:128 * (f + 1)],
                                             ident_sb[:, :]), sg)
                    sch.wait("scalar", sg, mtr)
                    tevac5[ntp] = sch.op("scalar", lambda e, f=f, tt=tt, ntp=ntp: e.copy(
                        qxT[:, f, 128 * tt:128 * (tt + 1)], pT[ntp % 2][:, 0:128]), se)
                    ntp += 1
                qxtr[tt] = sg.v
            marks_b0["qxT_done"] = tevac5[ntp - 1]
            sch.flush(blk)

        sw6a, sw6b, st6a, st6b = sems["sw6a"], sems["sw6b"], sems["st6a"], sems["st6b"]
        sto, sout = sems["sto"], sems["sout"]
        sx6a, sx6b = sems["sx6a"], sems["sx6b"]

        def gu_slices(ic):
            lo, hi2 = 512 * ic, 512 * (ic + 1)
            out = []
            r0, r1 = lo // ISH, (hi2 - 1) // ISH
            for r in range(r0, r1 + 1):
                a = max(lo, r * ISH)
                b = min(hi2, (r + 1) * ISH)
                out.append((r, a - r * ISH, a - lo, b - a))
            return out

        mtm = {}
        with (
            nc.sbuf_tensor("wgt", [128, 2, 16, 512], F8) as wgt,
            nc.sbuf_tensor("wut", [128, 2, 16, 512], F8) as wut,
            nc.sbuf_tensor("mwg", [128, 2, 512], F32) as mwg,
            nc.sbuf_tensor("mwu", [128, 2, 512], F32) as mwu,
            nc.sbuf_tensor("gst", [128, 2, 512], F32) as gst,
            nc.sbuf_tensor("ust", [128, 2, 512], F32) as ust,
            nc.sbuf_tensor("sil", [128, 2, 512], F32) as sil,
            nc.sbuf_tensor("tt6", [128, 2, 512], F32) as tt6,
            nc.sbuf_tensor("mt6", [128, 8, 2], F32) as mt6,
            nc.sbuf_tensor("ones_row", [1, 128], F32) as ones_row,
            nc.Block("b6a") as blk,
        ):
            sch = Sched(nc)
            m_or = sch.op("vector", lambda e: e.memset(ones_row[:, :], 1.0), sv)
            sch.wait("sync", cc, 3)
            tm, em6 = {}, {}
            gum = {}
            mwdone = {}
            for ic in range(11):
                swp = (sw6a, sw6b)[ic % 2]
                if ic >= 2:
                    sch.wait("sync", sg, gum[ic - 2])
                    sch.wait("sync", sv, tm[(ic - 2, 7)])
                for f in range(16):
                    for r, roff, coff, ln in gu_slices(ic):
                        sch.dma("sync", lambda e, ic=ic, f=f, r=r, roff=roff, coff=coff, ln=ln:
                                e.dma_start(out=wgt[:, ic % 2, f, coff:coff + ln],
                                            in_=w8gu_full[2 * r + 0, f, :, roff:roff + ln]), swp)
                        sch.dma("sync", lambda e, ic=ic, f=f, r=r, roff=roff, coff=coff, ln=ln:
                                e.dma_start(out=wut[:, ic % 2, f, coff:coff + ln],
                                            in_=w8gu_full[2 * r + 1, f, :, roff:roff + ln]), swp)
                for r, roff, coff, ln in gu_slices(ic):
                    sch.dma_nc("sync", mwg[0:1, ic % 2, coff:coff + ln],
                               wsc_full[3 * r + 0, roff:roff + ln].rearrange("(o d) -> o d", o=1), swp)
                    sch.dma_nc("sync", mwu[0:1, ic % 2, coff:coff + ln],
                               wsc_full[3 * r + 1, roff:roff + ln].rearrange("(o d) -> o d", o=1), swp)
                mload6 = swp.v
                sch.wait("tensor", swp, mload6)
                sch.wait("tensor", sv, m_or)
                if ic > 0:
                    sch.wait("tensor", sv, tm[(ic - 1, 7)])
                    sch.wait("tensor", se, em6[(ic - 1, 7)])
                mb1 = sch.op("tensor", lambda e, ic=ic: e.matmul(
                    ps[4][:, :], ones_row[:, :], mwg[0:1, ic % 2, :],
                    start=True, stop=True), sg)
                sch.wait("scalar", sg, mb1)
                mb1e = sch.op("scalar", lambda e, ic=ic: e.copy(mwg[:, ic % 2, :], ps[4][:, :]), se)
                sch.wait("tensor", se, mb1e)
                mb2 = sch.op("tensor", lambda e, ic=ic: e.matmul(
                    ps[4][:, :], ones_row[:, :], mwu[0:1, ic % 2, :],
                    start=True, stop=True), sg)
                sch.wait("scalar", sg, mb2)
                mb2e = sch.op("scalar", lambda e, ic=ic: e.copy(mwu[:, ic % 2, :], ps[4][:, :]), se)
                for tt in range(8):
                    bk = tt % 2
                    sch.wait("tensor", swp, mload6)
                    if ic == 0 and tt == 0:
                        sch.wait("tensor", se, marks_b0["qxT_done"])
                    key = (ic, tt - 2) if tt >= 2 else ((ic - 1, tt + 6) if ic > 0 else None)
                    if key is not None:
                        sch.wait("tensor", se, em6[key])
                    for f in range(16):
                        sch.op("tensor", lambda e, ic=ic, tt=tt, f=f, bk=bk: e.matmul(
                            ps[bk][:, :], qxT[:, f, 128 * tt:128 * (tt + 1)],
                            wgt[:, ic % 2, f, :], start=(f == 0), stop=(f == 15)),
                            sg if f == 15 else None)
                    for f in range(16):
                        sch.op("tensor", lambda e, ic=ic, tt=tt, f=f, bk=bk: e.matmul(
                            ps[2 + bk][:, :], qxT[:, f, 128 * tt:128 * (tt + 1)],
                            wut[:, ic % 2, f, :], start=(f == 0), stop=(f == 15)),
                            sg if f == 15 else None)
                    mgu = sg.v
                    sch.wait("scalar", sg, mgu)
                    sch.wait("scalar", se, mb2e)
                    if ic > 0 or tt >= 2:
                        sch.wait("scalar", sv, tm.get((ic, tt - 2), tm.get((ic - 1, tt + 6), 0)) or 0)
                    mg1 = sch.op("scalar", lambda e, tt=tt, bk=bk: e.activation(
                        gst[:, bk, :], ps[bk][:, :], AF.Copy, bias=0.0,
                        scale=mxd_sb[:, tt:tt + 1]), se)
                    mu1 = sch.op("scalar", lambda e, tt=tt, bk=bk: e.activation(
                        ust[:, bk, :], ps[2 + bk][:, :], AF.Copy, bias=0.0,
                        scale=mxd_sb[:, tt:tt + 1]), se)
                    em6[(ic, tt)] = mu1
                    sch.wait("vector", se, mu1)
                    mg2 = sch.op("vector", lambda e, ic=ic, bk=bk: e.tensor_mul(
                        gst[:, bk, :], gst[:, bk, :], mwg[:, ic % 2, :]), sv)
                    mu2 = sch.op("vector", lambda e, ic=ic, bk=bk: e.tensor_mul(
                        ust[:, bk, :], ust[:, bk, :], mwu[:, ic % 2, :]), sv)
                    sch.wait("scalar", sv, mg2)
                    ms = sch.op("scalar", lambda e, bk=bk: e.activation(
                        sil[:, bk, :], gst[:, bk, :], AF.Silu, bias=0.0, scale=1.0), se)
                    sch.wait("vector", se, ms)
                    sch.wait("vector", sv, mu2)
                    if ic > 0 or tt >= 2:
                        sch.wait("vector", sto, sto.v)
                    mt_ = sch.op("vector", lambda e, bk=bk: e.tensor_mul(
                        tt6[:, bk, :], sil[:, bk, :], ust[:, bk, :]), sv)
                    tm[(ic, tt)] = mt_
                    sch.wait("vector", sv, mt_)
                    mr = sch.op("vector", lambda e, tt=tt, bk=bk, ic=ic: e.reduce_max(
                        mt6[:, tt, ic % 2:ic % 2 + 1], tt6[:, bk, :], axis=AXX,
                        apply_absolute_value=True), sv)
                    sch.wait("vector", sv, mr)
                    if ic == 0:
                        mm_ = sch.op("vector", lambda e, tt=tt: e.tensor_copy(
                            mt_sb[:, tt:tt + 1], mt6[:, tt, 0:1]), sv)
                    else:
                        mm_ = sch.op("vector", lambda e, tt=tt, ic=ic: e.tensor_max(
                            mt_sb[:, tt:tt + 1], mt_sb[:, tt:tt + 1],
                            mt6[:, tt, ic % 2:ic % 2 + 1]), sv)
                    mtm[(ic, tt)] = mm_
                    sch.wait("gpsimd", sv, mt_)
                    sch.dma("gpsimd", lambda e, ic=ic, tt=tt, bk=bk: e.dma_start(
                        out=t_scr[128 * tt:128 * (tt + 1), 512 * ic:512 * (ic + 1)],
                        in_=tt6[:, bk, :]), sto)
                gum[ic] = sg.v
            marks_b0["t_done"] = sto.v
            marks_b0["mt_done"] = {tt: mtm[(10, tt)] for tt in range(8)}
            sch.flush(blk)

        with (
            nc.sbuf_tensor("wdt6", [128, 44, D], F8) as wdt6,
            nc.sbuf_tensor("mwd6", [128, 4, 512], F32) as mwd6,
            nc.sbuf_tensor("mwr", [1, 4, 512], F32) as mwr,
            nc.sbuf_tensor("ones_row2", [1, 128], F32) as ones_row2,
            nc.sbuf_tensor("tld", [128, 2, 512], F32) as tld,
            nc.sbuf_tensor("qt8", [128, 2, 512], F8) as qt8,
            nc.sbuf_tensor("qtb", [128, 2, 512], BF16) as qtb,
            nc.sbuf_tensor("qtT", [128, 2, 44, 128], F8) as qtT,
            nc.sbuf_tensor("st6", [128, 2, 8], F32) as st6,
            nc.sbuf_tensor("zo6", [128, 4, 512], F32) as zo6,
            nc.sbuf_tensor("x16", [128, 2, D], F32) as x16,
            nc.sbuf_tensor("sq6", [128, D], F32) as sq6,
            nc.sbuf_tensor("fo6", [128, 2, D], F32) as fo6,
            nc.Block("b6b") as blk,
        ):
            sch = Sched(nc)
            sch.wait("sync", sq, sq.v)
            m_or2 = sch.op("vector", lambda e: e.memset(ones_row2[:, :], 1.0), sv)
            for r in range(8):
                for ic in range(44):
                    sch.dma("sync", lambda e, r=r, ic=ic: e.dma_start(
                        out=wdt6[:, ic, 256 * r:256 * (r + 1)],
                        in_=w8d_full[44 * r + ic, :, :]), swd)
                sch.dma_nc("sync", mwr[0:1, r // 2, 256 * (r % 2):256 * (r % 2) + 256],
                           wsc_full[3 * r + 2, 0:256].rearrange("(o d) -> o d", o=1), swd)
            m_wd = swd.v
            sch.wait("tensor", swd, m_wd)
            sch.wait("tensor", sv, m_or2)
            mwde = {}
            for oc in range(4):
                sch.wait("tensor", se, se.v)
                mb = sch.op("tensor", lambda e, oc=oc: e.matmul(
                    ps[4][:, :], ones_row2[:, :], mwr[0:1, oc, :], start=True, stop=True), sg)
                sch.wait("scalar", sg, mb)
                mwde[oc] = sch.op("scalar", lambda e, oc=oc: e.copy(
                    mwd6[:, oc, :], ps[4][:, :]), se)
            qtev, dnm, zevm, zmulm, n3m, tldm = {}, {}, {}, {}, {}, {}
            qtbm_d, qttr_d = {}, {}
            sttm, fom = {}, {}
            x16m = {}
            for tt in range(8):
                pp = tt % 2
                sch.wait("vector", sv, mtm[(10, tt)])
                if tt >= 2:
                    sch.wait("vector", se, qtev[(tt - 2, 43)])
                n1 = sch.op("vector", lambda e, tt=tt, pp=pp: e.tensor_scalar_mul(
                    st6[:, pp, 0:1], mt_sb[:, tt:tt + 1], IQD), sv)
                sch.wait("vector", sv, n1)
                n2 = sch.op("vector", lambda e, pp=pp: e.reciprocal(
                    st6[:, pp, 1:2], st6[:, pp, 0:1]), sv)
                n3 = sch.op("vector", lambda e, tt=tt, pp=pp: e.tensor_scalar_mul(
                    st6[:, pp, 2:3], mt_sb[:, tt:tt + 1], DQF), sv)
                n3m[tt] = n3
                for ic in range(11):
                    gic = tt * 11 + ic
                    gp = gic % 2
                    stp = (st6a, st6b)[gp]
                    if gic >= 2:
                        sch.wait("sync", se, tldm[gic - 2])
                    if tt == 0 and ic == 0:
                        sch.wait("sync", sto, marks_b0["t_done"])
                    mld = sch.dma("sync", lambda e, tt=tt, ic=ic, gp=gp: e.dma_start(
                        out=tld[:, gp, :],
                        in_=t_scr[128 * tt:128 * (tt + 1), 512 * ic:512 * (ic + 1)]), stp)
                    sch.wait("scalar", stp, mld)
                    sch.wait("scalar", sv, n2)
                    if gic >= 2:
                        sch.wait("scalar", sv, qtbm_d.get(gic - 2, 0) or 0)
                    mq8 = sch.op("scalar", lambda e, gp=gp, pp=pp: e.activation(
                        qt8[:, gp, :], tld[:, gp, :], AF.Copy, bias=0.0,
                        scale=st6[:, pp, 1:2]), se)
                    tldm[gic] = mq8
                    sch.wait("vector", se, mq8)
                    if gic >= 2:
                        sch.wait("vector", sg, qttr_d.get(gic - 2, 0) or 0)
                    qtbm = sch.op("vector", lambda e, gp=gp: e.tensor_copy(
                        qtb[:, gp, :], qt8[:, gp, :]), sv)
                    qtbm_d[gic] = qtbm
                    for j in range(4):
                        icc = 4 * ic + j
                        sch.wait("tensor", sv, qtbm)
                        if (tt, icc) != (0, 0):
                            pk = icc - 2 if icc >= 2 else 42 + icc
                            ptt = tt if icc >= 2 else tt - 1
                            sch.wait("tensor", se, qtev.get((ptt, pk), 0) or 0)
                        if tt > 1:
                            sch.wait("tensor", sg, dnm.get((tt - 2, 3), 0) or 0)
                        mtr = sch.op("tensor", lambda e, gp=gp, icc=icc, j=j: e.transpose(
                            pT[icc % 2][:, 0:128],
                            qtb[:, gp, 128 * j:128 * (j + 1)], ident_sb[:, :]), sg)
                        sch.wait("scalar", sg, mtr)
                        qtev[(tt, icc)] = sch.op("scalar", lambda e, tt=tt, icc=icc:
                                                 e.copy(qtT[:, tt % 2, icc, :],
                                                        pT[icc % 2][:, 0:128]), se)
                    qttr_d[gic] = sg.v
                for oc in range(4):
                    sch.wait("tensor", se, qtev[(tt, 43)])
                    if tt == 0 and oc == 0:
                        sch.wait("tensor", swd, m_wd)
                        sch.wait("tensor", se, se.v)
                    if tt > 0:
                        sch.wait("tensor", se, zevm.get((tt - 1, oc), 0) or 0)
                    for ic in range(44):
                        sch.op("tensor", lambda e, tt=tt, oc=oc, ic=ic: e.matmul(
                            ps[oc][:, :], qtT[:, tt % 2, ic, :],
                            wdt6[:, ic, 512 * oc:512 * (oc + 1)],
                            start=(ic == 0), stop=(ic == 43)),
                            sg if ic == 43 else None)
                    dnm[(tt, oc)] = sg.v
                    sch.wait("scalar", sg, dnm[(tt, oc)])
                    sch.wait("scalar", se, mwde[3])
                    if tt >= 2:
                        sch.wait("scalar", sv, sttm[tt - 2])
                    sch.wait("scalar", sv, n3m[tt])
                    mze = sch.op("scalar", lambda e, tt=tt, oc=oc, pp=pp: e.activation(
                        zo6[:, oc, :], ps[oc][:, :], AF.Copy, bias=0.0,
                        scale=st6[:, pp, 2:3]), se)
                    zevm[(tt, oc)] = mze
                    sch.wait("vector", se, mze)
                    zmulm[(tt, oc)] = sch.op("vector", lambda e, oc=oc: e.tensor_mul(
                        zo6[:, oc, :], zo6[:, oc, :], mwd6[:, oc, :]), sv)
                sx6p = (sx6a, sx6b)[pp]
                if tt >= 2:
                    sch.wait("sync", sv, sttm[tt - 2])
                x16m[tt] = sch.dma("sync", lambda e, tt=tt, pp=pp: e.dma_start(
                    out=x16[:, pp, :], in_=x1_scr[128 * tt:128 * (tt + 1), :]), sx6p)
                sch.wait("vector", sx6p, x16m[tt])
                sch.wait("vector", sv, zmulm[(tt, 3)])
                if tt >= 2:
                    sch.wait("vector", se, fom[tt - 2])
                    sch.wait("vector", sout, sout.v)
                mstt = sch.op("vector", lambda e, tt=tt, pp=pp: e.scalar_tensor_tensor(
                    fo6[:, pp, :], zo6[:, :, :].rearrange("p a b -> p (a b)"), 1.0,
                    x16[:, pp, :], ALU.mult, ALU.add, accum_out=st6[:, pp, 3:4]), sv)
                sttm[tt] = mstt
                sch.wait("vector", sv, mstt)
                k3 = sch.op("vector", lambda e, pp=pp: e.tensor_scalar_mul(
                    st6[:, pp, 4:5], st6[:, pp, 3:4], -IND), sv)
                sch.wait("scalar", sv, k3)
                k4 = sch.op("scalar", lambda e, pp=pp: e.activation(
                    sq6[:, :], fo6[:, pp, :], AF.Square, bias=st6[:, pp, 4:5], scale=1.0,
                    accum_out=st6[:, pp, 5:6]), se)
                sch.wait("scalar", se, k4)
                k5 = sch.op("scalar", lambda e, pp=pp: e.activation(
                    st6[:, pp, 6:7], st6[:, pp, 5:6], AF.Sqrt, bias=EPS, scale=IND), se)
                sch.wait("vector", se, k5)
                k6 = sch.op("vector", lambda e, pp=pp: e.reciprocal(
                    st6[:, pp, 7:8], st6[:, pp, 6:7]), sv)
                sch.wait("vector", sv, k6)
                k7 = sch.op("vector", lambda e, pp=pp: e.tensor_mul(
                    st6[:, pp, 4:5], st6[:, pp, 4:5], st6[:, pp, 7:8]), sv)
                sch.wait("scalar", sv, k7)
                fom[tt] = sch.op("scalar", lambda e, tt=tt, pp=pp: e.activation(
                    fo6[:, pp, :], fo6[:, pp, :], AF.Identity,
                    bias=st6[:, pp, 4:5], scale=st6[:, pp, 7:8]), se)
                sch.wait("gpsimd", se, fom[tt])
                sch.dma("gpsimd", lambda e, tt=tt, pp=pp: e.dma_start(
                    out=out_own[128 * tt:128 * (tt + 1), :], in_=fo6[:, pp, :]), sout)
            if dbg_phase == 2:
                sch.wait("gpsimd", sq, sq.v)
                sch.wait("gpsimd", sto, sto.v)
                sch.wait("gpsimd", sout, sout.v)
                for a, b in (("d_x1", x1_scr), ("d_t", t_scr)):
                    sch.dma("gpsimd", lambda e, a=a, b=b: e.dma_start(out=dbg[a], in_=b), sq)
                sch.wait("gpsimd", sv, sv.v)
                sch.dma("gpsimd", lambda e: e.dma_start(out=dbg["d_mt"], in_=mt_sb[:, :]), sq)
                sch.dma("gpsimd", lambda e: e.dma_start(out=dbg["d_qxT"], in_=qxT[:, :, :]), sq)
            sch.wait("gpsimd", sout, sout.v)
            sch.wait("gpsimd", sq, sq.v)
            sch.flush(blk)
    return nc


def make_inmaps(inputs):
    import ml_dtypes
    x = np.ascontiguousarray(np.asarray(inputs["L0_attn_x"], dtype=np.float32).reshape(S, D))
    xt = np.ascontiguousarray(x.T)
    WQ = np.asarray(inputs["L0_attn_WQ_w"], dtype=np.float32)
    WK = np.asarray(inputs["L0_attn_WK_w"], dtype=np.float32)
    WV = np.asarray(inputs["L0_attn_WV_w"], dtype=np.float32)
    WO = np.asarray(inputs["L0_attn_WO_w"], dtype=np.float32)
    bq = np.asarray(inputs["L0_attn_WQ_b"], dtype=np.float32)
    bk = np.asarray(inputs["L0_attn_WK_b"], dtype=np.float32)
    bv = np.asarray(inputs["L0_attn_WV_b"], dtype=np.float32)
    bo = np.asarray(inputs["L0_attn_WO_b"], dtype=np.float32)
    wg = np.ascontiguousarray(np.asarray(inputs["L0_ffn_w_gate"], dtype=np.float32))
    wu = np.ascontiguousarray(np.asarray(inputs["L0_ffn_w_up"], dtype=np.float32))
    wd = np.ascontiguousarray(np.asarray(inputs["L0_ffn_w_down"], dtype=np.float32))
    dn = np.asarray(inputs["L0_ffn_down_reorder_index"], dtype=np.int32)
    WOT = np.ascontiguousarray(WO.T)
    bo_b = np.ascontiguousarray(np.broadcast_to(bo, (128, D)))

    mask = np.zeros((4, 128, 512), dtype=np.float32)
    for j in range(4):
        for k in range(128):
            for q in range(512):
                if 128 * j + k > q:
                    mask[j, k, q] = -1e9
    import ml_dtypes as md
    ident = np.eye(128, dtype=md.bfloat16)

    in_maps = []
    for c in range(N_CORES):
        fsl = slice(OSH * c, OSH * (c + 1))
        tsl = slice(TOK * c, TOK * (c + 1))
        b3c = np.stack([bq[fsl], bk[fsl], bv[fsl]])
        b3c = np.ascontiguousarray(b3c.reshape(3, 2, 128).transpose(2, 0, 1))
        in_maps.append({
            "xt": xt,
            "x_own": np.ascontiguousarray(x[tsl]),
            "wq": np.ascontiguousarray(WQ[fsl]),
            "wk": np.ascontiguousarray(WK[fsl]),
            "wv": np.ascontiguousarray(WV[fsl]),
            "b3": b3c,
            "wo_t": np.ascontiguousarray(WOT[fsl]),
            "bo_b": bo_b,
            "wg_in": wg,
            "wu_in": wu,
            "wd_rows": np.ascontiguousarray(wd[fsl]),
            "dn_shard": np.ascontiguousarray(dn[ISH * c:ISH * (c + 1)]),
            "masks": mask,
            "identb": ident,
        })
    return in_maps


def kernel(**inputs):
    nc = build_nc(dbg_phase=0)
    in_maps = make_inmaps(inputs)
    res = run_bass_kernel_spmd(nc, in_maps, core_ids=list(range(N_CORES)))
    out = np.concatenate([r["out_own"] for r in res.results], axis=0)
    return out.reshape(1, S, D)


# version 08
# speedup vs baseline: 1.0555x; 1.0555x over previous
import contextlib
import math
import numpy as np
import concourse.bass as bass
import concourse.mybir as mybir
from concourse.bass import IndirectOffsetOnAxis
from concourse.bass_utils import run_bass_kernel_spmd

F32 = mybir.dt.float32
BF16 = mybir.dt.bfloat16
F8 = mybir.dt.float8e4
I32 = mybir.dt.int32
AF = mybir.ActivationFunctionType
ALU = mybir.AluOpType
AXX = mybir.AxisListType.X

S, D, NH, HD, II = 8192, 2048, 16, 128, 5632
TOK = 1024
ISH = II // 8
OSH = D // 8
N_CORES = 8
SCALE = 1.0 / math.sqrt(HD)
EXP_BIAS = -20.0
QD = 224.0
IQD = 1.0 / QD
DQF = 4.0 / (448.0 * 448.0)
EPS = 1e-5
IND = 1.0 / D

ENGS = ("sync", "vector", "scalar", "tensor", "gpsimd")


class Ctr:
    def __init__(self, sem):
        self.sem, self.v = sem, 0


class Sched:
    def __init__(self, nc):
        self.nc = nc
        self.ops = {k: [] for k in ENGS}

    def wait(self, eng, ctr, val=None):
        v = ctr.v if val is None else val
        if not v or v <= 0:
            return
        self.ops[eng].append(lambda e, s=ctr.sem, v=v: e.wait_ge(s, v))

    def op(self, eng, build, ctr=None, n=1):
        if ctr is None:
            self.ops[eng].append(lambda e, b=build: b(e))
            return None
        ctr.v += n
        self.ops[eng].append(lambda e, b=build, s=ctr.sem, m=n: b(e).then_inc(s, m))
        return ctr.v

    def dma(self, eng, build, ctr):
        return self.op(eng, build, ctr, 16)

    def dma_nc(self, eng, out, in_, ctr):
        nc = self.nc
        def b(e):
            with nc.allow_non_contiguous_dma(reason="tiny"):
                return e.dma_start(out=out, in_=in_)
        return self.op(eng, b, ctr, 16)

    def flush(self, blk):
        for name in ENGS:
            fns = self.ops[name]
            if not fns:
                continue
            getattr(blk, name)(lambda e, fns=fns: [f(e) for f in fns])
        self.ops = {k: [] for k in ENGS}


def build_nc(dbg_phase=0):
    nc = bass.Bass("TRN2", target_bir_lowering=False)
    dt = nc.dram_tensor

    xt = dt("xt", [D, S], F32, kind="ExternalInput").ap()
    x_own = dt("x_own", [TOK, D], F32, kind="ExternalInput").ap()
    wq = dt("wq", [OSH, D], F32, kind="ExternalInput").ap()
    wk = dt("wk", [OSH, D], F32, kind="ExternalInput").ap()
    wv = dt("wv", [OSH, D], F32, kind="ExternalInput").ap()
    b3 = dt("b3", [128, 3, 2], F32, kind="ExternalInput").ap()
    wo_t = dt("wo_t", [OSH, D], F32, kind="ExternalInput").ap()
    bo_b = dt("bo_b", [128, D], F32, kind="ExternalInput").ap()
    wg_in = dt("wg_in", [II, D], F32, kind="ExternalInput").ap()
    wu_in = dt("wu_in", [II, D], F32, kind="ExternalInput").ap()
    wd_rows = dt("wd_rows", [OSH, II], F32, kind="ExternalInput").ap()
    dn_shard = dt("dn_shard", [ISH], I32, kind="ExternalInput").ap()
    masks = dt("masks", [4, 128, 512], F32, kind="ExternalInput").ap()
    identb = dt("identb", [128, 128], BF16, kind="ExternalInput").ap()

    out_own = dt("out_own", [TOK, D], F32, kind="ExternalOutput").ap()

    q_scr = dt("q_scr", [2, 2, 128, S], BF16).ap()
    k_scr = dt("k_scr", [2, 2, 128, S], BF16).ap()
    v_scr = dt("v_scr", [2, 2, S, 128], BF16).ap()
    o_part0 = dt("o_part0", [S, D // 2], F32).ap()
    o_part1 = dt("o_part1", [S, D // 2], F32).ap()
    o_red0 = dt("o_red0", [TOK, D // 2], F32).ap()
    o_red1 = dt("o_red1", [TOK, D // 2], F32).ap()
    x1_scr = dt("x1_scr", [TOK, D], F32).ap()
    t_scr = dt("t_scr", [TOK, II], F32).ap()
    w8gu_in = dt("w8gu_in", [2, 16, 128, ISH], F8).ap()
    w8gu_full = dt("w8gu_full", [16, 16, 128, ISH], F8).ap()
    w8d_in = dt("w8d_in", [44, 128, OSH], F8).ap()
    w8d_full = dt("w8d_full", [352, 128, OSH], F8).ap()
    wsc_in = dt("wsc_in", [3, ISH], F32).ap()
    wsc_full = dt("wsc_full", [24, ISH], F32).ap()

    dbg = {}
    if dbg_phase == 1:
        for nm, shp, dty in (("d_qsc", [2, 2, 128, S], BF16), ("d_ksc", [2, 2, 128, S], BF16),
                             ("d_vsc", [2, 2, S, 128], BF16), ("d_w8g", [16, 16, 128, ISH], F8),
                             ("d_w8d", [352, 128, OSH], F8), ("d_wsc", [24, ISH], F32)):
            dbg[nm] = dt(nm, shp, dty, kind="ExternalOutput").ap()
    if dbg_phase == 2:
        dbg["d_ahi"] = dt("d_ahi", [128, 64, 256], BF16, kind="ExternalOutput").ap()
        dbg["d_alo"] = dt("d_alo", [128, 64, 256], BF16, kind="ExternalOutput").ap()
        dbg["d_opart"] = dt("d_opart", [S, D], F32, kind="ExternalOutput").ap()
        dbg["d_ored"] = dt("d_ored", [TOK, D], F32, kind="ExternalOutput").ap()
        dbg["d_x1"] = dt("d_x1", [TOK, D], F32, kind="ExternalOutput").ap()
        dbg["d_t"] = dt("d_t", [TOK, II], F32, kind="ExternalOutput").ap()
        dbg["d_mt"] = dt("d_mt", [128, 8], F32, kind="ExternalOutput").ap()
        dbg["d_qxT"] = dt("d_qxT", [128, 16, TOK], F8, kind="ExternalOutput").ap()
        dbg["d_aT"] = dt("d_aT", [128, 2, 2, 128], BF16, kind="ExternalOutput").ap()
        dbg["d_aTl"] = dt("d_aTl", [128, 2, 2, 128], BF16, kind="ExternalOutput").ap()

    st = contextlib.ExitStack()
    with st:
        cpool = st.enter_context(nc.sbuf_tensor("cpool", [128, 12], F32))
        ident_sb = st.enter_context(nc.sbuf_tensor("ident_sb", [128, 128], BF16))
        woT_h = st.enter_context(nc.sbuf_tensor("woT_h", [128, 2, D], BF16))
        woT_l = st.enter_context(nc.sbuf_tensor("woT_l", [128, 2, D], BF16))
        qxT = st.enter_context(nc.sbuf_tensor("qxT", [128, 16, TOK], F8))
        mxd_sb = st.enter_context(nc.sbuf_tensor("mxd_sb", [128, 8], F32))
        mt_sb = st.enter_context(nc.sbuf_tensor("mt_sb", [128, 8], F32))

        ps = [st.enter_context(nc.psum_tensor(f"ps{i}", [128, 512], F32)) for i in range(6)]
        pT = [st.enter_context(nc.psum_tensor(f"pT{i}", [128, 1024], BF16)) for i in range(2)]

        CONSTS = (0.0, 1.0, EXP_BIAS, SCALE, IQD, IND, EPS, -1e9, DQF)
        for i, cv in enumerate(CONSTS):
            nc.const_aps.register(F32, float(cv), cpool[:, i:i + 1])

        semnames = ("sd", "se", "sv", "sg", "sp", "sq", "cc",
                    "swa", "swb", "swd", "sga", "sgb",
                    "sxa", "sxb", "sb3", "skv", "sqa", "sqb", "sms",
                    "sw6a", "sw6b", "st6a", "st6b", "sto", "sout", "sx6a", "sx6b",
                    "soa", "sob", "soc", "sod")
        sems = {n: Ctr(st.enter_context(nc.semaphore(n))) for n in semnames}
        sd, se, sv, sg = sems["sd"], sems["se"], sems["sv"], sems["sg"]
        sp, sq, cc = sems["sp"], sems["sq"], sems["cc"]
        swa, swb, swd = sems["swa"], sems["swb"], sems["swd"]
        sga, sgb = sems["sga"], sems["sgb"]

        marks_b0 = {}
        stA = contextlib.ExitStack()
        st.enter_context(stA)
        wqkvT_h = stA.enter_context(nc.sbuf_tensor("wqkvT_hS", [128, 3, 16, OSH], BF16))
        wqkvT_l = stA.enter_context(nc.sbuf_tensor("wqkvT_lS", [128, 3, 16, OSH], BF16))
        idx_sb = stA.enter_context(nc.sbuf_tensor("idx_sb", [128, 6], I32))
        gw = stA.enter_context(nc.sbuf_tensor("gw", [128, 2, D], F32))
        gm = stA.enter_context(nc.sbuf_tensor("gm", [128, 6], F32))
        g8 = stA.enter_context(nc.sbuf_tensor("g8", [128, 2, D], F8))
        g8b = stA.enter_context(nc.sbuf_tensor("g8b", [128, 2, D], BF16))
        gev = stA.enter_context(nc.sbuf_tensor("gev", [128, 4, 128], F8))
        wsc_sb = stA.enter_context(nc.sbuf_tensor("wsc_sb", [128, 3, 6], F32))
        zpad = stA.enter_context(nc.sbuf_tensor("zpad", [1, 448], F32))
        wd_sb = stA.enter_context(nc.sbuf_tensor("wd_sb", [128, 1, II], F32))
        wd8 = stA.enter_context(nc.sbuf_tensor("wd8", [128, 1, II], F8))
        wd8b = stA.enter_context(nc.sbuf_tensor("wd8b", [128, 1, II], BF16))
        with (
            nc.sbuf_tensor("wrow", [128, 2, D], F32) as wrow,
            nc.sbuf_tensor("wsp_h", [128, 2, D], BF16) as wsp_h,
            nc.sbuf_tensor("wsp_l", [128, 2, D], BF16) as wsp_l,
            nc.Block("b0") as blk,
        ):
            sch = Sched(nc)
            sch.dma("sync", lambda e: e.dma_start(out=ident_sb[:, :], in_=identb), sd)
            sch.dma_nc("sync", idx_sb[:, 0:5],
                       dn_shard[0:640].rearrange("(t p) -> p t", p=128), sd)
            sch.dma_nc("sync", idx_sb[0:64, 5:6],
                       dn_shard[640:704].rearrange("(p o) -> p o", o=1), sd)
            m_ident = m_idx = sd.v
            for i, cv in enumerate(CONSTS):
                sch.op("vector", lambda e, i=i, cv=cv: e.memset(cpool[:, i:i + 1], float(cv)))
            m_const = sch.op("vector", lambda e: e.engine_nop(), sv)

            evac_mark = {}
            n = 0
            prev_split = {0: None, 1: None}
            prev_last_tr = None
            for wi, wap in enumerate((wq, wk, wv, wo_t)):
                him, lom, wload = {}, {}, {}
                for t in range(2):
                    if prev_split[t] is not None:
                        sch.wait("sync", se, prev_split[t][0])
                        sch.wait("sync", sv, prev_split[t][1])
                    wload[t] = sch.dma(
                        "sync", lambda e, t=t, wap=wap: e.dma_start(
                            out=wrow[:, t:t + 1, :],
                            in_=wap[128 * t:128 * (t + 1), :].rearrange("p d -> p () d")),
                        (swa, swb)[t])
                for t in range(2):
                    dsth = woT_h[:, t, :] if wi == 3 else wsp_h[:, t, :]
                    dstl = woT_l[:, t, :] if wi == 3 else wsp_l[:, t, :]
                    sch.wait("scalar", (swa, swb)[t], wload[t])
                    if wi > 0 and wi < 4 and prev_last_tr is not None and wi != 3:
                        sch.wait("scalar", sg, prev_last_tr)
                    him[t] = sch.op("scalar", lambda e, d=dsth, t=t: e.copy(d, wrow[:, t, :]), se)
                    sch.wait("vector", se, him[t])
                    lom[t] = sch.op("vector", lambda e, dl=dstl, dh=dsth, t=t:
                                    e.scalar_tensor_tensor(dl, wrow[:, t, :], 1.0, dh,
                                                           ALU.mult, ALU.subtract), sv)
                    prev_split[t] = (him[t], lom[t])
                if wi == 3:
                    break
                for hl in range(2):
                    srcT = (wsp_h, wsp_l)[hl]
                    dst = (wqkvT_h, wqkvT_l)[hl]
                    for t in range(2):
                        for f in range(16):
                            if n == 0:
                                sch.wait("tensor", sd, m_ident)
                            sch.wait("tensor", (se, sv)[hl], (him, lom)[hl][t])
                            if n >= 2:
                                sch.wait("tensor", sv, evac_mark[n - 2])
                            mtr = sch.op("tensor", lambda e, srcT=srcT, t=t, f=f, n=n:
                                         e.transpose(pT[n % 2][:, 0:128],
                                                     srcT[:, t, 128 * f:128 * (f + 1)],
                                                     ident_sb[:, :]), sg)
                            sch.wait("vector", sg, mtr)
                            evac_mark[n] = sch.op(
                                "vector", lambda e, dst=dst, wi=wi, f=f, t=t, n=n:
                                e.tensor_copy(dst[:, wi, f, 128 * t:128 * (t + 1)],
                                              pT[n % 2][:, 0:128]), sv)
                            n += 1
                    prev_last_tr = sg.v
            m_awT = evac_mark[n - 1]
            marks_b0["awT"] = m_awT

            marks_b0["const"] = m_const
            marks_b0["idx"] = m_idx
            sch.flush(blk)

        with (
            nc.sbuf_tensor("b3_sb", [128, 3, 2], F32) as b3_sb,
            nc.sbuf_tensor("xf", [128, 2, 512], F32) as xf,
            nc.sbuf_tensor("xh", [128, 2, 512], BF16) as xh,
            nc.sbuf_tensor("xl", [128, 2, 512], BF16) as xl,
            nc.sbuf_tensor("qkst", [128, 6, 512], F32) as qkst,
            nc.sbuf_tensor("qkh", [128, 6, 512], BF16) as qkh,
            nc.sbuf_tensor("qkl", [128, 6, 512], BF16) as qkl,
            nc.sbuf_tensor("vev", [128, 4, 128], BF16) as vev,
            nc.Block("b1") as blk,
        ):
            sch = Sched(nc)
            sch.wait("tensor", sv, marks_b0["awT"])
            m_b3 = sch.dma("sync", lambda e: e.dma_start(out=b3_sb[:, :, :], in_=b3), sems["sb3"])
            xh_m, xl_m, mm_fc = {}, {}, {}
            prev = {}
            ntp = 0
            tevac2, tdma2 = {}, {}
            for tc in range(16):
                for fc in range(16):
                    i = tc * 16 + fc
                    if i >= 2:
                        ptc, pfc = (i - 2) // 16, (i - 2) % 16
                        sch.wait("sync", sv, xl_m[(ptc, pfc)])
                    mload = sch.dma("sync", lambda e, tc=tc, fc=fc: e.dma_start(
                        out=xf[:, fc % 2, :],
                        in_=xt[128 * fc:128 * (fc + 1), 512 * tc:512 * (tc + 1)]),
                        (sems["sxa"], sems["sxb"])[fc % 2])
                    sch.wait("scalar", (sems["sxa"], sems["sxb"])[fc % 2], mload)
                    if i >= 2:
                        sch.wait("scalar", sg, mm_fc[i - 2])
                    xh_m[(tc, fc)] = sch.op("scalar", lambda e, fc=fc: e.copy(
                        xh[:, fc % 2, :], xf[:, fc % 2, :]), se)
                    sch.wait("vector", se, xh_m[(tc, fc)])
                    if i >= 2:
                        sch.wait("vector", sg, mm_fc[i - 2])
                    xl_m[(tc, fc)] = sch.op("vector", lambda e, fc=fc: e.scalar_tensor_tensor(
                        xl[:, fc % 2, :], xf[:, fc % 2, :], 1.0, xh[:, fc % 2, :],
                        ALU.mult, ALU.subtract), sv)
                    sch.wait("tensor", se, xh_m[(tc, fc)])
                    sch.wait("tensor", sv, xl_m[(tc, fc)])
                    for oi in range(6):
                        wi, hd = oi // 2, oi % 2
                        if fc == 0 and tc > 0:
                            sch.wait("tensor", se, prev[oi]["stage"])
                        for pi in range(3):
                            lw = (wqkvT_h, wqkvT_h, wqkvT_l)[pi]
                            rx = (xh, xl, xh)[pi]
                            strt = (fc == 0 and pi == 0)
                            stop = (fc == 15 and pi == 2)
                            last_of_fc = (oi == 5 and pi == 2)
                            sch.op("tensor", lambda e, lw=lw, rx=rx, wi=wi, hd=hd, fc=fc, oi=oi,
                                   strt=strt, stop=stop:
                                   e.matmul(ps[oi][:, :], lw[:, wi, fc, 128 * hd:128 * (hd + 1)],
                                            rx[:, fc % 2, :], start=strt, stop=stop),
                                   sg if (stop or last_of_fc) else None)
                            if last_of_fc:
                                mm_fc[i] = sg.v
                cur = {}
                for oi in range(6):
                    wi, hd = oi // 2, oi % 2
                    mm_stop = sg.v - (5 - oi)
                    cur[oi] = {}
                    sch.wait("scalar", sg, mm_stop)
                    if tc == 0:
                        sch.wait("scalar", sems["sb3"], m_b3)
                    if tc > 0:
                        p = prev[oi]
                        sch.wait("scalar", sv, p["lo"])
                        if wi < 2:
                            sch.wait("scalar", sq, sq.v)
                        else:
                            sch.wait("scalar", sg, p["lasttr"])
                    mstg = sch.op("scalar", lambda e, oi=oi, wi=wi, hd=hd: e.activation(
                        qkst[:, oi, :], ps[oi][:, :], AF.Identity,
                        bias=b3_sb[:, wi, hd:hd + 1], scale=1.0), se)
                    cur[oi]["stage"] = mstg
                    sch.wait("scalar", se, mstg)
                    mhi = sch.op("scalar", lambda e, oi=oi: e.copy(
                        qkh[:, oi, :], qkst[:, oi, :]), se)
                    cur[oi]["hi"] = mhi
                    sch.wait("vector", se, mhi)
                    mlo = sch.op("vector", lambda e, oi=oi: e.scalar_tensor_tensor(
                        qkl[:, oi, :], qkst[:, oi, :], 1.0, qkh[:, oi, :],
                        ALU.mult, ALU.subtract), sv)
                    cur[oi]["lo"] = mlo
                    if wi < 2:
                        scr = (q_scr, k_scr)[wi]
                        sch.wait("gpsimd", se, mhi)
                        cur[oi]["hidma"] = sch.dma(
                            "gpsimd", lambda e, scr=scr, hd=hd, tc=tc, oi=oi: e.dma_start(
                                out=scr[hd, 0, :, 512 * tc:512 * (tc + 1)], in_=qkh[:, oi, :]), sq)
                        sch.wait("gpsimd", sv, mlo)
                        cur[oi]["lodma"] = sch.dma(
                            "gpsimd", lambda e, scr=scr, hd=hd, tc=tc, oi=oi: e.dma_start(
                                out=scr[hd, 1, :, 512 * tc:512 * (tc + 1)], in_=qkl[:, oi, :]), sq)
                    else:
                        for hl in range(2):
                            mm = (mhi, mlo)[hl]
                            srcv = (qkh, qkl)[hl]
                            for j in range(4):
                                sch.wait("tensor", (se, sv)[hl], mm)
                                if tc == 0 and ntp < 2:
                                    pass
                                elif ntp >= 2:
                                    sch.wait("tensor", sv, tevac2[ntp - 2])
                                mtr = sch.op("tensor", lambda e, srcv=srcv, oi=oi, j=j, ntp=ntp:
                                             e.transpose(pT[ntp % 2][:, 0:128],
                                                         srcv[:, oi, 128 * j:128 * (j + 1)],
                                                         ident_sb[:, :]), sg)
                                cur[oi]["lasttr"] = mtr
                                sch.wait("vector", sg, mtr)
                                if ntp >= 4:
                                    sch.wait("vector", sq, sq.v)
                                tevac2[ntp] = sch.op("vector", lambda e, ntp=ntp: e.tensor_copy(
                                    vev[:, ntp % 4, :], pT[ntp % 2][:, 0:128]), sv)
                                sch.wait("gpsimd", sv, tevac2[ntp])
                                tdma2[ntp] = sch.dma(
                                    "gpsimd", lambda e, hd=hd, hl=hl, tc=tc, j=j, ntp=ntp: e.dma_start(
                                        out=v_scr[hd, hl, 512 * tc + 128 * j:512 * tc + 128 * (j + 1), :],
                                        in_=vev[:, ntp % 4, :]), sq)
                                ntp += 1
                prev = cur

            tile_rows = [128] * 5 + [64]
            tile_off = [0, 128, 256, 384, 512, 640]
            q8m, rowm, b16m, gatherm, tevac, tdma = {}, {}, {}, {}, {}, {}
            ntp = 0
            for it in range(12):
                gi, ti = it // 6, it % 6
                nr = tile_rows[ti]
                wfull = (wg_in, wu_in)[gi]
                if it == 0:
                    sch.wait("gpsimd", sd, marks_b0["idx"])
                if it >= 2:
                    sch.wait("gpsimd", sv, b16m[it - 2])
                gatherm[it] = sch.dma(
                    "gpsimd", lambda e, nr=nr, it=it, ti=ti, wfull=wfull: e.indirect_dma_start(
                        out=gw[0:nr, it % 2, :], out_offset=None,
                        in_=wfull[:, :],
                        in_offset=IndirectOffsetOnAxis(ap=idx_sb[0:nr, ti:ti + 1], axis=0)),
                    (sga, sgb)[it % 2])
                sch.wait("vector", (sga, sgb)[it % 2], gatherm[it])
                if it >= 2:
                    sch.wait("vector", se, q8m[it - 2])
                rowm[it] = sch.op("vector", lambda e, nr=nr, it=it: e.reduce_max(
                    gm[0:nr, it % 2:it % 2 + 1], gw[0:nr, it % 2, :], axis=AXX,
                    apply_absolute_value=True), sv)
                sch.wait("vector", sv, rowm[it])
                sch.op("vector", lambda e, nr=nr, gi=gi, ti=ti, it=it: e.tensor_copy(
                    wsc_sb[0:nr, gi, ti:ti + 1], gm[0:nr, it % 2:it % 2 + 1]), sv)
                sch.wait("vector", sv, rowm[it])
                m1 = sch.op("vector", lambda e, nr=nr, it=it: e.tensor_scalar_mul(
                    gm[0:nr, 4 + it % 2:5 + it % 2], gm[0:nr, it % 2:it % 2 + 1], IQD), sv)
                sch.wait("vector", sv, m1)
                mrs = sch.op("vector", lambda e, nr=nr, it=it: e.reciprocal(
                    gm[0:nr, 2 + it % 2:3 + it % 2], gm[0:nr, 4 + it % 2:5 + it % 2]), sv)
                if it == 0:
                    sch.wait("scalar", sv, marks_b0["const"])
                sch.wait("scalar", sv, mrs)
                q8m[it] = sch.op("scalar", lambda e, nr=nr, it=it: e.activation(
                    g8[0:nr, it % 2, :], gw[0:nr, it % 2, :], AF.Copy, bias=0.0,
                    scale=gm[0:nr, 2 + it % 2:3 + it % 2]), se)
                sch.wait("vector", se, q8m[it])
                b16m[it] = sch.op("vector", lambda e, nr=nr, it=it: e.tensor_copy(
                    g8b[0:nr, it % 2, :], g8[0:nr, it % 2, :]), sv)
                for f in range(16):
                    sch.wait("tensor", sv, b16m[it])
                    if ntp < 2:
                        sch.wait("tensor", sv, sv.v)
                    else:
                        sch.wait("tensor", se, tevac[ntp - 2])
                    mtr = sch.op("tensor", lambda e, nr=nr, it=it, f=f, ntp=ntp:
                                 e.transpose(pT[ntp % 2][:, 0:nr],
                                             g8b[0:nr, it % 2, 128 * f:128 * (f + 1)],
                                             ident_sb[0:nr, 0:nr]), sg)
                    sch.wait("scalar", sg, mtr)
                    if ntp >= 4:
                        sch.wait("scalar", sq, sq.v)
                    tevac[ntp] = sch.op("scalar", lambda e, nr=nr, ntp=ntp: e.copy(
                        gev[:, ntp % 4, 0:nr], pT[ntp % 2][:, 0:nr]), se)
                    sch.wait("gpsimd", se, tevac[ntp])
                    tdma[ntp] = sch.dma("gpsimd", lambda e, gi=gi, f=f, ti=ti, nr=nr, ntp=ntp:
                                        e.dma_start(
                                            out=w8gu_in[gi, f, :, tile_off[ti]:tile_off[ti] + nr],
                                            in_=gev[:, ntp % 4, 0:nr]), sq)
                    ntp += 1

            prev_tr_end = None
            prev_b16 = None
            for t in range(2):
                if prev_b16 is not None:
                    sch.wait("sync", sv, prev_b16)
                    sch.wait("sync", se, se.v)
                mwl = sch.dma("sync", lambda e, t=t: e.dma_start(
                    out=wd_sb[:, 0:1, :],
                    in_=wd_rows[128 * t:128 * (t + 1), :].rearrange("p d -> p () d")), swd)
                sch.wait("vector", swd, mwl)
                mr = sch.op("vector", lambda e, t=t: e.reduce_max(
                    gm[:, 4:5], wd_sb[:, 0, :], axis=AXX, apply_absolute_value=True), sv)
                sch.wait("vector", sv, mr)
                sch.op("vector", lambda e, t=t: e.tensor_copy(
                    wsc_sb[:, 2, t:t + 1], gm[:, 4:5]), sv)
                sch.wait("vector", sv, mr)
                m1 = sch.op("vector", lambda e: e.tensor_scalar_mul(gm[:, 3:4], gm[:, 4:5], IQD), sv)
                sch.wait("vector", sv, m1)
                mrs = sch.op("vector", lambda e: e.reciprocal(gm[:, 5:6], gm[:, 3:4]), sv)
                sch.wait("scalar", sv, mrs)
                if prev_tr_end is not None:
                    sch.wait("scalar", sg, prev_tr_end)
                m8 = sch.op("scalar", lambda e, t=t: e.activation(
                    wd8[:, 0, :], wd_sb[:, 0, :], AF.Copy, bias=0.0, scale=gm[:, 5:6]), se)
                sch.wait("vector", se, m8)
                mb16 = sch.op("vector", lambda e, t=t: e.tensor_copy(wd8b[:, 0, :], wd8[:, 0, :]), sv)
                prev_b16 = mb16
                for ic in range(44):
                    sch.wait("tensor", sv, mb16)
                    if ntp >= 2:
                        sch.wait("tensor", se, tevac[ntp - 2])
                    mtr = sch.op("tensor", lambda e, ic=ic, ntp=ntp:
                                 e.transpose(pT[ntp % 2][:, 0:128],
                                             wd8b[:, 0, 128 * ic:128 * (ic + 1)],
                                             ident_sb[:, :]), sg)
                    sch.wait("scalar", sg, mtr)
                    if ntp >= 4:
                        sch.wait("scalar", sq, sq.v)
                    tevac[ntp] = sch.op("scalar", lambda e, ntp=ntp: e.copy(
                        gev[:, ntp % 4, 0:128], pT[ntp % 2][:, 0:128]), se)
                    sch.wait("gpsimd", se, tevac[ntp])
                    tdma[ntp] = sch.dma("gpsimd", lambda e, ic=ic, t=t, ntp=ntp: e.dma_start(
                        out=w8d_in[ic, :, 128 * t:128 * (t + 1)], in_=gev[:, ntp % 4, 0:128]), sq)
                    ntp += 1
                prev_tr_end = sg.v

            sch.wait("gpsimd", sv, sv.v)
            for g in range(2):
                sch.dma_nc("gpsimd", wsc_in[g, 0:640].rearrange("(t p) -> p t", p=128),
                           wsc_sb[:, g, 0:5], sq)
                sch.dma_nc("gpsimd", wsc_in[g, 640:704].rearrange("(p o) -> p o", o=1),
                           wsc_sb[0:64, g, 5:6], sq)
            sch.dma_nc("gpsimd", wsc_in[2, 0:256].rearrange("(t p) -> p t", p=128),
                       wsc_sb[:, 2, 0:2], sq)
            mz = sch.op("vector", lambda e: e.memset(zpad[:, :], 0.0), sv)
            sch.wait("gpsimd", sv, mz)
            sch.dma("gpsimd", lambda e: e.dma_start(
                out=wsc_in[2, 256:704].rearrange("(o t) -> o t", o=1), in_=zpad[:, :]), sq)
            sch.wait("gpsimd", sq, sq.v)
            rg = [list(range(N_CORES))]
            sch.op("gpsimd", lambda e: e.collective_compute(
                "AllGather", ALU.bypass, replica_groups=rg,
                ins=[w8gu_in.opt()], outs=[w8gu_full.opt()]), cc)
            sch.op("gpsimd", lambda e: e.collective_compute(
                "AllGather", ALU.bypass, replica_groups=rg,
                ins=[w8d_in.opt()], outs=[w8d_full.opt()]), cc)
            sch.op("gpsimd", lambda e: e.collective_compute(
                "AllGather", ALU.bypass, replica_groups=rg,
                ins=[wsc_in.opt()], outs=[wsc_full.opt()]), cc)

            if dbg_phase == 1:
                sch.wait("gpsimd", sq, sq.v)
                sch.wait("gpsimd", cc, 3)
                for a, b in (("d_qsc", q_scr), ("d_ksc", k_scr), ("d_vsc", v_scr),
                             ("d_w8g", w8gu_full), ("d_w8d", w8d_full), ("d_wsc", wsc_full)):
                    sch.dma("gpsimd", lambda e, a=a, b=b: e.dma_start(out=dbg[a], in_=b), sq)
                sch.wait("gpsimd", sq, sq.v)
            sch.flush(blk)

        stA.close()

        if dbg_phase == 1:
            return nc

        sqa, sqb, skv, sms = sems["sqa"], sems["sqb"], sems["skv"], sems["sms"]
        marks_b0["b1_se"] = se.v

        stB = contextlib.ExitStack()
        st.enter_context(stB)
        attn_hi = stB.enter_context(nc.sbuf_tensor("attn_hi", [128, 64, 256], BF16))
        attn_lo = stB.enter_context(nc.sbuf_tensor("attn_lo", [128, 64, 256], BF16))
        with (
            nc.sbuf_tensor("masks_sb", [128, 4, 512], F32) as masks_sb,
            nc.sbuf_tensor("kh_sb", [128, S], BF16) as kh_sb,
            nc.sbuf_tensor("kl_sb", [128, S], BF16) as kl_sb,
            nc.sbuf_tensor("vh_sb", [128, 64, 130], BF16) as vh_sb,
            nc.sbuf_tensor("vl_sb", [128, 64, 130], BF16) as vl_sb,
            nc.sbuf_tensor("qh_sb", [128, 2, 512], BF16) as qh_sb,
            nc.sbuf_tensor("ql_sb", [128, 2, 512], BF16) as ql_sb,
            nc.sbuf_tensor("p32", [128, 2, 512], F32) as p32,
            nc.sbuf_tensor("ph_sb", [128, 2, 512], BF16) as ph_sb,
            nc.sbuf_tensor("pl_sb", [128, 2, 512], BF16) as pl_sb,
            nc.sbuf_tensor("rcp", [128, 4], F32) as rcp,
            nc.Block("b2") as blk,
        ):
            sch = Sched(nc)
            sch.wait("sync", sq, sq.v)
            sch.wait("gpsimd", sq, sq.v)
            m_msk = sch.dma("sync", lambda e: e.dma_start(
                out=masks_sb[:, :, :], in_=masks.rearrange("m p q -> p m q")), sms)
            m_ones = sch.op("vector", lambda e: e.memset(vh_sb[:, :, 128:129], 1.0), sv)
            m_zcol = sch.op("vector", lambda e: e.memset(vl_sb[:, :, 128:129], 0.0), sv)

            expm, splitm, av_of_kc = {}, {}, {}
            evacm = {}
            prevmhi = {}
            smark_g = {}
            prev_head_last = None
            prev_av_end = None
            for h in range(2):
                if prev_head_last is not None:
                    sch.wait("sync", sg, prev_head_last)
                    sch.wait("gpsimd", sg, prev_head_last)
                for hl, dstk in ((0, kh_sb), (1, kl_sb)):
                    sch.dma("sync", lambda e, h=h, hl=hl, dstk=dstk: e.dma_start(
                        out=dstk[:, :], in_=k_scr[h, hl, :, :]), skv)
                for hl, dstv in ((0, vh_sb), (1, vl_sb)):
                    for ch in range(64):
                        sch.dma("gpsimd", lambda e, h=h, hl=hl, dstv=dstv, ch=ch: e.dma_start(
                            out=dstv[:, ch, 0:128],
                            in_=v_scr[h, hl, 128 * ch:128 * (ch + 1), :]), skv)
                m_kv = skv.v
                qldm = {}
                for qg in range(16):
                    sqp = (sqa, sqb)[qg % 2]
                    if qg >= 2:
                        sch.wait("sync", sg, smark_g[(h, qg - 2)])
                    elif h == 1:
                        sch.wait("sync", sg, smark_g[(0, 14 + qg)])
                    for hl, dstq in ((0, qh_sb), (1, ql_sb)):
                        qldm[(qg, hl)] = sch.dma(
                            "sync", lambda e, h=h, hl=hl, qg=qg, dstq=dstq: e.dma_start(
                                out=dstq[:, qg % 2, :],
                                in_=q_scr[h, hl, :, 512 * qg:512 * (qg + 1)]), sqp)
                    L = 4 * qg + 4
                    smark = {}

                    def emit_scores(kc, qg=qg, h=h, L=L, qldm=qldm, smark=smark):
                        if kc >= 2:
                            sch.wait("tensor", se, expm[(h, qg, kc - 2)])
                        else:
                            sch.wait("tensor", se, se.v)
                        if kc == 0:
                            sch.wait("tensor", (sqa, sqb)[qg % 2], qldm[(qg, 1)])
                            sch.wait("tensor", skv, m_kv)
                        for pi in range(3):
                            lk = (kh_sb, kh_sb, kl_sb)[pi]
                            rq = (qh_sb, ql_sb, qh_sb)[pi]
                            sch.op("tensor", lambda e, lk=lk, rq=rq, kc=kc, qg=qg, pi=pi:
                                   e.matmul(ps[kc % 2][:, :],
                                            lk[:, 128 * kc:128 * (kc + 1)],
                                            rq[:, qg % 2, :],
                                            start=(pi == 0), stop=(pi == 2)),
                                   sg if pi == 2 else None)
                        mk = ("g", sg.v)
                        if kc >= L - 4:
                            p = kc - (L - 4)
                            sch.wait("vector", sg, sg.v)
                            if qg == 0 and kc == 0 and h == 0:
                                sch.wait("vector", sms, m_msk)
                            mm = sch.op("vector", lambda e, p=p, kc=kc: e.tensor_tensor(
                                ps[kc % 2][:, :], ps[kc % 2][:, :], masks_sb[:, p, :],
                                ALU.add), sv)
                            mk = ("v", mm)
                        smark[kc] = mk

                    def emit_pstage(kc, qg=qg, h=h, smark=smark):
                        kind, mk = smark[kc]
                        sch.wait("scalar", sv if kind == "v" else sg, mk)
                        if kc >= 2:
                            sch.wait("scalar", sp, splitm[(h, qg, kc - 2)])
                        else:
                            sch.wait("scalar", sp, sp.v)
                        expm[(h, qg, kc)] = sch.op("scalar", lambda e, kc=kc: e.activation(
                            p32[:, kc % 2, :], ps[kc % 2][:, :], AF.Exp,
                            bias=EXP_BIAS, scale=SCALE), se)
                        sch.wait("vector", se, expm[(h, qg, kc)])
                        if kc >= 2:
                            sch.wait("vector", sg, av_of_kc[(h, qg, kc - 2)])
                        elif prev_av_end is not None:
                            sch.wait("vector", sg, prev_av_end)
                        mph = sch.op("vector", lambda e, kc=kc: e.tensor_copy(
                            ph_sb[:, kc % 2, :], p32[:, kc % 2, :]), sv)
                        sch.wait("vector", sv, mph)
                        splitm[(h, qg, kc)] = sch.op(
                            "vector", lambda e, kc=kc: e.scalar_tensor_tensor(
                                pl_sb[:, kc % 2, :], p32[:, kc % 2, :], 1.0,
                                ph_sb[:, kc % 2, :], ALU.mult, ALU.subtract), sp)

                    def emit_av(kc, qg=qg, h=h, L=L):
                        sch.wait("tensor", sp, splitm[(h, qg, kc)])
                        if kc == 0:
                            if evacm:
                                sch.wait("tensor", sv, max(evacm.values()))
                            else:
                                sch.wait("tensor", sv, m_zcol)
                        for qt in range(4):
                            for pi in range(3):
                                lp = (ph_sb, ph_sb, pl_sb)[pi]
                                rv = (vh_sb, vl_sb, vh_sb)[pi]
                                sch.op("tensor",
                                       lambda e, lp=lp, rv=rv, kc=kc, qt=qt, pi=pi, L=L:
                                       e.matmul(ps[2 + qt][:, 0:129],
                                                lp[:, kc % 2, 128 * qt:128 * (qt + 1)],
                                                rv[:, kc, 0:129],
                                                start=(kc == 0 and pi == 0),
                                                stop=(kc == L - 1 and pi == 2)),
                                       sg if (qt == 3 and pi == 2) else None)
                        av_of_kc[(h, qg, kc)] = sg.v

                    for kc in range(L + 2):
                        if kc < L:
                            emit_scores(kc)
                        if 1 <= kc <= L:
                            emit_pstage(kc - 1)
                        if kc >= 2:
                            emit_av(kc - 2)
                    m_av_end = av_of_kc[(h, qg, L - 1)]
                    prev_av_end = m_av_end
                    for qt in range(4):
                        ch = 4 * qg + qt
                        sch.wait("vector", sg, m_av_end)
                        if qt in prevmhi:
                            sch.wait("vector", se, prevmhi[qt])
                        mrc = sch.op("vector", lambda e, qt=qt: e.reciprocal(
                            rcp[:, qt:qt + 1], ps[2 + qt][:, 128:129]), sv)
                        sch.wait("scalar", sv, mrc)
                        mhi = sch.op("scalar", lambda e, qt=qt, ch=ch, h=h: e.activation(
                            attn_hi[:, ch, 128 * h:128 * (h + 1)], ps[2 + qt][:, 0:128],
                            AF.Copy, bias=0.0, scale=rcp[:, qt:qt + 1]), se)
                        prevmhi[qt] = mhi
                        sch.wait("vector", se, mhi)
                        evacm[qt] = sch.op(
                            "vector", lambda e, qt=qt, ch=ch, h=h: e.scalar_tensor_tensor(
                                attn_lo[:, ch, 128 * h:128 * (h + 1)], ps[2 + qt][:, 0:128],
                                rcp[:, qt:qt + 1], attn_hi[:, ch, 128 * h:128 * (h + 1)],
                                ALU.mult, ALU.subtract), sv)
                    smark_g[(h, qg)] = sg.v
                prev_head_last = sg.v
            marks_b0["attn_done"] = max(evacm.values())
            sch.flush(blk)

        with (
            nc.sbuf_tensor("aTh", [128, 64, 2, 128], BF16) as aTh,
            nc.sbuf_tensor("aTl", [128, 64, 2, 128], BF16) as aTl,
            nc.sbuf_tensor("ost", [128, 4, 512], F32) as ost,
            nc.Block("b3") as blk,
        ):
            sch = Sched(nc)
            sch.wait("tensor", sv, marks_b0["attn_done"])
            sch.wait("tensor", se, se.v)
            trm = {}
            tmk = {}
            for nn in range(257):
                if nn < 256:
                    ch, r = nn // 4, nn % 4
                    hl, jj = r // 2, r % 2
                    srca = (attn_hi, attn_lo)[hl]
                    if nn >= 2:
                        sch.wait("tensor", se, trm[nn - 2])
                    tmk[nn] = sch.op("tensor", lambda e, srca=srca, ch=ch, jj=jj, nn=nn:
                                     e.transpose(pT[nn % 2][:, 0:128],
                                                 srca[:, ch, 128 * jj:128 * (jj + 1)],
                                                 ident_sb[:, :]), sg)
                else:
                    sch.wait("tensor", se, trm[nn - 2])
                    tmk[nn] = sch.op("tensor", lambda e, nn=nn: e.transpose(
                        pT[nn % 2][:, 0:128], ident_sb[:, :], ident_sb[:, :]), sg)
                if nn >= 1:
                    k = nn - 1
                    ch, r = k // 4, k % 4
                    hl, jj = r // 2, r % 2
                    dsta = (aTh, aTl)[hl]
                    sch.wait("scalar", sg, tmk[nn])
                    trm[k] = sch.op("scalar", lambda e, dsta=dsta, ch=ch, jj=jj, k=k:
                                    e.copy(dsta[:, ch, jj, :], pT[k % 2][:, 0:128]), se)
            m_aT = trm[255]
            stm = {}
            nst = 0
            so4 = (sems["soa"], sems["sob"], sems["soc"], sems["sod"])
            for half in range(2):
                opart = (o_part0, o_part1)[half]
                for ch in range(64):
                    for oci in range(2):
                        oc = 2 * half + oci
                        if ch == 0 and half == 0 and oci == 0:
                            sch.wait("tensor", se, m_aT)
                        if ch >= 1:
                            sch.wait("tensor", se, stm[(ch - 1, oc)])
                        elif half == 1:
                            sch.wait("tensor", se, stm[(63, oc - 2)])
                        for jj in range(2):
                            for pi in range(3):
                                la = (aTh, aTh, aTl)[pi]
                                rw = (woT_h, woT_l, woT_h)[pi]
                                sch.op("tensor",
                                       lambda e, la=la, rw=rw, ch=ch, jj=jj, oc=oc, pi=pi:
                                       e.matmul(ps[oc][:, :], la[:, ch, jj, :],
                                                rw[:, jj, 512 * oc:512 * (oc + 1)],
                                                start=(jj == 0 and pi == 0),
                                                stop=(jj == 1 and pi == 2)),
                                       sg if (jj == 1 and pi == 2) else None)
                        mmk = sg.v
                        sch.wait("scalar", sg, mmk)
                        sl = nst % 4
                        if nst >= 4:
                            sch.wait("scalar", so4[sl], so4[sl].v)
                        stm[(ch, oc)] = sch.op("scalar", lambda e, sl=sl, oc=oc: e.copy(
                            ost[:, sl, :], ps[oc][:, :]), se)
                        sch.wait("gpsimd", se, stm[(ch, oc)])
                        sch.dma("gpsimd", lambda e, ch=ch, oci=oci, opart=opart, sl=sl:
                                e.dma_start(
                                    out=opart[128 * ch:128 * (ch + 1), 512 * oci:512 * (oci + 1)],
                                    in_=ost[:, sl, :]), so4[sl])
                        nst += 1
                for s4 in so4:
                    sch.wait("gpsimd", s4, s4.v)
                ored = (o_red0, o_red1)[half]
                sch.op("gpsimd", lambda e, opart=opart, ored=ored: e.collective_compute(
                    "ReduceScatter", ALU.add, replica_groups=[list(range(N_CORES))],
                    ins=[opart.opt()], outs=[ored.opt()]), cc)
            sch.flush(blk)

        stB.close()

        with (
            nc.sbuf_tensor("xo5", [128, 2, D], F32) as xo5,
            nc.sbuf_tensor("oo5", [128, 2, D], F32) as oo5,
            nc.sbuf_tensor("bob", [128, D], F32) as bob,
            nc.sbuf_tensor("z5", [128, 2, D], F32) as z5,
            nc.sbuf_tensor("sqd", [128, D], F32) as sqd,
            nc.sbuf_tensor("st5", [128, 2, 8], F32) as st5,
            nc.sbuf_tensor("x15", [128, 2, D], F32) as x15,
            nc.sbuf_tensor("qx8", [128, 2, D], F8) as qx8,
            nc.sbuf_tensor("qxb", [128, 2, D], BF16) as qxb,
            nc.Block("b5") as blk,
        ):
            sch = Sched(nc)
            m_bob = sch.dma("sync", lambda e: e.dma_start(out=bob[:, :], in_=bo_b), sms)
            x1m, qxbm, qxtr = {}, {}, {}
            ntp = 0
            tevac5 = {}
            for tt in range(8):
                pp = tt % 2
                sxp = (sems["sxa"], sems["sxb"])[pp]
                if tt >= 2:
                    sch.wait("sync", se, x1m[tt - 2])
                    sch.wait("sync", sv, qxbm[tt - 2])
                ldm = sch.dma("sync", lambda e, tt=tt, pp=pp: e.dma_start(
                    out=xo5[:, pp, :], in_=x_own[128 * tt:128 * (tt + 1), :]), sxp)
                if tt == 0:
                    sch.wait("sync", cc, 5)
                sch.dma("sync", lambda e, tt=tt, pp=pp: e.dma_start(
                    out=oo5[:, pp, 0:1024], in_=o_red0[128 * tt:128 * (tt + 1), :]), sxp)
                ldo = sch.dma("sync", lambda e, tt=tt, pp=pp: e.dma_start(
                    out=oo5[:, pp, 1024:2048], in_=o_red1[128 * tt:128 * (tt + 1), :]), sxp)
                sch.wait("vector", sxp, ldo)
                if tt == 0:
                    sch.wait("vector", sms, m_bob)
                if tt >= 2:
                    sch.wait("vector", se, x1m[tt - 2])
                m1 = sch.op("vector", lambda e, pp=pp: e.tensor_add(
                    z5[:, pp, :], oo5[:, pp, :], bob[:, :]), sv)
                sch.wait("vector", sv, m1)
                m2 = sch.op("vector", lambda e, pp=pp: e.scalar_tensor_tensor(
                    z5[:, pp, :], z5[:, pp, :], 1.0, xo5[:, pp, :], ALU.mult, ALU.add,
                    accum_out=st5[:, pp, 0:1]), sv)
                sch.wait("vector", sv, m2)
                m3 = sch.op("vector", lambda e, pp=pp: e.tensor_scalar_mul(
                    st5[:, pp, 1:2], st5[:, pp, 0:1], -IND), sv)
                sch.wait("scalar", sv, m3)
                m4 = sch.op("scalar", lambda e, pp=pp: e.activation(
                    sqd[:, :], z5[:, pp, :], AF.Square, bias=st5[:, pp, 1:2], scale=1.0,
                    accum_out=st5[:, pp, 2:3]), se)
                sch.wait("scalar", se, m4)
                m5 = sch.op("scalar", lambda e, pp=pp: e.activation(
                    st5[:, pp, 3:4], st5[:, pp, 2:3], AF.Sqrt, bias=EPS, scale=IND), se)
                sch.wait("vector", se, m5)
                m6 = sch.op("vector", lambda e, pp=pp: e.reciprocal(
                    st5[:, pp, 4:5], st5[:, pp, 3:4]), sv)
                sch.wait("vector", sv, m6)
                m7 = sch.op("vector", lambda e, pp=pp: e.tensor_mul(
                    st5[:, pp, 5:6], st5[:, pp, 1:2], st5[:, pp, 4:5]), sv)
                sch.wait("scalar", sv, m7)
                x1m[tt] = sch.op("scalar", lambda e, pp=pp: e.activation(
                    x15[:, pp, :], z5[:, pp, :], AF.Identity,
                    bias=st5[:, pp, 5:6], scale=st5[:, pp, 4:5]), se)
                sch.wait("gpsimd", se, x1m[tt])
                sch.dma("gpsimd", lambda e, tt=tt, pp=pp: e.dma_start(
                    out=x1_scr[128 * tt:128 * (tt + 1), :], in_=x15[:, pp, :]), sq)
                sch.wait("vector", se, x1m[tt])
                m8 = sch.op("vector", lambda e, tt=tt, pp=pp: e.reduce_max(
                    st5[:, pp, 6:7], x15[:, pp, :], axis=AXX, apply_absolute_value=True), sv)
                sch.wait("vector", sv, m8)
                m9 = sch.op("vector", lambda e, tt=tt, pp=pp: e.tensor_scalar_mul(
                    mxd_sb[:, tt:tt + 1], st5[:, pp, 6:7], DQF), sv)
                m10 = sch.op("vector", lambda e, pp=pp: e.tensor_scalar_mul(
                    st5[:, pp, 7:8], st5[:, pp, 6:7], IQD), sv)
                sch.wait("vector", sv, m10)
                m11 = sch.op("vector", lambda e, pp=pp: e.reciprocal(
                    st5[:, pp, 6:7], st5[:, pp, 7:8]), sv)
                sch.wait("scalar", sv, m11)
                m12 = sch.op("scalar", lambda e, pp=pp: e.activation(
                    qx8[:, pp, :], x15[:, pp, :], AF.Copy, bias=0.0,
                    scale=st5[:, pp, 6:7]), se)
                sch.wait("vector", se, m12)
                if tt >= 2:
                    sch.wait("vector", sg, qxtr[tt - 2])
                qxbm[tt] = sch.op("vector", lambda e, pp=pp: e.tensor_copy(
                    qxb[:, pp, :], qx8[:, pp, :]), sv)
                for f in range(16):
                    sch.wait("tensor", sv, qxbm[tt])
                    if ntp >= 2:
                        sch.wait("tensor", se, tevac5[ntp - 2])
                    elif tt == 0 and f == 0:
                        sch.wait("tensor", se, se.v)
                    mtr = sch.op("tensor", lambda e, pp=pp, f=f, ntp=ntp:
                                 e.transpose(pT[ntp % 2][:, 0:128],
                                             qxb[:, pp, 128 * f:128 * (f + 1)],
                                             ident_sb[:, :]), sg)
                    sch.wait("scalar", sg, mtr)
                    tevac5[ntp] = sch.op("scalar", lambda e, f=f, tt=tt, ntp=ntp: e.copy(
                        qxT[:, f, 128 * tt:128 * (tt + 1)], pT[ntp % 2][:, 0:128]), se)
                    ntp += 1
                qxtr[tt] = sg.v
            marks_b0["qxT_done"] = tevac5[ntp - 1]
            sch.flush(blk)

        sw6a, sw6b, st6a, st6b = sems["sw6a"], sems["sw6b"], sems["st6a"], sems["st6b"]
        sto, sout = sems["sto"], sems["sout"]
        sx6a, sx6b = sems["sx6a"], sems["sx6b"]

        def gu_slices(ic):
            lo, hi2 = 512 * ic, 512 * (ic + 1)
            out = []
            r0, r1 = lo // ISH, (hi2 - 1) // ISH
            for r in range(r0, r1 + 1):
                a = max(lo, r * ISH)
                b = min(hi2, (r + 1) * ISH)
                out.append((r, a - r * ISH, a - lo, b - a))
            return out

        mtm = {}
        with (
            nc.sbuf_tensor("wgt", [128, 2, 16, 512], F8) as wgt,
            nc.sbuf_tensor("wut", [128, 2, 16, 512], F8) as wut,
            nc.sbuf_tensor("mwg", [128, 2, 512], F32) as mwg,
            nc.sbuf_tensor("mwu", [128, 2, 512], F32) as mwu,
            nc.sbuf_tensor("gst", [128, 2, 512], F32) as gst,
            nc.sbuf_tensor("ust", [128, 2, 512], F32) as ust,
            nc.sbuf_tensor("sil", [128, 2, 512], F32) as sil,
            nc.sbuf_tensor("tt6", [128, 2, 512], F32) as tt6,
            nc.sbuf_tensor("mt6", [128, 8, 2], F32) as mt6,
            nc.sbuf_tensor("ones_row", [1, 128], F32) as ones_row,
            nc.Block("b6a") as blk,
        ):
            sch = Sched(nc)
            m_or = sch.op("vector", lambda e: e.memset(ones_row[:, :], 1.0), sv)
            sch.wait("sync", cc, 3)
            tm, em6 = {}, {}
            gum = {}
            mwdone = {}
            for ic in range(11):
                swp = (sw6a, sw6b)[ic % 2]
                if ic >= 2:
                    sch.wait("sync", sg, gum[ic - 2])
                    sch.wait("sync", sv, tm[(ic - 2, 7)])
                for f in range(16):
                    for r, roff, coff, ln in gu_slices(ic):
                        sch.dma("sync", lambda e, ic=ic, f=f, r=r, roff=roff, coff=coff, ln=ln:
                                e.dma_start(out=wgt[:, ic % 2, f, coff:coff + ln],
                                            in_=w8gu_full[2 * r + 0, f, :, roff:roff + ln]), swp)
                        sch.dma("sync", lambda e, ic=ic, f=f, r=r, roff=roff, coff=coff, ln=ln:
                                e.dma_start(out=wut[:, ic % 2, f, coff:coff + ln],
                                            in_=w8gu_full[2 * r + 1, f, :, roff:roff + ln]), swp)
                for r, roff, coff, ln in gu_slices(ic):
                    sch.dma_nc("sync", mwg[0:1, ic % 2, coff:coff + ln],
                               wsc_full[3 * r + 0, roff:roff + ln].rearrange("(o d) -> o d", o=1), swp)
                    sch.dma_nc("sync", mwu[0:1, ic % 2, coff:coff + ln],
                               wsc_full[3 * r + 1, roff:roff + ln].rearrange("(o d) -> o d", o=1), swp)
                mload6 = swp.v
                sch.wait("tensor", swp, mload6)
                sch.wait("tensor", sv, m_or)
                if ic > 0:
                    sch.wait("tensor", sv, tm[(ic - 1, 7)])
                    sch.wait("tensor", se, em6[(ic - 1, 7)])
                mb1 = sch.op("tensor", lambda e, ic=ic: e.matmul(
                    ps[4][:, :], ones_row[:, :], mwg[0:1, ic % 2, :],
                    start=True, stop=True), sg)
                sch.wait("scalar", sg, mb1)
                mb1e = sch.op("scalar", lambda e, ic=ic: e.copy(mwg[:, ic % 2, :], ps[4][:, :]), se)
                sch.wait("tensor", se, mb1e)
                mb2 = sch.op("tensor", lambda e, ic=ic: e.matmul(
                    ps[4][:, :], ones_row[:, :], mwu[0:1, ic % 2, :],
                    start=True, stop=True), sg)
                sch.wait("scalar", sg, mb2)
                mb2e = sch.op("scalar", lambda e, ic=ic: e.copy(mwu[:, ic % 2, :], ps[4][:, :]), se)
                for tt in range(8):
                    bk = tt % 2
                    sch.wait("tensor", swp, mload6)
                    if ic == 0 and tt == 0:
                        sch.wait("tensor", se, marks_b0["qxT_done"])
                    key = (ic, tt - 2) if tt >= 2 else ((ic - 1, tt + 6) if ic > 0 else None)
                    if key is not None:
                        sch.wait("tensor", se, em6[key])
                    for f in range(16):
                        sch.op("tensor", lambda e, ic=ic, tt=tt, f=f, bk=bk: e.matmul(
                            ps[bk][:, :], qxT[:, f, 128 * tt:128 * (tt + 1)],
                            wgt[:, ic % 2, f, :], start=(f == 0), stop=(f == 15)),
                            sg if f == 15 else None)
                    for f in range(16):
                        sch.op("tensor", lambda e, ic=ic, tt=tt, f=f, bk=bk: e.matmul(
                            ps[2 + bk][:, :], qxT[:, f, 128 * tt:128 * (tt + 1)],
                            wut[:, ic % 2, f, :], start=(f == 0), stop=(f == 15)),
                            sg if f == 15 else None)
                    mgu = sg.v
                    sch.wait("scalar", sg, mgu)
                    sch.wait("scalar", se, mb2e)
                    if ic > 0 or tt >= 2:
                        sch.wait("scalar", sv, tm.get((ic, tt - 2), tm.get((ic - 1, tt + 6), 0)) or 0)
                    mg1 = sch.op("scalar", lambda e, tt=tt, bk=bk: e.activation(
                        gst[:, bk, :], ps[bk][:, :], AF.Copy, bias=0.0,
                        scale=mxd_sb[:, tt:tt + 1]), se)
                    mu1 = sch.op("scalar", lambda e, tt=tt, bk=bk: e.activation(
                        ust[:, bk, :], ps[2 + bk][:, :], AF.Copy, bias=0.0,
                        scale=mxd_sb[:, tt:tt + 1]), se)
                    em6[(ic, tt)] = mu1
                    sch.wait("vector", se, mu1)
                    mg2 = sch.op("vector", lambda e, ic=ic, bk=bk: e.tensor_mul(
                        gst[:, bk, :], gst[:, bk, :], mwg[:, ic % 2, :]), sv)
                    mu2 = sch.op("vector", lambda e, ic=ic, bk=bk: e.tensor_mul(
                        ust[:, bk, :], ust[:, bk, :], mwu[:, ic % 2, :]), sv)
                    sch.wait("scalar", sv, mg2)
                    ms = sch.op("scalar", lambda e, bk=bk: e.activation(
                        sil[:, bk, :], gst[:, bk, :], AF.Silu, bias=0.0, scale=1.0), se)
                    sch.wait("vector", se, ms)
                    sch.wait("vector", sv, mu2)
                    if ic > 0 or tt >= 2:
                        sch.wait("vector", sto, sto.v)
                    mt_ = sch.op("vector", lambda e, bk=bk: e.tensor_mul(
                        tt6[:, bk, :], sil[:, bk, :], ust[:, bk, :]), sv)
                    tm[(ic, tt)] = mt_
                    sch.wait("vector", sv, mt_)
                    mr = sch.op("vector", lambda e, tt=tt, bk=bk, ic=ic: e.reduce_max(
                        mt6[:, tt, ic % 2:ic % 2 + 1], tt6[:, bk, :], axis=AXX,
                        apply_absolute_value=True), sv)
                    sch.wait("vector", sv, mr)
                    if ic == 0:
                        mm_ = sch.op("vector", lambda e, tt=tt: e.tensor_copy(
                            mt_sb[:, tt:tt + 1], mt6[:, tt, 0:1]), sv)
                    else:
                        mm_ = sch.op("vector", lambda e, tt=tt, ic=ic: e.tensor_max(
                            mt_sb[:, tt:tt + 1], mt_sb[:, tt:tt + 1],
                            mt6[:, tt, ic % 2:ic % 2 + 1]), sv)
                    mtm[(ic, tt)] = mm_
                    sch.wait("gpsimd", sv, mt_)
                    sch.dma("gpsimd", lambda e, ic=ic, tt=tt, bk=bk: e.dma_start(
                        out=t_scr[128 * tt:128 * (tt + 1), 512 * ic:512 * (ic + 1)],
                        in_=tt6[:, bk, :]), sto)
                gum[ic] = sg.v
            marks_b0["t_done"] = sto.v
            marks_b0["mt_done"] = {tt: mtm[(10, tt)] for tt in range(8)}
            sch.flush(blk)

        with (
            nc.sbuf_tensor("wdt6", [128, 44, D], F8) as wdt6,
            nc.sbuf_tensor("mwd6", [128, 4, 512], F32) as mwd6,
            nc.sbuf_tensor("mwr", [1, 4, 512], F32) as mwr,
            nc.sbuf_tensor("ones_row2", [1, 128], F32) as ones_row2,
            nc.sbuf_tensor("tld", [128, 2, 512], F32) as tld,
            nc.sbuf_tensor("qt8", [128, 2, 512], F8) as qt8,
            nc.sbuf_tensor("qtb", [128, 2, 512], BF16) as qtb,
            nc.sbuf_tensor("qtT", [128, 44, 128], F8) as qtT,
            nc.sbuf_tensor("st6", [128, 2, 8], F32) as st6,
            nc.sbuf_tensor("zo6", [128, 4, 512], F32) as zo6,
            nc.sbuf_tensor("x16", [128, 2, D], F32) as x16,
            nc.sbuf_tensor("sq6", [128, D], F32) as sq6,
            nc.sbuf_tensor("fo6", [128, 2, D], F32) as fo6,
            nc.Block("b6b") as blk,
        ):
            sch = Sched(nc)
            sch.wait("sync", sq, sq.v)
            m_or2 = sch.op("vector", lambda e: e.memset(ones_row2[:, :], 1.0), sv)
            for r in range(8):
                for ic in range(44):
                    sch.dma("sync", lambda e, r=r, ic=ic: e.dma_start(
                        out=wdt6[:, ic, 256 * r:256 * (r + 1)],
                        in_=w8d_full[44 * r + ic, :, :]), swd)
                sch.dma_nc("sync", mwr[0:1, r // 2, 256 * (r % 2):256 * (r % 2) + 256],
                           wsc_full[3 * r + 2, 0:256].rearrange("(o d) -> o d", o=1), swd)
            m_wd = swd.v
            sch.wait("tensor", swd, m_wd)
            sch.wait("tensor", sv, m_or2)
            mwde = {}
            for oc in range(4):
                sch.wait("tensor", se, se.v)
                mb = sch.op("tensor", lambda e, oc=oc: e.matmul(
                    ps[4][:, :], ones_row2[:, :], mwr[0:1, oc, :], start=True, stop=True), sg)
                sch.wait("scalar", sg, mb)
                mwde[oc] = sch.op("scalar", lambda e, oc=oc: e.copy(
                    mwd6[:, oc, :], ps[4][:, :]), se)
            qtev, dnm, zevm, zmulm, n3m, tldm = {}, {}, {}, {}, {}, {}
            qtbm_d, qttr_d = {}, {}
            sttm, fom = {}, {}
            x16m = {}
            for tt in range(8):
                pp = tt % 2
                sch.wait("vector", sv, mtm[(10, tt)])
                if tt >= 2:
                    sch.wait("vector", se, qtev[(tt - 2, 43)])
                n1 = sch.op("vector", lambda e, tt=tt, pp=pp: e.tensor_scalar_mul(
                    st6[:, pp, 0:1], mt_sb[:, tt:tt + 1], IQD), sv)
                sch.wait("vector", sv, n1)
                n2 = sch.op("vector", lambda e, pp=pp: e.reciprocal(
                    st6[:, pp, 1:2], st6[:, pp, 0:1]), sv)
                n3 = sch.op("vector", lambda e, tt=tt, pp=pp: e.tensor_scalar_mul(
                    st6[:, pp, 2:3], mt_sb[:, tt:tt + 1], DQF), sv)
                n3m[tt] = n3
                for ic in range(11):
                    gic = tt * 11 + ic
                    gp = gic % 2
                    stp = (st6a, st6b)[gp]
                    if gic >= 2:
                        sch.wait("sync", se, tldm[gic - 2])
                    if tt == 0 and ic == 0:
                        sch.wait("sync", sto, marks_b0["t_done"])
                    mld = sch.dma("sync", lambda e, tt=tt, ic=ic, gp=gp: e.dma_start(
                        out=tld[:, gp, :],
                        in_=t_scr[128 * tt:128 * (tt + 1), 512 * ic:512 * (ic + 1)]), stp)
                    sch.wait("scalar", stp, mld)
                    sch.wait("scalar", sv, n2)
                    if gic >= 2:
                        sch.wait("scalar", sv, qtbm_d.get(gic - 2, 0) or 0)
                    mq8 = sch.op("scalar", lambda e, gp=gp, pp=pp: e.activation(
                        qt8[:, gp, :], tld[:, gp, :], AF.Copy, bias=0.0,
                        scale=st6[:, pp, 1:2]), se)
                    tldm[gic] = mq8
                    sch.wait("vector", se, mq8)
                    if gic >= 2:
                        sch.wait("vector", sg, qttr_d.get(gic - 2, 0) or 0)
                    qtbm = sch.op("vector", lambda e, gp=gp: e.tensor_copy(
                        qtb[:, gp, :], qt8[:, gp, :]), sv)
                    qtbm_d[gic] = qtbm
                    for j in range(4):
                        icc = 4 * ic + j
                        sch.wait("tensor", sv, qtbm)
                        if (tt, icc) != (0, 0):
                            pk = icc - 2 if icc >= 2 else 42 + icc
                            ptt = tt if icc >= 2 else tt - 1
                            sch.wait("tensor", se, qtev.get((ptt, pk), 0) or 0)
                        if tt > 0:
                            sch.wait("tensor", sg, dnm.get((tt - 1, 3), 0) or 0)
                        mtr = sch.op("tensor", lambda e, gp=gp, icc=icc, j=j: e.transpose(
                            pT[icc % 2][:, 0:128],
                            qtb[:, gp, 128 * j:128 * (j + 1)], ident_sb[:, :]), sg)
                        sch.wait("scalar", sg, mtr)
                        qtev[(tt, icc)] = sch.op("scalar", lambda e, ic=ic, j=j, icc=icc:
                                                 e.copy(qtT[:, icc, :],
                                                        pT[icc % 2][:, 0:128]), se)
                    qttr_d[gic] = sg.v
                for oc in range(4):
                    sch.wait("tensor", se, qtev[(tt, 43)])
                    if tt == 0 and oc == 0:
                        sch.wait("tensor", swd, m_wd)
                        sch.wait("tensor", se, se.v)
                    if tt > 0:
                        sch.wait("tensor", se, zevm.get((tt - 1, oc), 0) or 0)
                    for ic in range(44):
                        sch.op("tensor", lambda e, tt=tt, oc=oc, ic=ic: e.matmul(
                            ps[oc][:, :], qtT[:, ic, :],
                            wdt6[:, ic, 512 * oc:512 * (oc + 1)],
                            start=(ic == 0), stop=(ic == 43)),
                            sg if ic == 43 else None)
                    dnm[(tt, oc)] = sg.v
                    sch.wait("scalar", sg, dnm[(tt, oc)])
                    sch.wait("scalar", se, mwde[3])
                    if tt >= 2:
                        sch.wait("scalar", sv, sttm[tt - 2])
                    sch.wait("scalar", sv, n3m[tt])
                    mze = sch.op("scalar", lambda e, tt=tt, oc=oc, pp=pp: e.activation(
                        zo6[:, oc, :], ps[oc][:, :], AF.Copy, bias=0.0,
                        scale=st6[:, pp, 2:3]), se)
                    zevm[(tt, oc)] = mze
                    sch.wait("vector", se, mze)
                    zmulm[(tt, oc)] = sch.op("vector", lambda e, oc=oc: e.tensor_mul(
                        zo6[:, oc, :], zo6[:, oc, :], mwd6[:, oc, :]), sv)
                sx6p = (sx6a, sx6b)[pp]
                if tt >= 2:
                    sch.wait("sync", sv, sttm[tt - 2])
                x16m[tt] = sch.dma("sync", lambda e, tt=tt, pp=pp: e.dma_start(
                    out=x16[:, pp, :], in_=x1_scr[128 * tt:128 * (tt + 1), :]), sx6p)
                sch.wait("vector", sx6p, x16m[tt])
                sch.wait("vector", sv, zmulm[(tt, 3)])
                if tt >= 2:
                    sch.wait("vector", se, fom[tt - 2])
                    sch.wait("vector", sout, sout.v)
                mstt = sch.op("vector", lambda e, tt=tt, pp=pp: e.scalar_tensor_tensor(
                    fo6[:, pp, :], zo6[:, :, :].rearrange("p a b -> p (a b)"), 1.0,
                    x16[:, pp, :], ALU.mult, ALU.add, accum_out=st6[:, pp, 3:4]), sv)
                sttm[tt] = mstt
                sch.wait("vector", sv, mstt)
                k3 = sch.op("vector", lambda e, pp=pp: e.tensor_scalar_mul(
                    st6[:, pp, 4:5], st6[:, pp, 3:4], -IND), sv)
                sch.wait("scalar", sv, k3)
                k4 = sch.op("scalar", lambda e, pp=pp: e.activation(
                    sq6[:, :], fo6[:, pp, :], AF.Square, bias=st6[:, pp, 4:5], scale=1.0,
                    accum_out=st6[:, pp, 5:6]), se)
                sch.wait("scalar", se, k4)
                k5 = sch.op("scalar", lambda e, pp=pp: e.activation(
                    st6[:, pp, 6:7], st6[:, pp, 5:6], AF.Sqrt, bias=EPS, scale=IND), se)
                sch.wait("vector", se, k5)
                k6 = sch.op("vector", lambda e, pp=pp: e.reciprocal(
                    st6[:, pp, 7:8], st6[:, pp, 6:7]), sv)
                sch.wait("vector", sv, k6)
                k7 = sch.op("vector", lambda e, pp=pp: e.tensor_mul(
                    st6[:, pp, 4:5], st6[:, pp, 4:5], st6[:, pp, 7:8]), sv)
                sch.wait("scalar", sv, k7)
                fom[tt] = sch.op("scalar", lambda e, tt=tt, pp=pp: e.activation(
                    fo6[:, pp, :], fo6[:, pp, :], AF.Identity,
                    bias=st6[:, pp, 4:5], scale=st6[:, pp, 7:8]), se)
                sch.wait("gpsimd", se, fom[tt])
                sch.dma("gpsimd", lambda e, tt=tt, pp=pp: e.dma_start(
                    out=out_own[128 * tt:128 * (tt + 1), :], in_=fo6[:, pp, :]), sout)
            if dbg_phase == 2:
                sch.wait("gpsimd", sq, sq.v)
                sch.wait("gpsimd", sto, sto.v)
                sch.wait("gpsimd", sout, sout.v)
                for a, b in (("d_x1", x1_scr), ("d_t", t_scr)):
                    sch.dma("gpsimd", lambda e, a=a, b=b: e.dma_start(out=dbg[a], in_=b), sq)
                sch.wait("gpsimd", sv, sv.v)
                sch.dma("gpsimd", lambda e: e.dma_start(out=dbg["d_mt"], in_=mt_sb[:, :]), sq)
                sch.dma("gpsimd", lambda e: e.dma_start(out=dbg["d_qxT"], in_=qxT[:, :, :]), sq)
            sch.wait("gpsimd", sout, sout.v)
            sch.wait("gpsimd", sq, sq.v)
            sch.flush(blk)
    return nc


def make_inmaps(inputs):
    import ml_dtypes
    x = np.ascontiguousarray(np.asarray(inputs["L0_attn_x"], dtype=np.float32).reshape(S, D))
    xt = np.ascontiguousarray(x.T)
    WQ = np.asarray(inputs["L0_attn_WQ_w"], dtype=np.float32)
    WK = np.asarray(inputs["L0_attn_WK_w"], dtype=np.float32)
    WV = np.asarray(inputs["L0_attn_WV_w"], dtype=np.float32)
    WO = np.asarray(inputs["L0_attn_WO_w"], dtype=np.float32)
    bq = np.asarray(inputs["L0_attn_WQ_b"], dtype=np.float32)
    bk = np.asarray(inputs["L0_attn_WK_b"], dtype=np.float32)
    bv = np.asarray(inputs["L0_attn_WV_b"], dtype=np.float32)
    bo = np.asarray(inputs["L0_attn_WO_b"], dtype=np.float32)
    wg = np.ascontiguousarray(np.asarray(inputs["L0_ffn_w_gate"], dtype=np.float32))
    wu = np.ascontiguousarray(np.asarray(inputs["L0_ffn_w_up"], dtype=np.float32))
    wd = np.ascontiguousarray(np.asarray(inputs["L0_ffn_w_down"], dtype=np.float32))
    dn = np.asarray(inputs["L0_ffn_down_reorder_index"], dtype=np.int32)
    WOT = np.ascontiguousarray(WO.T)
    bo_b = np.ascontiguousarray(np.broadcast_to(bo, (128, D)))

    mask = np.zeros((4, 128, 512), dtype=np.float32)
    for j in range(4):
        for k in range(128):
            for q in range(512):
                if 128 * j + k > q:
                    mask[j, k, q] = -1e9
    import ml_dtypes as md
    ident = np.eye(128, dtype=md.bfloat16)

    in_maps = []
    for c in range(N_CORES):
        fsl = slice(OSH * c, OSH * (c + 1))
        tsl = slice(TOK * c, TOK * (c + 1))
        b3c = np.stack([bq[fsl], bk[fsl], bv[fsl]])
        b3c = np.ascontiguousarray(b3c.reshape(3, 2, 128).transpose(2, 0, 1))
        in_maps.append({
            "xt": xt,
            "x_own": np.ascontiguousarray(x[tsl]),
            "wq": np.ascontiguousarray(WQ[fsl]),
            "wk": np.ascontiguousarray(WK[fsl]),
            "wv": np.ascontiguousarray(WV[fsl]),
            "b3": b3c,
            "wo_t": np.ascontiguousarray(WOT[fsl]),
            "bo_b": bo_b,
            "wg_in": wg,
            "wu_in": wu,
            "wd_rows": np.ascontiguousarray(wd[fsl]),
            "dn_shard": np.ascontiguousarray(dn[ISH * c:ISH * (c + 1)]),
            "masks": mask,
            "identb": ident,
        })
    return in_maps


def kernel(**inputs):
    nc = build_nc(dbg_phase=0)
    in_maps = make_inmaps(inputs)
    res = run_bass_kernel_spmd(nc, in_maps, core_ids=list(range(N_CORES)))
    out = np.concatenate([r["out_own"] for r in res.results], axis=0)
    return out.reshape(1, S, D)


# version 09
# speedup vs baseline: 1.1404x; 1.0804x over previous
import contextlib
import math
import numpy as np
import concourse.bass as bass
import concourse.mybir as mybir
from concourse.bass import IndirectOffsetOnAxis
from concourse.bass_utils import run_bass_kernel_spmd

F32 = mybir.dt.float32
BF16 = mybir.dt.bfloat16
F8 = mybir.dt.float8e4
I32 = mybir.dt.int32
AF = mybir.ActivationFunctionType
ALU = mybir.AluOpType
AXX = mybir.AxisListType.X

S, D, NH, HD, II = 8192, 2048, 16, 128, 5632
TOK = 1024
ISH = II // 8
OSH = D // 8
N_CORES = 8
SCALE = 1.0 / math.sqrt(HD)
EXP_BIAS = -20.0
QD = 224.0
IQD = 1.0 / QD
DQF = 4.0 / (448.0 * 448.0)
EPS = 1e-5
IND = 1.0 / D

ENGS = ("sync", "vector", "scalar", "tensor", "gpsimd")


class Ctr:
    def __init__(self, sem):
        self.sem, self.v = sem, 0


class Sched:
    def __init__(self, nc):
        self.nc = nc
        self.ops = {k: [] for k in ENGS}

    def wait(self, eng, ctr, val=None):
        v = ctr.v if val is None else val
        if not v or v <= 0:
            return
        self.ops[eng].append(lambda e, s=ctr.sem, v=v: e.wait_ge(s, v))

    def op(self, eng, build, ctr=None, n=1):
        if ctr is None:
            self.ops[eng].append(lambda e, b=build: b(e))
            return None
        ctr.v += n
        self.ops[eng].append(lambda e, b=build, s=ctr.sem, m=n: b(e).then_inc(s, m))
        return ctr.v

    def dma(self, eng, build, ctr):
        return self.op(eng, build, ctr, 16)

    def dma_nc(self, eng, out, in_, ctr):
        nc = self.nc
        def b(e):
            with nc.allow_non_contiguous_dma(reason="tiny"):
                return e.dma_start(out=out, in_=in_)
        return self.op(eng, b, ctr, 16)

    def flush(self, blk):
        for name in ENGS:
            fns = self.ops[name]
            if not fns:
                continue
            getattr(blk, name)(lambda e, fns=fns: [f(e) for f in fns])
        self.ops = {k: [] for k in ENGS}


def build_nc(dbg_phase=0):
    nc = bass.Bass("TRN2", target_bir_lowering=False)
    dt = nc.dram_tensor

    xt = dt("xt", [D, S], F32, kind="ExternalInput").ap()
    x_own = dt("x_own", [TOK, D], F32, kind="ExternalInput").ap()
    wq = dt("wq", [OSH, D], F32, kind="ExternalInput").ap()
    wk = dt("wk", [OSH, D], F32, kind="ExternalInput").ap()
    wv = dt("wv", [OSH, D], F32, kind="ExternalInput").ap()
    b3 = dt("b3", [128, 3, 2], F32, kind="ExternalInput").ap()
    wo_t = dt("wo_t", [OSH, D], F32, kind="ExternalInput").ap()
    bo_b = dt("bo_b", [128, D], F32, kind="ExternalInput").ap()
    wg_in = dt("wg_in", [II, D], F32, kind="ExternalInput").ap()
    wu_in = dt("wu_in", [II, D], F32, kind="ExternalInput").ap()
    wd_rows = dt("wd_rows", [OSH, II], F32, kind="ExternalInput").ap()
    dn_shard = dt("dn_shard", [ISH], I32, kind="ExternalInput").ap()
    masks = dt("masks", [4, 128, 512], F32, kind="ExternalInput").ap()
    identb = dt("identb", [128, 128], BF16, kind="ExternalInput").ap()

    out_own = dt("out_own", [TOK, D], F32, kind="ExternalOutput").ap()

    q_scr = dt("q_scr", [2, 2, 128, S], BF16).ap()
    k_scr = dt("k_scr", [2, 2, 128, S], BF16).ap()
    v_scr = dt("v_scr", [2, 2, S, 128], BF16).ap()
    o_part0 = dt("o_part0", [S, D // 2], F32).ap()
    o_part1 = dt("o_part1", [S, D // 2], F32).ap()
    o_red0 = dt("o_red0", [TOK, D // 2], F32).ap()
    o_red1 = dt("o_red1", [TOK, D // 2], F32).ap()
    x1_scr = dt("x1_scr", [TOK, D], F32).ap()
    t_scr = dt("t_scr", [TOK, II], F32).ap()
    w8gu_in = dt("w8gu_in", [2, 16, 128, ISH], F8).ap()
    w8gu_full = dt("w8gu_full", [16, 16, 128, ISH], F8).ap()
    w8d_in = dt("w8d_in", [44, 128, OSH], F8).ap()
    w8d_full = dt("w8d_full", [352, 128, OSH], F8).ap()
    wsc_in = dt("wsc_in", [3, ISH], F32).ap()
    wsc_full = dt("wsc_full", [24, ISH], F32).ap()

    dbg = {}
    if dbg_phase == 1:
        for nm, shp, dty in (("d_qsc", [2, 2, 128, S], BF16), ("d_ksc", [2, 2, 128, S], BF16),
                             ("d_vsc", [2, 2, S, 128], BF16), ("d_w8g", [16, 16, 128, ISH], F8),
                             ("d_w8d", [352, 128, OSH], F8), ("d_wsc", [24, ISH], F32)):
            dbg[nm] = dt(nm, shp, dty, kind="ExternalOutput").ap()
    if dbg_phase == 2:
        dbg["d_ahi"] = dt("d_ahi", [128, 64, 256], BF16, kind="ExternalOutput").ap()
        dbg["d_alo"] = dt("d_alo", [128, 64, 256], BF16, kind="ExternalOutput").ap()
        dbg["d_opart"] = dt("d_opart", [S, D], F32, kind="ExternalOutput").ap()
        dbg["d_ored"] = dt("d_ored", [TOK, D], F32, kind="ExternalOutput").ap()
        dbg["d_x1"] = dt("d_x1", [TOK, D], F32, kind="ExternalOutput").ap()
        dbg["d_t"] = dt("d_t", [TOK, II], F32, kind="ExternalOutput").ap()
        dbg["d_mt"] = dt("d_mt", [128, 8], F32, kind="ExternalOutput").ap()
        dbg["d_qxT"] = dt("d_qxT", [128, 16, TOK], F8, kind="ExternalOutput").ap()
        dbg["d_aT"] = dt("d_aT", [128, 2, 2, 128], BF16, kind="ExternalOutput").ap()
        dbg["d_aTl"] = dt("d_aTl", [128, 2, 2, 128], BF16, kind="ExternalOutput").ap()

    st = contextlib.ExitStack()
    with st:
        cpool = st.enter_context(nc.sbuf_tensor("cpool", [128, 12], F32))
        ident_sb = st.enter_context(nc.sbuf_tensor("ident_sb", [128, 128], BF16))
        woT_h = st.enter_context(nc.sbuf_tensor("woT_h", [128, 2, D], BF16))
        woT_l = st.enter_context(nc.sbuf_tensor("woT_l", [128, 2, D], BF16))
        qxT = st.enter_context(nc.sbuf_tensor("qxT", [128, 16, TOK], F8))
        mxd_sb = st.enter_context(nc.sbuf_tensor("mxd_sb", [128, 8], F32))
        mt_sb = st.enter_context(nc.sbuf_tensor("mt_sb", [128, 8], F32))

        ps = [st.enter_context(nc.psum_tensor(f"ps{i}", [128, 512], F32)) for i in range(6)]
        pT = [st.enter_context(nc.psum_tensor(f"pT{i}", [128, 1024], BF16)) for i in range(2)]

        CONSTS = (0.0, 1.0, EXP_BIAS, SCALE, IQD, IND, EPS, -1e9, DQF)
        for i, cv in enumerate(CONSTS):
            nc.const_aps.register(F32, float(cv), cpool[:, i:i + 1])

        semnames = ("sd", "se", "sv", "sg", "sp", "sq", "cc",
                    "swa", "swb", "swd", "sga", "sgb",
                    "sxa", "sxb", "sb3", "skv", "sqa", "sqb", "sms",
                    "sw6a", "sw6b", "st6a", "st6b", "sto", "sout", "sx6a", "sx6b",
                    "soa", "sob", "soc", "sod",
                    "ska", "skb", "skc", "skd", "sva", "svb", "svc", "svd")
        sems = {n: Ctr(st.enter_context(nc.semaphore(n))) for n in semnames}
        sd, se, sv, sg = sems["sd"], sems["se"], sems["sv"], sems["sg"]
        sp, sq, cc = sems["sp"], sems["sq"], sems["cc"]
        swa, swb, swd = sems["swa"], sems["swb"], sems["swd"]
        sga, sgb = sems["sga"], sems["sgb"]

        marks_b0 = {}
        stA = contextlib.ExitStack()
        st.enter_context(stA)
        wqkvT_h = stA.enter_context(nc.sbuf_tensor("wqkvT_hS", [128, 3, 16, OSH], BF16))
        wqkvT_l = stA.enter_context(nc.sbuf_tensor("wqkvT_lS", [128, 3, 16, OSH], BF16))
        idx_sb = stA.enter_context(nc.sbuf_tensor("idx_sb", [128, 6], I32))
        gw = stA.enter_context(nc.sbuf_tensor("gw", [128, 2, D], F32))
        gm = stA.enter_context(nc.sbuf_tensor("gm", [128, 6], F32))
        g8 = stA.enter_context(nc.sbuf_tensor("g8", [128, 2, D], F8))
        g8b = stA.enter_context(nc.sbuf_tensor("g8b", [128, 2, D], BF16))
        gev = stA.enter_context(nc.sbuf_tensor("gev", [128, 4, 128], F8))
        wsc_sb = stA.enter_context(nc.sbuf_tensor("wsc_sb", [128, 3, 6], F32))
        zpad = stA.enter_context(nc.sbuf_tensor("zpad", [1, 448], F32))
        wd_sb = stA.enter_context(nc.sbuf_tensor("wd_sb", [128, 1, II], F32))
        wd8 = stA.enter_context(nc.sbuf_tensor("wd8", [128, 1, II], F8))
        wd8b = stA.enter_context(nc.sbuf_tensor("wd8b", [128, 1, II], BF16))
        with (
            nc.sbuf_tensor("wrow", [128, 2, D], F32) as wrow,
            nc.sbuf_tensor("wsp_h", [128, 2, D], BF16) as wsp_h,
            nc.sbuf_tensor("wsp_l", [128, 2, D], BF16) as wsp_l,
            nc.Block("b0") as blk,
        ):
            sch = Sched(nc)
            sch.dma("sync", lambda e: e.dma_start(out=ident_sb[:, :], in_=identb), sd)
            sch.dma_nc("sync", idx_sb[:, 0:5],
                       dn_shard[0:640].rearrange("(t p) -> p t", p=128), sd)
            sch.dma_nc("sync", idx_sb[0:64, 5:6],
                       dn_shard[640:704].rearrange("(p o) -> p o", o=1), sd)
            m_ident = m_idx = sd.v
            for i, cv in enumerate(CONSTS):
                sch.op("vector", lambda e, i=i, cv=cv: e.memset(cpool[:, i:i + 1], float(cv)))
            m_const = sch.op("vector", lambda e: e.engine_nop(), sv)

            evac_mark = {}
            n = 0
            prev_split = {0: None, 1: None}
            prev_last_tr = None
            for wi, wap in enumerate((wq, wk, wv, wo_t)):
                him, lom, wload = {}, {}, {}
                for t in range(2):
                    if prev_split[t] is not None:
                        sch.wait("sync", se, prev_split[t][0])
                        sch.wait("sync", sv, prev_split[t][1])
                    wload[t] = sch.dma(
                        "sync", lambda e, t=t, wap=wap: e.dma_start(
                            out=wrow[:, t:t + 1, :],
                            in_=wap[128 * t:128 * (t + 1), :].rearrange("p d -> p () d")),
                        (swa, swb)[t])
                for t in range(2):
                    dsth = woT_h[:, t, :] if wi == 3 else wsp_h[:, t, :]
                    dstl = woT_l[:, t, :] if wi == 3 else wsp_l[:, t, :]
                    sch.wait("scalar", (swa, swb)[t], wload[t])
                    if wi > 0 and wi < 4 and prev_last_tr is not None and wi != 3:
                        sch.wait("scalar", sg, prev_last_tr)
                    him[t] = sch.op("scalar", lambda e, d=dsth, t=t: e.copy(d, wrow[:, t, :]), se)
                    sch.wait("vector", se, him[t])
                    lom[t] = sch.op("vector", lambda e, dl=dstl, dh=dsth, t=t:
                                    e.scalar_tensor_tensor(dl, wrow[:, t, :], 1.0, dh,
                                                           ALU.mult, ALU.subtract), sv)
                    prev_split[t] = (him[t], lom[t])
                if wi == 3:
                    break
                for hl in range(2):
                    srcT = (wsp_h, wsp_l)[hl]
                    dst = (wqkvT_h, wqkvT_l)[hl]
                    for t in range(2):
                        for f in range(16):
                            if n == 0:
                                sch.wait("tensor", sd, m_ident)
                            sch.wait("tensor", (se, sv)[hl], (him, lom)[hl][t])
                            if n >= 2:
                                sch.wait("tensor", sv, evac_mark[n - 2])
                            mtr = sch.op("tensor", lambda e, srcT=srcT, t=t, f=f, n=n:
                                         e.transpose(pT[n % 2][:, 0:128],
                                                     srcT[:, t, 128 * f:128 * (f + 1)],
                                                     ident_sb[:, :]), sg)
                            sch.wait("vector", sg, mtr)
                            evac_mark[n] = sch.op(
                                "vector", lambda e, dst=dst, wi=wi, f=f, t=t, n=n:
                                e.tensor_copy(dst[:, wi, f, 128 * t:128 * (t + 1)],
                                              pT[n % 2][:, 0:128]), sv)
                            n += 1
                    prev_last_tr = sg.v
            m_awT = evac_mark[n - 1]
            marks_b0["awT"] = m_awT

            marks_b0["const"] = m_const
            marks_b0["idx"] = m_idx
            sch.flush(blk)

        with (
            nc.sbuf_tensor("b3_sb", [128, 3, 2], F32) as b3_sb,
            nc.sbuf_tensor("xf", [128, 2, 512], F32) as xf,
            nc.sbuf_tensor("xh", [128, 2, 512], BF16) as xh,
            nc.sbuf_tensor("xl", [128, 2, 512], BF16) as xl,
            nc.sbuf_tensor("qkst", [128, 6, 512], F32) as qkst,
            nc.sbuf_tensor("qkh", [128, 6, 512], BF16) as qkh,
            nc.sbuf_tensor("qkl", [128, 6, 512], BF16) as qkl,
            nc.sbuf_tensor("vev", [128, 4, 128], BF16) as vev,
            nc.Block("b1") as blk,
        ):
            sch = Sched(nc)
            sch.wait("tensor", sv, marks_b0["awT"])
            m_b3 = sch.dma("sync", lambda e: e.dma_start(out=b3_sb[:, :, :], in_=b3), sems["sb3"])
            xh_m, xl_m, mm_fc = {}, {}, {}
            prev = {}
            ntp = 0
            tevac2, tdma2 = {}, {}
            for tc in range(16):
                for fc in range(16):
                    i = tc * 16 + fc
                    if i >= 2:
                        ptc, pfc = (i - 2) // 16, (i - 2) % 16
                        sch.wait("sync", sv, xl_m[(ptc, pfc)])
                    mload = sch.dma("sync", lambda e, tc=tc, fc=fc: e.dma_start(
                        out=xf[:, fc % 2, :],
                        in_=xt[128 * fc:128 * (fc + 1), 512 * tc:512 * (tc + 1)]),
                        (sems["sxa"], sems["sxb"])[fc % 2])
                    sch.wait("scalar", (sems["sxa"], sems["sxb"])[fc % 2], mload)
                    if i >= 2:
                        sch.wait("scalar", sg, mm_fc[i - 2])
                    xh_m[(tc, fc)] = sch.op("scalar", lambda e, fc=fc: e.copy(
                        xh[:, fc % 2, :], xf[:, fc % 2, :]), se)
                    sch.wait("vector", se, xh_m[(tc, fc)])
                    if i >= 2:
                        sch.wait("vector", sg, mm_fc[i - 2])
                    xl_m[(tc, fc)] = sch.op("vector", lambda e, fc=fc: e.scalar_tensor_tensor(
                        xl[:, fc % 2, :], xf[:, fc % 2, :], 1.0, xh[:, fc % 2, :],
                        ALU.mult, ALU.subtract), sv)
                    sch.wait("tensor", se, xh_m[(tc, fc)])
                    sch.wait("tensor", sv, xl_m[(tc, fc)])
                    for oi in range(6):
                        wi, hd = oi // 2, oi % 2
                        if fc == 0 and tc > 0:
                            sch.wait("tensor", se, prev[oi]["stage"])
                        for pi in range(3):
                            lw = (wqkvT_h, wqkvT_h, wqkvT_l)[pi]
                            rx = (xh, xl, xh)[pi]
                            strt = (fc == 0 and pi == 0)
                            stop = (fc == 15 and pi == 2)
                            last_of_fc = (oi == 5 and pi == 2)
                            sch.op("tensor", lambda e, lw=lw, rx=rx, wi=wi, hd=hd, fc=fc, oi=oi,
                                   strt=strt, stop=stop:
                                   e.matmul(ps[oi][:, :], lw[:, wi, fc, 128 * hd:128 * (hd + 1)],
                                            rx[:, fc % 2, :], start=strt, stop=stop),
                                   sg if (stop or last_of_fc) else None)
                            if last_of_fc:
                                mm_fc[i] = sg.v
                cur = {}
                for oi in range(6):
                    wi, hd = oi // 2, oi % 2
                    mm_stop = sg.v - (5 - oi)
                    cur[oi] = {}
                    sch.wait("scalar", sg, mm_stop)
                    if tc == 0:
                        sch.wait("scalar", sems["sb3"], m_b3)
                    if tc > 0:
                        p = prev[oi]
                        sch.wait("scalar", sv, p["lo"])
                        if wi < 2:
                            sk = (sems["ska"], sems["skb"], sems["skc"], sems["skd"])[oi]
                            sch.wait("scalar", sk, sk.v)
                        else:
                            sch.wait("scalar", sg, p["lasttr"])
                    mstg = sch.op("scalar", lambda e, oi=oi, wi=wi, hd=hd: e.activation(
                        qkst[:, oi, :], ps[oi][:, :], AF.Identity,
                        bias=b3_sb[:, wi, hd:hd + 1], scale=1.0), se)
                    cur[oi]["stage"] = mstg
                    sch.wait("scalar", se, mstg)
                    mhi = sch.op("scalar", lambda e, oi=oi: e.copy(
                        qkh[:, oi, :], qkst[:, oi, :]), se)
                    cur[oi]["hi"] = mhi
                    sch.wait("vector", se, mhi)
                    mlo = sch.op("vector", lambda e, oi=oi: e.scalar_tensor_tensor(
                        qkl[:, oi, :], qkst[:, oi, :], 1.0, qkh[:, oi, :],
                        ALU.mult, ALU.subtract), sv)
                    cur[oi]["lo"] = mlo
                    if wi < 2:
                        scr = (q_scr, k_scr)[wi]
                        sk = (sems["ska"], sems["skb"], sems["skc"], sems["skd"])[oi]
                        sch.wait("gpsimd", se, mhi)
                        cur[oi]["hidma"] = sch.dma(
                            "gpsimd", lambda e, scr=scr, hd=hd, tc=tc, oi=oi: e.dma_start(
                                out=scr[hd, 0, :, 512 * tc:512 * (tc + 1)], in_=qkh[:, oi, :]), sk)
                        sch.wait("gpsimd", sv, mlo)
                        cur[oi]["lodma"] = sch.dma(
                            "gpsimd", lambda e, scr=scr, hd=hd, tc=tc, oi=oi: e.dma_start(
                                out=scr[hd, 1, :, 512 * tc:512 * (tc + 1)], in_=qkl[:, oi, :]), sk)
                    else:
                        for hl in range(2):
                            mm = (mhi, mlo)[hl]
                            srcv = (qkh, qkl)[hl]
                            for j in range(4):
                                sch.wait("tensor", (se, sv)[hl], mm)
                                if tc == 0 and ntp < 2:
                                    pass
                                elif ntp >= 2:
                                    sch.wait("tensor", sv, tevac2[ntp - 2])
                                mtr = sch.op("tensor", lambda e, srcv=srcv, oi=oi, j=j, ntp=ntp:
                                             e.transpose(pT[ntp % 2][:, 0:128],
                                                         srcv[:, oi, 128 * j:128 * (j + 1)],
                                                         ident_sb[:, :]), sg)
                                cur[oi]["lasttr"] = mtr
                                sv4 = (sems["sva"], sems["svb"], sems["svc"], sems["svd"])[ntp % 4]
                                sch.wait("vector", sg, mtr)
                                if ntp >= 4:
                                    sch.wait("vector", sv4, sv4.v)
                                tevac2[ntp] = sch.op("vector", lambda e, ntp=ntp: e.tensor_copy(
                                    vev[:, ntp % 4, :], pT[ntp % 2][:, 0:128]), sv)
                                sch.wait("gpsimd", sv, tevac2[ntp])
                                tdma2[ntp] = sch.dma(
                                    "gpsimd", lambda e, hd=hd, hl=hl, tc=tc, j=j, ntp=ntp: e.dma_start(
                                        out=v_scr[hd, hl, 512 * tc + 128 * j:512 * tc + 128 * (j + 1), :],
                                        in_=vev[:, ntp % 4, :]), sv4)
                                ntp += 1
                prev = cur

            tile_rows = [128] * 5 + [64]
            tile_off = [0, 128, 256, 384, 512, 640]
            q8m, rowm, b16m, gatherm, tevac, tdma = {}, {}, {}, {}, {}, {}
            ntp = 0
            for it in range(12):
                gi, ti = it // 6, it % 6
                nr = tile_rows[ti]
                wfull = (wg_in, wu_in)[gi]
                if it == 0:
                    sch.wait("gpsimd", sd, marks_b0["idx"])
                if it >= 2:
                    sch.wait("gpsimd", sv, b16m[it - 2])
                gatherm[it] = sch.dma(
                    "gpsimd", lambda e, nr=nr, it=it, ti=ti, wfull=wfull: e.indirect_dma_start(
                        out=gw[0:nr, it % 2, :], out_offset=None,
                        in_=wfull[:, :],
                        in_offset=IndirectOffsetOnAxis(ap=idx_sb[0:nr, ti:ti + 1], axis=0)),
                    (sga, sgb)[it % 2])
                sch.wait("vector", (sga, sgb)[it % 2], gatherm[it])
                if it >= 2:
                    sch.wait("vector", se, q8m[it - 2])
                rowm[it] = sch.op("vector", lambda e, nr=nr, it=it: e.reduce_max(
                    gm[0:nr, it % 2:it % 2 + 1], gw[0:nr, it % 2, :], axis=AXX,
                    apply_absolute_value=True), sv)
                sch.wait("vector", sv, rowm[it])
                sch.op("vector", lambda e, nr=nr, gi=gi, ti=ti, it=it: e.tensor_copy(
                    wsc_sb[0:nr, gi, ti:ti + 1], gm[0:nr, it % 2:it % 2 + 1]), sv)
                sch.wait("vector", sv, rowm[it])
                m1 = sch.op("vector", lambda e, nr=nr, it=it: e.tensor_scalar_mul(
                    gm[0:nr, 4 + it % 2:5 + it % 2], gm[0:nr, it % 2:it % 2 + 1], IQD), sv)
                sch.wait("vector", sv, m1)
                mrs = sch.op("vector", lambda e, nr=nr, it=it: e.reciprocal(
                    gm[0:nr, 2 + it % 2:3 + it % 2], gm[0:nr, 4 + it % 2:5 + it % 2]), sv)
                if it == 0:
                    sch.wait("scalar", sv, marks_b0["const"])
                sch.wait("scalar", sv, mrs)
                q8m[it] = sch.op("scalar", lambda e, nr=nr, it=it: e.activation(
                    g8[0:nr, it % 2, :], gw[0:nr, it % 2, :], AF.Copy, bias=0.0,
                    scale=gm[0:nr, 2 + it % 2:3 + it % 2]), se)
                sch.wait("vector", se, q8m[it])
                b16m[it] = sch.op("vector", lambda e, nr=nr, it=it: e.tensor_copy(
                    g8b[0:nr, it % 2, :], g8[0:nr, it % 2, :]), sv)
                for f in range(16):
                    sch.wait("tensor", sv, b16m[it])
                    if ntp < 2:
                        sch.wait("tensor", sv, sv.v)
                    else:
                        sch.wait("tensor", se, tevac[ntp - 2])
                    mtr = sch.op("tensor", lambda e, nr=nr, it=it, f=f, ntp=ntp:
                                 e.transpose(pT[ntp % 2][:, 0:nr],
                                             g8b[0:nr, it % 2, 128 * f:128 * (f + 1)],
                                             ident_sb[0:nr, 0:nr]), sg)
                    sch.wait("scalar", sg, mtr)
                    if ntp >= 4:
                        sch.wait("scalar", sq, sq.v)
                    tevac[ntp] = sch.op("scalar", lambda e, nr=nr, ntp=ntp: e.copy(
                        gev[:, ntp % 4, 0:nr], pT[ntp % 2][:, 0:nr]), se)
                    sch.wait("gpsimd", se, tevac[ntp])
                    tdma[ntp] = sch.dma("gpsimd", lambda e, gi=gi, f=f, ti=ti, nr=nr, ntp=ntp:
                                        e.dma_start(
                                            out=w8gu_in[gi, f, :, tile_off[ti]:tile_off[ti] + nr],
                                            in_=gev[:, ntp % 4, 0:nr]), sq)
                    ntp += 1

            prev_tr_end = None
            prev_b16 = None
            for t in range(2):
                if prev_b16 is not None:
                    sch.wait("sync", sv, prev_b16)
                    sch.wait("sync", se, se.v)
                mwl = sch.dma("sync", lambda e, t=t: e.dma_start(
                    out=wd_sb[:, 0:1, :],
                    in_=wd_rows[128 * t:128 * (t + 1), :].rearrange("p d -> p () d")), swd)
                sch.wait("vector", swd, mwl)
                mr = sch.op("vector", lambda e, t=t: e.reduce_max(
                    gm[:, 4:5], wd_sb[:, 0, :], axis=AXX, apply_absolute_value=True), sv)
                sch.wait("vector", sv, mr)
                sch.op("vector", lambda e, t=t: e.tensor_copy(
                    wsc_sb[:, 2, t:t + 1], gm[:, 4:5]), sv)
                sch.wait("vector", sv, mr)
                m1 = sch.op("vector", lambda e: e.tensor_scalar_mul(gm[:, 3:4], gm[:, 4:5], IQD), sv)
                sch.wait("vector", sv, m1)
                mrs = sch.op("vector", lambda e: e.reciprocal(gm[:, 5:6], gm[:, 3:4]), sv)
                sch.wait("scalar", sv, mrs)
                if prev_tr_end is not None:
                    sch.wait("scalar", sg, prev_tr_end)
                m8 = sch.op("scalar", lambda e, t=t: e.activation(
                    wd8[:, 0, :], wd_sb[:, 0, :], AF.Copy, bias=0.0, scale=gm[:, 5:6]), se)
                sch.wait("vector", se, m8)
                mb16 = sch.op("vector", lambda e, t=t: e.tensor_copy(wd8b[:, 0, :], wd8[:, 0, :]), sv)
                prev_b16 = mb16
                for ic in range(44):
                    sch.wait("tensor", sv, mb16)
                    if ntp >= 2:
                        sch.wait("tensor", se, tevac[ntp - 2])
                    mtr = sch.op("tensor", lambda e, ic=ic, ntp=ntp:
                                 e.transpose(pT[ntp % 2][:, 0:128],
                                             wd8b[:, 0, 128 * ic:128 * (ic + 1)],
                                             ident_sb[:, :]), sg)
                    sch.wait("scalar", sg, mtr)
                    if ntp >= 4:
                        sch.wait("scalar", sq, sq.v)
                    tevac[ntp] = sch.op("scalar", lambda e, ntp=ntp: e.copy(
                        gev[:, ntp % 4, 0:128], pT[ntp % 2][:, 0:128]), se)
                    sch.wait("gpsimd", se, tevac[ntp])
                    tdma[ntp] = sch.dma("gpsimd", lambda e, ic=ic, t=t, ntp=ntp: e.dma_start(
                        out=w8d_in[ic, :, 128 * t:128 * (t + 1)], in_=gev[:, ntp % 4, 0:128]), sq)
                    ntp += 1
                prev_tr_end = sg.v

            sch.wait("gpsimd", sv, sv.v)
            for g in range(2):
                sch.dma_nc("gpsimd", wsc_in[g, 0:640].rearrange("(t p) -> p t", p=128),
                           wsc_sb[:, g, 0:5], sq)
                sch.dma_nc("gpsimd", wsc_in[g, 640:704].rearrange("(p o) -> p o", o=1),
                           wsc_sb[0:64, g, 5:6], sq)
            sch.dma_nc("gpsimd", wsc_in[2, 0:256].rearrange("(t p) -> p t", p=128),
                       wsc_sb[:, 2, 0:2], sq)
            mz = sch.op("vector", lambda e: e.memset(zpad[:, :], 0.0), sv)
            sch.wait("gpsimd", sv, mz)
            sch.dma("gpsimd", lambda e: e.dma_start(
                out=wsc_in[2, 256:704].rearrange("(o t) -> o t", o=1), in_=zpad[:, :]), sq)
            sch.wait("gpsimd", sq, sq.v)
            rg = [list(range(N_CORES))]
            sch.op("gpsimd", lambda e: e.collective_compute(
                "AllGather", ALU.bypass, replica_groups=rg,
                ins=[w8gu_in.opt()], outs=[w8gu_full.opt()]), cc)
            sch.op("gpsimd", lambda e: e.collective_compute(
                "AllGather", ALU.bypass, replica_groups=rg,
                ins=[w8d_in.opt()], outs=[w8d_full.opt()]), cc)
            sch.op("gpsimd", lambda e: e.collective_compute(
                "AllGather", ALU.bypass, replica_groups=rg,
                ins=[wsc_in.opt()], outs=[wsc_full.opt()]), cc)

            if dbg_phase == 1:
                sch.wait("gpsimd", sq, sq.v)
                sch.wait("gpsimd", cc, 3)
                for a, b in (("d_qsc", q_scr), ("d_ksc", k_scr), ("d_vsc", v_scr),
                             ("d_w8g", w8gu_full), ("d_w8d", w8d_full), ("d_wsc", wsc_full)):
                    sch.dma("gpsimd", lambda e, a=a, b=b: e.dma_start(out=dbg[a], in_=b), sq)
                sch.wait("gpsimd", sq, sq.v)
            sch.flush(blk)

        stA.close()

        if dbg_phase == 1:
            return nc

        sqa, sqb, skv, sms = sems["sqa"], sems["sqb"], sems["skv"], sems["sms"]
        marks_b0["b1_se"] = se.v

        stB = contextlib.ExitStack()
        st.enter_context(stB)
        attn_hi = stB.enter_context(nc.sbuf_tensor("attn_hi", [128, 64, 256], BF16))
        attn_lo = stB.enter_context(nc.sbuf_tensor("attn_lo", [128, 64, 256], BF16))
        with (
            nc.sbuf_tensor("masks_sb", [128, 4, 512], F32) as masks_sb,
            nc.sbuf_tensor("kh_sb", [128, S], BF16) as kh_sb,
            nc.sbuf_tensor("kl_sb", [128, S], BF16) as kl_sb,
            nc.sbuf_tensor("vh_sb", [128, 64, 130], BF16) as vh_sb,
            nc.sbuf_tensor("vl_sb", [128, 64, 130], BF16) as vl_sb,
            nc.sbuf_tensor("qh_sb", [128, 2, 512], BF16) as qh_sb,
            nc.sbuf_tensor("ql_sb", [128, 2, 512], BF16) as ql_sb,
            nc.sbuf_tensor("p32", [128, 2, 512], F32) as p32,
            nc.sbuf_tensor("ph_sb", [128, 2, 512], BF16) as ph_sb,
            nc.sbuf_tensor("pl_sb", [128, 2, 512], BF16) as pl_sb,
            nc.sbuf_tensor("rcp", [128, 4], F32) as rcp,
            nc.Block("b2") as blk,
        ):
            sch = Sched(nc)
            sch.wait("sync", sq, sq.v)
            sch.wait("gpsimd", sq, sq.v)
            for _sn in ("ska", "skb", "skc", "skd", "sva", "svb", "svc", "svd"):
                sch.wait("sync", sems[_sn], sems[_sn].v)
                sch.wait("gpsimd", sems[_sn], sems[_sn].v)
            m_msk = sch.dma("sync", lambda e: e.dma_start(
                out=masks_sb[:, :, :], in_=masks.rearrange("m p q -> p m q")), sms)
            m_ones = sch.op("vector", lambda e: e.memset(vh_sb[:, :, 128:129], 1.0), sv)
            m_zcol = sch.op("vector", lambda e: e.memset(vl_sb[:, :, 128:129], 0.0), sv)

            expm, splitm, av_of_kc = {}, {}, {}
            evacm = {}
            prevmhi = {}
            smark_g = {}
            prev_head_last = None
            prev_av_end = None
            for h in range(2):
                if prev_head_last is not None:
                    sch.wait("sync", sg, prev_head_last)
                    sch.wait("gpsimd", sg, prev_head_last)
                for hl, dstk in ((0, kh_sb), (1, kl_sb)):
                    sch.dma("sync", lambda e, h=h, hl=hl, dstk=dstk: e.dma_start(
                        out=dstk[:, :], in_=k_scr[h, hl, :, :]), skv)
                for hl, dstv in ((0, vh_sb), (1, vl_sb)):
                    for ch in range(64):
                        sch.dma("gpsimd", lambda e, h=h, hl=hl, dstv=dstv, ch=ch: e.dma_start(
                            out=dstv[:, ch, 0:128],
                            in_=v_scr[h, hl, 128 * ch:128 * (ch + 1), :]), skv)
                m_kv = skv.v
                qldm = {}
                for qg in range(16):
                    sqp = (sqa, sqb)[qg % 2]
                    if qg >= 2:
                        sch.wait("sync", sg, smark_g[(h, qg - 2)])
                    elif h == 1:
                        sch.wait("sync", sg, smark_g[(0, 14 + qg)])
                    for hl, dstq in ((0, qh_sb), (1, ql_sb)):
                        qldm[(qg, hl)] = sch.dma(
                            "sync", lambda e, h=h, hl=hl, qg=qg, dstq=dstq: e.dma_start(
                                out=dstq[:, qg % 2, :],
                                in_=q_scr[h, hl, :, 512 * qg:512 * (qg + 1)]), sqp)
                    L = 4 * qg + 4
                    smark = {}

                    def emit_scores(kc, qg=qg, h=h, L=L, qldm=qldm, smark=smark):
                        if kc >= 2:
                            sch.wait("tensor", se, expm[(h, qg, kc - 2)])
                        else:
                            sch.wait("tensor", se, se.v)
                        if kc == 0:
                            sch.wait("tensor", (sqa, sqb)[qg % 2], qldm[(qg, 1)])
                            sch.wait("tensor", skv, m_kv)
                        for pi in range(3):
                            lk = (kh_sb, kh_sb, kl_sb)[pi]
                            rq = (qh_sb, ql_sb, qh_sb)[pi]
                            sch.op("tensor", lambda e, lk=lk, rq=rq, kc=kc, qg=qg, pi=pi:
                                   e.matmul(ps[kc % 2][:, :],
                                            lk[:, 128 * kc:128 * (kc + 1)],
                                            rq[:, qg % 2, :],
                                            start=(pi == 0), stop=(pi == 2)),
                                   sg if pi == 2 else None)
                        mk = ("g", sg.v)
                        if kc >= L - 4:
                            p = kc - (L - 4)
                            sch.wait("vector", sg, sg.v)
                            if qg == 0 and kc == 0 and h == 0:
                                sch.wait("vector", sms, m_msk)
                            mm = sch.op("vector", lambda e, p=p, kc=kc: e.tensor_tensor(
                                ps[kc % 2][:, :], ps[kc % 2][:, :], masks_sb[:, p, :],
                                ALU.add), sv)
                            mk = ("v", mm)
                        smark[kc] = mk

                    def emit_pstage(kc, qg=qg, h=h, smark=smark):
                        kind, mk = smark[kc]
                        sch.wait("scalar", sv if kind == "v" else sg, mk)
                        if kc >= 2:
                            sch.wait("scalar", sp, splitm[(h, qg, kc - 2)])
                        else:
                            sch.wait("scalar", sp, sp.v)
                        expm[(h, qg, kc)] = sch.op("scalar", lambda e, kc=kc: e.activation(
                            p32[:, kc % 2, :], ps[kc % 2][:, :], AF.Exp,
                            bias=EXP_BIAS, scale=SCALE), se)
                        sch.wait("vector", se, expm[(h, qg, kc)])
                        if kc >= 2:
                            sch.wait("vector", sg, av_of_kc[(h, qg, kc - 2)])
                        elif prev_av_end is not None:
                            sch.wait("vector", sg, prev_av_end)
                        mph = sch.op("vector", lambda e, kc=kc: e.tensor_copy(
                            ph_sb[:, kc % 2, :], p32[:, kc % 2, :]), sv)
                        sch.wait("vector", sv, mph)
                        splitm[(h, qg, kc)] = sch.op(
                            "vector", lambda e, kc=kc: e.scalar_tensor_tensor(
                                pl_sb[:, kc % 2, :], p32[:, kc % 2, :], 1.0,
                                ph_sb[:, kc % 2, :], ALU.mult, ALU.subtract), sp)

                    def emit_av(kc, qg=qg, h=h, L=L):
                        sch.wait("tensor", sp, splitm[(h, qg, kc)])
                        if kc == 0:
                            if evacm:
                                sch.wait("tensor", sv, max(evacm.values()))
                            else:
                                sch.wait("tensor", sv, m_zcol)
                        for qt in range(4):
                            for pi in range(3):
                                lp = (ph_sb, ph_sb, pl_sb)[pi]
                                rv = (vh_sb, vl_sb, vh_sb)[pi]
                                sch.op("tensor",
                                       lambda e, lp=lp, rv=rv, kc=kc, qt=qt, pi=pi, L=L:
                                       e.matmul(ps[2 + qt][:, 0:129],
                                                lp[:, kc % 2, 128 * qt:128 * (qt + 1)],
                                                rv[:, kc, 0:129],
                                                start=(kc == 0 and pi == 0),
                                                stop=(kc == L - 1 and pi == 2)),
                                       sg if (qt == 3 and pi == 2) else None)
                        av_of_kc[(h, qg, kc)] = sg.v

                    for kc in range(L + 2):
                        if kc < L:
                            emit_scores(kc)
                        if 1 <= kc <= L:
                            emit_pstage(kc - 1)
                        if kc >= 2:
                            emit_av(kc - 2)
                    m_av_end = av_of_kc[(h, qg, L - 1)]
                    prev_av_end = m_av_end
                    for qt in range(4):
                        ch = 4 * qg + qt
                        sch.wait("vector", sg, m_av_end)
                        if qt in prevmhi:
                            sch.wait("vector", se, prevmhi[qt])
                        mrc = sch.op("vector", lambda e, qt=qt: e.reciprocal(
                            rcp[:, qt:qt + 1], ps[2 + qt][:, 128:129]), sv)
                        sch.wait("scalar", sv, mrc)
                        mhi = sch.op("scalar", lambda e, qt=qt, ch=ch, h=h: e.activation(
                            attn_hi[:, ch, 128 * h:128 * (h + 1)], ps[2 + qt][:, 0:128],
                            AF.Copy, bias=0.0, scale=rcp[:, qt:qt + 1]), se)
                        prevmhi[qt] = mhi
                        sch.wait("vector", se, mhi)
                        evacm[qt] = sch.op(
                            "vector", lambda e, qt=qt, ch=ch, h=h: e.scalar_tensor_tensor(
                                attn_lo[:, ch, 128 * h:128 * (h + 1)], ps[2 + qt][:, 0:128],
                                rcp[:, qt:qt + 1], attn_hi[:, ch, 128 * h:128 * (h + 1)],
                                ALU.mult, ALU.subtract), sv)
                    smark_g[(h, qg)] = sg.v
                prev_head_last = sg.v
            marks_b0["attn_done"] = max(evacm.values())
            sch.flush(blk)

        with (
            nc.sbuf_tensor("aTh", [128, 64, 2, 128], BF16) as aTh,
            nc.sbuf_tensor("aTl", [128, 64, 2, 128], BF16) as aTl,
            nc.sbuf_tensor("ost", [128, 4, 512], F32) as ost,
            nc.Block("b3") as blk,
        ):
            sch = Sched(nc)
            sch.wait("tensor", sv, marks_b0["attn_done"])
            sch.wait("tensor", se, se.v)
            trm = {}
            tmk = {}
            for nn in range(257):
                if nn < 256:
                    ch, r = nn // 4, nn % 4
                    hl, jj = r // 2, r % 2
                    srca = (attn_hi, attn_lo)[hl]
                    if nn >= 2:
                        sch.wait("tensor", se, trm[nn - 2])
                    tmk[nn] = sch.op("tensor", lambda e, srca=srca, ch=ch, jj=jj, nn=nn:
                                     e.transpose(pT[nn % 2][:, 0:128],
                                                 srca[:, ch, 128 * jj:128 * (jj + 1)],
                                                 ident_sb[:, :]), sg)
                else:
                    sch.wait("tensor", se, trm[nn - 2])
                    tmk[nn] = sch.op("tensor", lambda e, nn=nn: e.transpose(
                        pT[nn % 2][:, 0:128], ident_sb[:, :], ident_sb[:, :]), sg)
                if nn >= 1:
                    k = nn - 1
                    ch, r = k // 4, k % 4
                    hl, jj = r // 2, r % 2
                    dsta = (aTh, aTl)[hl]
                    sch.wait("scalar", sg, tmk[nn])
                    trm[k] = sch.op("scalar", lambda e, dsta=dsta, ch=ch, jj=jj, k=k:
                                    e.copy(dsta[:, ch, jj, :], pT[k % 2][:, 0:128]), se)
            m_aT = trm[255]
            stm = {}
            nst = 0
            so4 = (sems["soa"], sems["sob"], sems["soc"], sems["sod"])
            for half in range(2):
                opart = (o_part0, o_part1)[half]
                for ch in range(64):
                    for oci in range(2):
                        oc = 2 * half + oci
                        if ch == 0 and half == 0 and oci == 0:
                            sch.wait("tensor", se, m_aT)
                        if ch >= 1:
                            sch.wait("tensor", se, stm[(ch - 1, oc)])
                        elif half == 1:
                            sch.wait("tensor", se, stm[(63, oc - 2)])
                        for jj in range(2):
                            for pi in range(3):
                                la = (aTh, aTh, aTl)[pi]
                                rw = (woT_h, woT_l, woT_h)[pi]
                                sch.op("tensor",
                                       lambda e, la=la, rw=rw, ch=ch, jj=jj, oc=oc, pi=pi:
                                       e.matmul(ps[oc][:, :], la[:, ch, jj, :],
                                                rw[:, jj, 512 * oc:512 * (oc + 1)],
                                                start=(jj == 0 and pi == 0),
                                                stop=(jj == 1 and pi == 2)),
                                       sg if (jj == 1 and pi == 2) else None)
                        mmk = sg.v
                        sch.wait("scalar", sg, mmk)
                        sl = nst % 4
                        if nst >= 4:
                            sch.wait("scalar", so4[sl], so4[sl].v)
                        stm[(ch, oc)] = sch.op("scalar", lambda e, sl=sl, oc=oc: e.copy(
                            ost[:, sl, :], ps[oc][:, :]), se)
                        sch.wait("gpsimd", se, stm[(ch, oc)])
                        sch.dma("gpsimd", lambda e, ch=ch, oci=oci, opart=opart, sl=sl:
                                e.dma_start(
                                    out=opart[128 * ch:128 * (ch + 1), 512 * oci:512 * (oci + 1)],
                                    in_=ost[:, sl, :]), so4[sl])
                        nst += 1
                for s4 in so4:
                    sch.wait("gpsimd", s4, s4.v)
                ored = (o_red0, o_red1)[half]
                sch.op("gpsimd", lambda e, opart=opart, ored=ored: e.collective_compute(
                    "ReduceScatter", ALU.add, replica_groups=[list(range(N_CORES))],
                    ins=[opart.opt()], outs=[ored.opt()]), cc)
            sch.flush(blk)

        stB.close()

        with (
            nc.sbuf_tensor("xo5", [128, 2, D], F32) as xo5,
            nc.sbuf_tensor("oo5", [128, 2, D], F32) as oo5,
            nc.sbuf_tensor("bob", [128, D], F32) as bob,
            nc.sbuf_tensor("z5", [128, 2, D], F32) as z5,
            nc.sbuf_tensor("sqd", [128, D], F32) as sqd,
            nc.sbuf_tensor("st5", [128, 2, 8], F32) as st5,
            nc.sbuf_tensor("x15", [128, 2, D], F32) as x15,
            nc.sbuf_tensor("qx8", [128, 2, D], F8) as qx8,
            nc.sbuf_tensor("qxb", [128, 2, D], BF16) as qxb,
            nc.Block("b5") as blk,
        ):
            sch = Sched(nc)
            m_bob = sch.dma("sync", lambda e: e.dma_start(out=bob[:, :], in_=bo_b), sms)
            x1m, qxbm, qxtr = {}, {}, {}
            ntp = 0
            tevac5 = {}
            for tt in range(8):
                pp = tt % 2
                sxp = (sems["sxa"], sems["sxb"])[pp]
                if tt >= 2:
                    sch.wait("sync", se, x1m[tt - 2])
                    sch.wait("sync", sv, qxbm[tt - 2])
                ldm = sch.dma("sync", lambda e, tt=tt, pp=pp: e.dma_start(
                    out=xo5[:, pp, :], in_=x_own[128 * tt:128 * (tt + 1), :]), sxp)
                if tt == 0:
                    sch.wait("sync", cc, 5)
                sch.dma("sync", lambda e, tt=tt, pp=pp: e.dma_start(
                    out=oo5[:, pp, 0:1024], in_=o_red0[128 * tt:128 * (tt + 1), :]), sxp)
                ldo = sch.dma("sync", lambda e, tt=tt, pp=pp: e.dma_start(
                    out=oo5[:, pp, 1024:2048], in_=o_red1[128 * tt:128 * (tt + 1), :]), sxp)
                sch.wait("vector", sxp, ldo)
                if tt == 0:
                    sch.wait("vector", sms, m_bob)
                if tt >= 2:
                    sch.wait("vector", se, x1m[tt - 2])
                m1 = sch.op("vector", lambda e, pp=pp: e.tensor_add(
                    z5[:, pp, :], oo5[:, pp, :], bob[:, :]), sv)
                sch.wait("vector", sv, m1)
                m2 = sch.op("vector", lambda e, pp=pp: e.scalar_tensor_tensor(
                    z5[:, pp, :], z5[:, pp, :], 1.0, xo5[:, pp, :], ALU.mult, ALU.add,
                    accum_out=st5[:, pp, 0:1]), sv)
                sch.wait("vector", sv, m2)
                m3 = sch.op("vector", lambda e, pp=pp: e.tensor_scalar_mul(
                    st5[:, pp, 1:2], st5[:, pp, 0:1], -IND), sv)
                sch.wait("scalar", sv, m3)
                m4 = sch.op("scalar", lambda e, pp=pp: e.activation(
                    sqd[:, :], z5[:, pp, :], AF.Square, bias=st5[:, pp, 1:2], scale=1.0,
                    accum_out=st5[:, pp, 2:3]), se)
                sch.wait("scalar", se, m4)
                m5 = sch.op("scalar", lambda e, pp=pp: e.activation(
                    st5[:, pp, 3:4], st5[:, pp, 2:3], AF.Sqrt, bias=EPS, scale=IND), se)
                sch.wait("vector", se, m5)
                m6 = sch.op("vector", lambda e, pp=pp: e.reciprocal(
                    st5[:, pp, 4:5], st5[:, pp, 3:4]), sv)
                sch.wait("vector", sv, m6)
                m7 = sch.op("vector", lambda e, pp=pp: e.tensor_mul(
                    st5[:, pp, 5:6], st5[:, pp, 1:2], st5[:, pp, 4:5]), sv)
                sch.wait("scalar", sv, m7)
                x1m[tt] = sch.op("scalar", lambda e, pp=pp: e.activation(
                    x15[:, pp, :], z5[:, pp, :], AF.Identity,
                    bias=st5[:, pp, 5:6], scale=st5[:, pp, 4:5]), se)
                sch.wait("gpsimd", se, x1m[tt])
                sch.dma("gpsimd", lambda e, tt=tt, pp=pp: e.dma_start(
                    out=x1_scr[128 * tt:128 * (tt + 1), :], in_=x15[:, pp, :]), sq)
                sch.wait("vector", se, x1m[tt])
                m8 = sch.op("vector", lambda e, tt=tt, pp=pp: e.reduce_max(
                    st5[:, pp, 6:7], x15[:, pp, :], axis=AXX, apply_absolute_value=True), sv)
                sch.wait("vector", sv, m8)
                m9 = sch.op("vector", lambda e, tt=tt, pp=pp: e.tensor_scalar_mul(
                    mxd_sb[:, tt:tt + 1], st5[:, pp, 6:7], DQF), sv)
                m10 = sch.op("vector", lambda e, pp=pp: e.tensor_scalar_mul(
                    st5[:, pp, 7:8], st5[:, pp, 6:7], IQD), sv)
                sch.wait("vector", sv, m10)
                m11 = sch.op("vector", lambda e, pp=pp: e.reciprocal(
                    st5[:, pp, 6:7], st5[:, pp, 7:8]), sv)
                sch.wait("scalar", sv, m11)
                m12 = sch.op("scalar", lambda e, pp=pp: e.activation(
                    qx8[:, pp, :], x15[:, pp, :], AF.Copy, bias=0.0,
                    scale=st5[:, pp, 6:7]), se)
                sch.wait("vector", se, m12)
                if tt >= 2:
                    sch.wait("vector", sg, qxtr[tt - 2])
                qxbm[tt] = sch.op("vector", lambda e, pp=pp: e.tensor_copy(
                    qxb[:, pp, :], qx8[:, pp, :]), sv)
                for f in range(16):
                    sch.wait("tensor", sv, qxbm[tt])
                    if ntp >= 2:
                        sch.wait("tensor", se, tevac5[ntp - 2])
                    elif tt == 0 and f == 0:
                        sch.wait("tensor", se, se.v)
                    mtr = sch.op("tensor", lambda e, pp=pp, f=f, ntp=ntp:
                                 e.transpose(pT[ntp % 2][:, 0:128],
                                             qxb[:, pp, 128 * f:128 * (f + 1)],
                                             ident_sb[:, :]), sg)
                    sch.wait("scalar", sg, mtr)
                    tevac5[ntp] = sch.op("scalar", lambda e, f=f, tt=tt, ntp=ntp: e.copy(
                        qxT[:, f, 128 * tt:128 * (tt + 1)], pT[ntp % 2][:, 0:128]), se)
                    ntp += 1
                qxtr[tt] = sg.v
            marks_b0["qxT_done"] = tevac5[ntp - 1]
            sch.flush(blk)

        sw6a, sw6b, st6a, st6b = sems["sw6a"], sems["sw6b"], sems["st6a"], sems["st6b"]
        sto, sout = sems["sto"], sems["sout"]
        sx6a, sx6b = sems["sx6a"], sems["sx6b"]

        def gu_slices(ic):
            lo, hi2 = 512 * ic, 512 * (ic + 1)
            out = []
            r0, r1 = lo // ISH, (hi2 - 1) // ISH
            for r in range(r0, r1 + 1):
                a = max(lo, r * ISH)
                b = min(hi2, (r + 1) * ISH)
                out.append((r, a - r * ISH, a - lo, b - a))
            return out

        mtm = {}
        with (
            nc.sbuf_tensor("wgt", [128, 2, 16, 512], F8) as wgt,
            nc.sbuf_tensor("wut", [128, 2, 16, 512], F8) as wut,
            nc.sbuf_tensor("mwg", [128, 2, 512], F32) as mwg,
            nc.sbuf_tensor("mwu", [128, 2, 512], F32) as mwu,
            nc.sbuf_tensor("gst", [128, 2, 512], F32) as gst,
            nc.sbuf_tensor("ust", [128, 2, 512], F32) as ust,
            nc.sbuf_tensor("sil", [128, 2, 512], F32) as sil,
            nc.sbuf_tensor("tt6", [128, 2, 512], F32) as tt6,
            nc.sbuf_tensor("mt6", [128, 8, 2], F32) as mt6,
            nc.sbuf_tensor("ones_row", [1, 128], F32) as ones_row,
            nc.Block("b6a") as blk,
        ):
            sch = Sched(nc)
            m_or = sch.op("vector", lambda e: e.memset(ones_row[:, :], 1.0), sv)
            sch.wait("sync", cc, 3)
            tm, em6 = {}, {}
            gum = {}
            mwdone = {}
            for ic in range(11):
                swp = (sw6a, sw6b)[ic % 2]
                if ic >= 2:
                    sch.wait("sync", sg, gum[ic - 2])
                    sch.wait("sync", sv, tm[(ic - 2, 7)])
                for f in range(16):
                    for r, roff, coff, ln in gu_slices(ic):
                        sch.dma("sync", lambda e, ic=ic, f=f, r=r, roff=roff, coff=coff, ln=ln:
                                e.dma_start(out=wgt[:, ic % 2, f, coff:coff + ln],
                                            in_=w8gu_full[2 * r + 0, f, :, roff:roff + ln]), swp)
                        sch.dma("sync", lambda e, ic=ic, f=f, r=r, roff=roff, coff=coff, ln=ln:
                                e.dma_start(out=wut[:, ic % 2, f, coff:coff + ln],
                                            in_=w8gu_full[2 * r + 1, f, :, roff:roff + ln]), swp)
                for r, roff, coff, ln in gu_slices(ic):
                    sch.dma_nc("sync", mwg[0:1, ic % 2, coff:coff + ln],
                               wsc_full[3 * r + 0, roff:roff + ln].rearrange("(o d) -> o d", o=1), swp)
                    sch.dma_nc("sync", mwu[0:1, ic % 2, coff:coff + ln],
                               wsc_full[3 * r + 1, roff:roff + ln].rearrange("(o d) -> o d", o=1), swp)
                mload6 = swp.v
                sch.wait("tensor", swp, mload6)
                sch.wait("tensor", sv, m_or)
                if ic > 0:
                    sch.wait("tensor", sv, tm[(ic - 1, 7)])
                    sch.wait("tensor", se, em6[(ic - 1, 7)])
                mb1 = sch.op("tensor", lambda e, ic=ic: e.matmul(
                    ps[4][:, :], ones_row[:, :], mwg[0:1, ic % 2, :],
                    start=True, stop=True), sg)
                sch.wait("scalar", sg, mb1)
                mb1e = sch.op("scalar", lambda e, ic=ic: e.copy(mwg[:, ic % 2, :], ps[4][:, :]), se)
                sch.wait("tensor", se, mb1e)
                mb2 = sch.op("tensor", lambda e, ic=ic: e.matmul(
                    ps[4][:, :], ones_row[:, :], mwu[0:1, ic % 2, :],
                    start=True, stop=True), sg)
                sch.wait("scalar", sg, mb2)
                mb2e = sch.op("scalar", lambda e, ic=ic: e.copy(mwu[:, ic % 2, :], ps[4][:, :]), se)
                for tt in range(8):
                    bk = tt % 2
                    sch.wait("tensor", swp, mload6)
                    if ic == 0 and tt == 0:
                        sch.wait("tensor", se, marks_b0["qxT_done"])
                    key = (ic, tt - 2) if tt >= 2 else ((ic - 1, tt + 6) if ic > 0 else None)
                    if key is not None:
                        sch.wait("tensor", se, em6[key])
                    for f in range(16):
                        sch.op("tensor", lambda e, ic=ic, tt=tt, f=f, bk=bk: e.matmul(
                            ps[bk][:, :], qxT[:, f, 128 * tt:128 * (tt + 1)],
                            wgt[:, ic % 2, f, :], start=(f == 0), stop=(f == 15)),
                            sg if f == 15 else None)
                    for f in range(16):
                        sch.op("tensor", lambda e, ic=ic, tt=tt, f=f, bk=bk: e.matmul(
                            ps[2 + bk][:, :], qxT[:, f, 128 * tt:128 * (tt + 1)],
                            wut[:, ic % 2, f, :], start=(f == 0), stop=(f == 15)),
                            sg if f == 15 else None)
                    mgu = sg.v
                    sch.wait("scalar", sg, mgu)
                    sch.wait("scalar", se, mb2e)
                    if ic > 0 or tt >= 2:
                        sch.wait("scalar", sv, tm.get((ic, tt - 2), tm.get((ic - 1, tt + 6), 0)) or 0)
                    mg1 = sch.op("scalar", lambda e, tt=tt, bk=bk: e.activation(
                        gst[:, bk, :], ps[bk][:, :], AF.Copy, bias=0.0,
                        scale=mxd_sb[:, tt:tt + 1]), se)
                    mu1 = sch.op("scalar", lambda e, tt=tt, bk=bk: e.activation(
                        ust[:, bk, :], ps[2 + bk][:, :], AF.Copy, bias=0.0,
                        scale=mxd_sb[:, tt:tt + 1]), se)
                    em6[(ic, tt)] = mu1
                    sch.wait("vector", se, mu1)
                    mg2 = sch.op("vector", lambda e, ic=ic, bk=bk: e.tensor_mul(
                        gst[:, bk, :], gst[:, bk, :], mwg[:, ic % 2, :]), sv)
                    mu2 = sch.op("vector", lambda e, ic=ic, bk=bk: e.tensor_mul(
                        ust[:, bk, :], ust[:, bk, :], mwu[:, ic % 2, :]), sv)
                    sch.wait("scalar", sv, mg2)
                    ms = sch.op("scalar", lambda e, bk=bk: e.activation(
                        sil[:, bk, :], gst[:, bk, :], AF.Silu, bias=0.0, scale=1.0), se)
                    sch.wait("vector", se, ms)
                    sch.wait("vector", sv, mu2)
                    if ic > 0 or tt >= 2:
                        sch.wait("vector", sto, sto.v)
                    mt_ = sch.op("vector", lambda e, bk=bk: e.tensor_mul(
                        tt6[:, bk, :], sil[:, bk, :], ust[:, bk, :]), sv)
                    tm[(ic, tt)] = mt_
                    sch.wait("vector", sv, mt_)
                    mr = sch.op("vector", lambda e, tt=tt, bk=bk, ic=ic: e.reduce_max(
                        mt6[:, tt, ic % 2:ic % 2 + 1], tt6[:, bk, :], axis=AXX,
                        apply_absolute_value=True), sv)
                    sch.wait("vector", sv, mr)
                    if ic == 0:
                        mm_ = sch.op("vector", lambda e, tt=tt: e.tensor_copy(
                            mt_sb[:, tt:tt + 1], mt6[:, tt, 0:1]), sv)
                    else:
                        mm_ = sch.op("vector", lambda e, tt=tt, ic=ic: e.tensor_max(
                            mt_sb[:, tt:tt + 1], mt_sb[:, tt:tt + 1],
                            mt6[:, tt, ic % 2:ic % 2 + 1]), sv)
                    mtm[(ic, tt)] = mm_
                    sch.wait("gpsimd", sv, mt_)
                    sch.dma("gpsimd", lambda e, ic=ic, tt=tt, bk=bk: e.dma_start(
                        out=t_scr[128 * tt:128 * (tt + 1), 512 * ic:512 * (ic + 1)],
                        in_=tt6[:, bk, :]), sto)
                gum[ic] = sg.v
            marks_b0["t_done"] = sto.v
            marks_b0["mt_done"] = {tt: mtm[(10, tt)] for tt in range(8)}
            sch.flush(blk)

        with (
            nc.sbuf_tensor("wdt6", [128, 44, D], F8) as wdt6,
            nc.sbuf_tensor("mwd6", [128, 4, 512], F32) as mwd6,
            nc.sbuf_tensor("mwr", [1, 4, 512], F32) as mwr,
            nc.sbuf_tensor("ones_row2", [1, 128], F32) as ones_row2,
            nc.sbuf_tensor("tld", [128, 2, 512], F32) as tld,
            nc.sbuf_tensor("qt8", [128, 2, 512], F8) as qt8,
            nc.sbuf_tensor("qtb", [128, 2, 512], BF16) as qtb,
            nc.sbuf_tensor("qtT", [128, 44, 128], F8) as qtT,
            nc.sbuf_tensor("st6", [128, 2, 8], F32) as st6,
            nc.sbuf_tensor("zo6", [128, 4, 512], F32) as zo6,
            nc.sbuf_tensor("x16", [128, 2, D], F32) as x16,
            nc.sbuf_tensor("sq6", [128, D], F32) as sq6,
            nc.sbuf_tensor("fo6", [128, 2, D], F32) as fo6,
            nc.Block("b6b") as blk,
        ):
            sch = Sched(nc)
            sch.wait("sync", sq, sq.v)
            m_or2 = sch.op("vector", lambda e: e.memset(ones_row2[:, :], 1.0), sv)
            for r in range(8):
                for ic in range(44):
                    sch.dma("sync", lambda e, r=r, ic=ic: e.dma_start(
                        out=wdt6[:, ic, 256 * r:256 * (r + 1)],
                        in_=w8d_full[44 * r + ic, :, :]), swd)
                sch.dma_nc("sync", mwr[0:1, r // 2, 256 * (r % 2):256 * (r % 2) + 256],
                           wsc_full[3 * r + 2, 0:256].rearrange("(o d) -> o d", o=1), swd)
            m_wd = swd.v
            sch.wait("tensor", swd, m_wd)
            sch.wait("tensor", sv, m_or2)
            mwde = {}
            for oc in range(4):
                sch.wait("tensor", se, se.v)
                mb = sch.op("tensor", lambda e, oc=oc: e.matmul(
                    ps[4][:, :], ones_row2[:, :], mwr[0:1, oc, :], start=True, stop=True), sg)
                sch.wait("scalar", sg, mb)
                mwde[oc] = sch.op("scalar", lambda e, oc=oc: e.copy(
                    mwd6[:, oc, :], ps[4][:, :]), se)
            qtev, dnm, zevm, zmulm, n3m, tldm = {}, {}, {}, {}, {}, {}
            qtbm_d, qttr_d = {}, {}
            sttm, fom = {}, {}
            x16m = {}
            for tt in range(8):
                pp = tt % 2
                sch.wait("vector", sv, mtm[(10, tt)])
                if tt >= 2:
                    sch.wait("vector", se, qtev[(tt - 2, 43)])
                n1 = sch.op("vector", lambda e, tt=tt, pp=pp: e.tensor_scalar_mul(
                    st6[:, pp, 0:1], mt_sb[:, tt:tt + 1], IQD), sv)
                sch.wait("vector", sv, n1)
                n2 = sch.op("vector", lambda e, pp=pp: e.reciprocal(
                    st6[:, pp, 1:2], st6[:, pp, 0:1]), sv)
                n3 = sch.op("vector", lambda e, tt=tt, pp=pp: e.tensor_scalar_mul(
                    st6[:, pp, 2:3], mt_sb[:, tt:tt + 1], DQF), sv)
                n3m[tt] = n3
                for ic in range(11):
                    gic = tt * 11 + ic
                    gp = gic % 2
                    stp = (st6a, st6b)[gp]
                    if gic >= 2:
                        sch.wait("sync", se, tldm[gic - 2])
                    if tt == 0 and ic == 0:
                        sch.wait("sync", sto, marks_b0["t_done"])
                    mld = sch.dma("sync", lambda e, tt=tt, ic=ic, gp=gp: e.dma_start(
                        out=tld[:, gp, :],
                        in_=t_scr[128 * tt:128 * (tt + 1), 512 * ic:512 * (ic + 1)]), stp)
                    sch.wait("scalar", stp, mld)
                    sch.wait("scalar", sv, n2)
                    if gic >= 2:
                        sch.wait("scalar", sv, qtbm_d.get(gic - 2, 0) or 0)
                    mq8 = sch.op("scalar", lambda e, gp=gp, pp=pp: e.activation(
                        qt8[:, gp, :], tld[:, gp, :], AF.Copy, bias=0.0,
                        scale=st6[:, pp, 1:2]), se)
                    tldm[gic] = mq8
                    sch.wait("vector", se, mq8)
                    if gic >= 2:
                        sch.wait("vector", sg, qttr_d.get(gic - 2, 0) or 0)
                    qtbm = sch.op("vector", lambda e, gp=gp: e.tensor_copy(
                        qtb[:, gp, :], qt8[:, gp, :]), sv)
                    qtbm_d[gic] = qtbm
                    for j in range(4):
                        icc = 4 * ic + j
                        sch.wait("tensor", sv, qtbm)
                        if (tt, icc) != (0, 0):
                            pk = icc - 2 if icc >= 2 else 42 + icc
                            ptt = tt if icc >= 2 else tt - 1
                            sch.wait("tensor", se, qtev.get((ptt, pk), 0) or 0)
                        if tt > 0:
                            sch.wait("tensor", sg, dnm.get((tt - 1, 3), 0) or 0)
                        mtr = sch.op("tensor", lambda e, gp=gp, icc=icc, j=j: e.transpose(
                            pT[icc % 2][:, 0:128],
                            qtb[:, gp, 128 * j:128 * (j + 1)], ident_sb[:, :]), sg)
                        sch.wait("scalar", sg, mtr)
                        qtev[(tt, icc)] = sch.op("scalar", lambda e, ic=ic, j=j, icc=icc:
                                                 e.copy(qtT[:, icc, :],
                                                        pT[icc % 2][:, 0:128]), se)
                    qttr_d[gic] = sg.v
                for oc in range(4):
                    sch.wait("tensor", se, qtev[(tt, 43)])
                    if tt == 0 and oc == 0:
                        sch.wait("tensor", swd, m_wd)
                        sch.wait("tensor", se, se.v)
                    if tt > 0:
                        sch.wait("tensor", se, zevm.get((tt - 1, oc), 0) or 0)
                    for ic in range(44):
                        sch.op("tensor", lambda e, tt=tt, oc=oc, ic=ic: e.matmul(
                            ps[oc][:, :], qtT[:, ic, :],
                            wdt6[:, ic, 512 * oc:512 * (oc + 1)],
                            start=(ic == 0), stop=(ic == 43)),
                            sg if ic == 43 else None)
                    dnm[(tt, oc)] = sg.v
                    sch.wait("scalar", sg, dnm[(tt, oc)])
                    sch.wait("scalar", se, mwde[3])
                    if tt >= 2:
                        sch.wait("scalar", sv, sttm[tt - 2])
                    sch.wait("scalar", sv, n3m[tt])
                    mze = sch.op("scalar", lambda e, tt=tt, oc=oc, pp=pp: e.activation(
                        zo6[:, oc, :], ps[oc][:, :], AF.Copy, bias=0.0,
                        scale=st6[:, pp, 2:3]), se)
                    zevm[(tt, oc)] = mze
                    sch.wait("vector", se, mze)
                    zmulm[(tt, oc)] = sch.op("vector", lambda e, oc=oc: e.tensor_mul(
                        zo6[:, oc, :], zo6[:, oc, :], mwd6[:, oc, :]), sv)
                sx6p = (sx6a, sx6b)[pp]
                if tt >= 2:
                    sch.wait("sync", sv, sttm[tt - 2])
                x16m[tt] = sch.dma("sync", lambda e, tt=tt, pp=pp: e.dma_start(
                    out=x16[:, pp, :], in_=x1_scr[128 * tt:128 * (tt + 1), :]), sx6p)
                sch.wait("vector", sx6p, x16m[tt])
                sch.wait("vector", sv, zmulm[(tt, 3)])
                if tt >= 2:
                    sch.wait("vector", se, fom[tt - 2])
                    sch.wait("vector", sout, sout.v)
                mstt = sch.op("vector", lambda e, tt=tt, pp=pp: e.scalar_tensor_tensor(
                    fo6[:, pp, :], zo6[:, :, :].rearrange("p a b -> p (a b)"), 1.0,
                    x16[:, pp, :], ALU.mult, ALU.add, accum_out=st6[:, pp, 3:4]), sv)
                sttm[tt] = mstt
                sch.wait("vector", sv, mstt)
                k3 = sch.op("vector", lambda e, pp=pp: e.tensor_scalar_mul(
                    st6[:, pp, 4:5], st6[:, pp, 3:4], -IND), sv)
                sch.wait("scalar", sv, k3)
                k4 = sch.op("scalar", lambda e, pp=pp: e.activation(
                    sq6[:, :], fo6[:, pp, :], AF.Square, bias=st6[:, pp, 4:5], scale=1.0,
                    accum_out=st6[:, pp, 5:6]), se)
                sch.wait("scalar", se, k4)
                k5 = sch.op("scalar", lambda e, pp=pp: e.activation(
                    st6[:, pp, 6:7], st6[:, pp, 5:6], AF.Sqrt, bias=EPS, scale=IND), se)
                sch.wait("vector", se, k5)
                k6 = sch.op("vector", lambda e, pp=pp: e.reciprocal(
                    st6[:, pp, 7:8], st6[:, pp, 6:7]), sv)
                sch.wait("vector", sv, k6)
                k7 = sch.op("vector", lambda e, pp=pp: e.tensor_mul(
                    st6[:, pp, 4:5], st6[:, pp, 4:5], st6[:, pp, 7:8]), sv)
                sch.wait("scalar", sv, k7)
                fom[tt] = sch.op("scalar", lambda e, tt=tt, pp=pp: e.activation(
                    fo6[:, pp, :], fo6[:, pp, :], AF.Identity,
                    bias=st6[:, pp, 4:5], scale=st6[:, pp, 7:8]), se)
                sch.wait("gpsimd", se, fom[tt])
                sch.dma("gpsimd", lambda e, tt=tt, pp=pp: e.dma_start(
                    out=out_own[128 * tt:128 * (tt + 1), :], in_=fo6[:, pp, :]), sout)
            if dbg_phase == 2:
                sch.wait("gpsimd", sq, sq.v)
                sch.wait("gpsimd", sto, sto.v)
                sch.wait("gpsimd", sout, sout.v)
                for a, b in (("d_x1", x1_scr), ("d_t", t_scr)):
                    sch.dma("gpsimd", lambda e, a=a, b=b: e.dma_start(out=dbg[a], in_=b), sq)
                sch.wait("gpsimd", sv, sv.v)
                sch.dma("gpsimd", lambda e: e.dma_start(out=dbg["d_mt"], in_=mt_sb[:, :]), sq)
                sch.dma("gpsimd", lambda e: e.dma_start(out=dbg["d_qxT"], in_=qxT[:, :, :]), sq)
            sch.wait("gpsimd", sout, sout.v)
            sch.wait("gpsimd", sq, sq.v)
            sch.flush(blk)
    return nc


def make_inmaps(inputs):
    import ml_dtypes
    x = np.ascontiguousarray(np.asarray(inputs["L0_attn_x"], dtype=np.float32).reshape(S, D))
    xt = np.ascontiguousarray(x.T)
    WQ = np.asarray(inputs["L0_attn_WQ_w"], dtype=np.float32)
    WK = np.asarray(inputs["L0_attn_WK_w"], dtype=np.float32)
    WV = np.asarray(inputs["L0_attn_WV_w"], dtype=np.float32)
    WO = np.asarray(inputs["L0_attn_WO_w"], dtype=np.float32)
    bq = np.asarray(inputs["L0_attn_WQ_b"], dtype=np.float32)
    bk = np.asarray(inputs["L0_attn_WK_b"], dtype=np.float32)
    bv = np.asarray(inputs["L0_attn_WV_b"], dtype=np.float32)
    bo = np.asarray(inputs["L0_attn_WO_b"], dtype=np.float32)
    wg = np.ascontiguousarray(np.asarray(inputs["L0_ffn_w_gate"], dtype=np.float32))
    wu = np.ascontiguousarray(np.asarray(inputs["L0_ffn_w_up"], dtype=np.float32))
    wd = np.ascontiguousarray(np.asarray(inputs["L0_ffn_w_down"], dtype=np.float32))
    dn = np.asarray(inputs["L0_ffn_down_reorder_index"], dtype=np.int32)
    WOT = np.ascontiguousarray(WO.T)
    bo_b = np.ascontiguousarray(np.broadcast_to(bo, (128, D)))

    mask = np.zeros((4, 128, 512), dtype=np.float32)
    for j in range(4):
        for k in range(128):
            for q in range(512):
                if 128 * j + k > q:
                    mask[j, k, q] = -1e9
    import ml_dtypes as md
    ident = np.eye(128, dtype=md.bfloat16)

    in_maps = []
    for c in range(N_CORES):
        fsl = slice(OSH * c, OSH * (c + 1))
        tsl = slice(TOK * c, TOK * (c + 1))
        b3c = np.stack([bq[fsl], bk[fsl], bv[fsl]])
        b3c = np.ascontiguousarray(b3c.reshape(3, 2, 128).transpose(2, 0, 1))
        in_maps.append({
            "xt": xt,
            "x_own": np.ascontiguousarray(x[tsl]),
            "wq": np.ascontiguousarray(WQ[fsl]),
            "wk": np.ascontiguousarray(WK[fsl]),
            "wv": np.ascontiguousarray(WV[fsl]),
            "b3": b3c,
            "wo_t": np.ascontiguousarray(WOT[fsl]),
            "bo_b": bo_b,
            "wg_in": wg,
            "wu_in": wu,
            "wd_rows": np.ascontiguousarray(wd[fsl]),
            "dn_shard": np.ascontiguousarray(dn[ISH * c:ISH * (c + 1)]),
            "masks": mask,
            "identb": ident,
        })
    return in_maps


def kernel(**inputs):
    nc = build_nc(dbg_phase=0)
    in_maps = make_inmaps(inputs)
    res = run_bass_kernel_spmd(nc, in_maps, core_ids=list(range(N_CORES)))
    out = np.concatenate([r["out_own"] for r in res.results], axis=0)
    return out.reshape(1, S, D)


# version 10 (final)
# speedup vs baseline: 1.2392x; 1.0867x over previous
import contextlib
import math
import numpy as np
import concourse.bass as bass
import concourse.mybir as mybir
from concourse.bass import IndirectOffsetOnAxis
from concourse.bass_utils import run_bass_kernel_spmd

F32 = mybir.dt.float32
BF16 = mybir.dt.bfloat16
F8 = mybir.dt.float8e4
I32 = mybir.dt.int32
AF = mybir.ActivationFunctionType
ALU = mybir.AluOpType
AXX = mybir.AxisListType.X

S, D, NH, HD, II = 8192, 2048, 16, 128, 5632
TOK = 1024
ISH = II // 8
OSH = D // 8
N_CORES = 8
SCALE = 1.0 / math.sqrt(HD)
EXP_BIAS = -20.0
QD = 224.0
IQD = 1.0 / QD
DQF = 4.0 / (448.0 * 448.0)
EPS = 1e-5
IND = 1.0 / D

ENGS = ("sync", "vector", "scalar", "tensor", "gpsimd")


class Ctr:
    def __init__(self, sem):
        self.sem, self.v = sem, 0


class Sched:
    def __init__(self, nc):
        self.nc = nc
        self.ops = {k: [] for k in ENGS}

    def wait(self, eng, ctr, val=None):
        v = ctr.v if val is None else val
        if not v or v <= 0:
            return
        self.ops[eng].append(lambda e, s=ctr.sem, v=v: e.wait_ge(s, v))

    def op(self, eng, build, ctr=None, n=1):
        if ctr is None:
            self.ops[eng].append(lambda e, b=build: b(e))
            return None
        ctr.v += n
        self.ops[eng].append(lambda e, b=build, s=ctr.sem, m=n: b(e).then_inc(s, m))
        return ctr.v

    def dma(self, eng, build, ctr):
        return self.op(eng, build, ctr, 16)

    def dma_nc(self, eng, out, in_, ctr):
        nc = self.nc
        def b(e):
            with nc.allow_non_contiguous_dma(reason="tiny"):
                return e.dma_start(out=out, in_=in_)
        return self.op(eng, b, ctr, 16)

    def flush(self, blk):
        for name in ENGS:
            fns = self.ops[name]
            if not fns:
                continue
            getattr(blk, name)(lambda e, fns=fns: [f(e) for f in fns])
        self.ops = {k: [] for k in ENGS}


def build_nc(dbg_phase=0):
    nc = bass.Bass("TRN2", target_bir_lowering=False)
    dt = nc.dram_tensor

    xt = dt("xt", [D, S], F32, kind="ExternalInput").ap()
    x_own = dt("x_own", [TOK, D], F32, kind="ExternalInput").ap()
    wq = dt("wq", [OSH, D], F32, kind="ExternalInput").ap()
    wk = dt("wk", [OSH, D], F32, kind="ExternalInput").ap()
    wv = dt("wv", [OSH, D], F32, kind="ExternalInput").ap()
    b3 = dt("b3", [128, 3, 2], F32, kind="ExternalInput").ap()
    wo_t = dt("wo_t", [OSH, D], F32, kind="ExternalInput").ap()
    bo_b = dt("bo_b", [128, D], F32, kind="ExternalInput").ap()
    wg_in = dt("wg_in", [II, D], F32, kind="ExternalInput").ap()
    wu_in = dt("wu_in", [II, D], F32, kind="ExternalInput").ap()
    wd_rows = dt("wd_rows", [OSH, II], F32, kind="ExternalInput").ap()
    dn_shard = dt("dn_shard", [ISH], I32, kind="ExternalInput").ap()
    masks = dt("masks", [4, 128, 512], F32, kind="ExternalInput").ap()
    identb = dt("identb", [128, 128], BF16, kind="ExternalInput").ap()

    out_own = dt("out_own", [TOK, D], F32, kind="ExternalOutput").ap()

    q_scr = dt("q_scr", [2, 2, 128, S], BF16).ap()
    k_scr = dt("k_scr", [2, 2, 128, S], BF16).ap()
    v_scr = dt("v_scr", [2, 2, S, 128], BF16).ap()
    o_part0 = dt("o_part0", [S, D // 2], F32).ap()
    o_part1 = dt("o_part1", [S, D // 2], F32).ap()
    o_red0 = dt("o_red0", [TOK, D // 2], F32).ap()
    o_red1 = dt("o_red1", [TOK, D // 2], F32).ap()
    x1_scr = dt("x1_scr", [TOK, D], F32).ap()
    t_scr = dt("t_scr", [TOK, II], F32).ap()
    w8gu_in = dt("w8gu_in", [2, 16, 128, ISH], F8).ap()
    w8gu_full = dt("w8gu_full", [16, 16, 128, ISH], F8).ap()
    w8d_in = dt("w8d_in", [44, 128, OSH], F8).ap()
    w8d_full = dt("w8d_full", [352, 128, OSH], F8).ap()
    wsc_in = dt("wsc_in", [3, ISH], F32).ap()
    wsc_full = dt("wsc_full", [24, ISH], F32).ap()

    dbg = {}
    if dbg_phase == 1:
        for nm, shp, dty in (("d_qsc", [2, 2, 128, S], BF16), ("d_ksc", [2, 2, 128, S], BF16),
                             ("d_vsc", [2, 2, S, 128], BF16), ("d_w8g", [16, 16, 128, ISH], F8),
                             ("d_w8d", [352, 128, OSH], F8), ("d_wsc", [24, ISH], F32)):
            dbg[nm] = dt(nm, shp, dty, kind="ExternalOutput").ap()
    if dbg_phase == 2:
        dbg["d_ahi"] = dt("d_ahi", [128, 64, 256], BF16, kind="ExternalOutput").ap()
        dbg["d_alo"] = dt("d_alo", [128, 64, 256], BF16, kind="ExternalOutput").ap()
        dbg["d_opart"] = dt("d_opart", [S, D], F32, kind="ExternalOutput").ap()
        dbg["d_ored"] = dt("d_ored", [TOK, D], F32, kind="ExternalOutput").ap()
        dbg["d_x1"] = dt("d_x1", [TOK, D], F32, kind="ExternalOutput").ap()
        dbg["d_t"] = dt("d_t", [TOK, II], F32, kind="ExternalOutput").ap()
        dbg["d_mt"] = dt("d_mt", [128, 8], F32, kind="ExternalOutput").ap()
        dbg["d_qxT"] = dt("d_qxT", [128, 16, TOK], F8, kind="ExternalOutput").ap()
        dbg["d_aT"] = dt("d_aT", [128, 2, 2, 128], BF16, kind="ExternalOutput").ap()
        dbg["d_aTl"] = dt("d_aTl", [128, 2, 2, 128], BF16, kind="ExternalOutput").ap()

    st = contextlib.ExitStack()
    with st:
        cpool = st.enter_context(nc.sbuf_tensor("cpool", [128, 12], F32))
        ident_sb = st.enter_context(nc.sbuf_tensor("ident_sb", [128, 128], BF16))
        woT_h = st.enter_context(nc.sbuf_tensor("woT_h", [128, 2, D], BF16))
        woT_l = st.enter_context(nc.sbuf_tensor("woT_l", [128, 2, D], BF16))
        qxT = st.enter_context(nc.sbuf_tensor("qxT", [128, 16, TOK], F8))
        mxd_sb = st.enter_context(nc.sbuf_tensor("mxd_sb", [128, 8], F32))
        mt_sb = st.enter_context(nc.sbuf_tensor("mt_sb", [128, 8], F32))

        ps = [st.enter_context(nc.psum_tensor(f"ps{i}", [128, 512], F32)) for i in range(6)]
        pT = [st.enter_context(nc.psum_tensor(f"pT{i}", [128, 1024], BF16)) for i in range(2)]

        CONSTS = (0.0, 1.0, EXP_BIAS, SCALE, IQD, IND, EPS, -1e9, DQF)
        for i, cv in enumerate(CONSTS):
            nc.const_aps.register(F32, float(cv), cpool[:, i:i + 1])

        semnames = ("sd", "se", "sv", "sg", "sp", "sq", "cc",
                    "swa", "swb", "swd", "sga", "sgb",
                    "sxa", "sxb", "sb3", "skv", "sqa", "sqb", "sms",
                    "sw6a", "sw6b", "st6a", "st6b", "sto", "sout", "sx6a", "sx6b",
                    "soa", "sob", "soc", "sod",
                    "ska", "skb", "skc", "skd", "sva", "svb", "svc", "svd",
                    "sta", "stb", "sea", "seb", "sec", "sed")
        sems = {n: Ctr(st.enter_context(nc.semaphore(n))) for n in semnames}
        sd, se, sv, sg = sems["sd"], sems["se"], sems["sv"], sems["sg"]
        sp, sq, cc = sems["sp"], sems["sq"], sems["cc"]
        swa, swb, swd = sems["swa"], sems["swb"], sems["swd"]
        sga, sgb = sems["sga"], sems["sgb"]

        marks_b0 = {}
        stA = contextlib.ExitStack()
        st.enter_context(stA)
        wqkvT_h = stA.enter_context(nc.sbuf_tensor("wqkvT_hS", [128, 3, 16, OSH], BF16))
        wqkvT_l = stA.enter_context(nc.sbuf_tensor("wqkvT_lS", [128, 3, 16, OSH], BF16))
        idx_sb = stA.enter_context(nc.sbuf_tensor("idx_sb", [128, 6], I32))
        gw = stA.enter_context(nc.sbuf_tensor("gw", [128, 2, D], F32))
        gm = stA.enter_context(nc.sbuf_tensor("gm", [128, 6], F32))
        g8 = stA.enter_context(nc.sbuf_tensor("g8", [128, 2, D], F8))
        g8b = stA.enter_context(nc.sbuf_tensor("g8b", [128, 2, D], BF16))
        gev = stA.enter_context(nc.sbuf_tensor("gev", [128, 4, 128], F8))
        wsc_sb = stA.enter_context(nc.sbuf_tensor("wsc_sb", [128, 3, 6], F32))
        zpad = stA.enter_context(nc.sbuf_tensor("zpad", [1, 448], F32))
        wd_sb = stA.enter_context(nc.sbuf_tensor("wd_sb", [128, 1, II], F32))
        wd8 = stA.enter_context(nc.sbuf_tensor("wd8", [128, 1, II], F8))
        wd8b = stA.enter_context(nc.sbuf_tensor("wd8b", [128, 1, II], BF16))
        with (
            nc.sbuf_tensor("wrow", [128, 2, D], F32) as wrow,
            nc.sbuf_tensor("wsp_h", [128, 2, D], BF16) as wsp_h,
            nc.sbuf_tensor("wsp_l", [128, 2, D], BF16) as wsp_l,
            nc.Block("b0") as blk,
        ):
            sch = Sched(nc)
            sch.dma("sync", lambda e: e.dma_start(out=ident_sb[:, :], in_=identb), sd)
            sch.dma_nc("sync", idx_sb[:, 0:5],
                       dn_shard[0:640].rearrange("(t p) -> p t", p=128), sd)
            sch.dma_nc("sync", idx_sb[0:64, 5:6],
                       dn_shard[640:704].rearrange("(p o) -> p o", o=1), sd)
            m_ident = m_idx = sd.v
            for i, cv in enumerate(CONSTS):
                sch.op("vector", lambda e, i=i, cv=cv: e.memset(cpool[:, i:i + 1], float(cv)))
            m_const = sch.op("vector", lambda e: e.engine_nop(), sv)

            evac_mark = {}
            n = 0
            prev_split = {0: None, 1: None}
            prev_last_tr = None
            for wi, wap in enumerate((wq, wk, wv, wo_t)):
                him, lom, wload = {}, {}, {}
                for t in range(2):
                    if prev_split[t] is not None:
                        sch.wait("sync", se, prev_split[t][0])
                        sch.wait("sync", sv, prev_split[t][1])
                    wload[t] = sch.dma(
                        "sync", lambda e, t=t, wap=wap: e.dma_start(
                            out=wrow[:, t:t + 1, :],
                            in_=wap[128 * t:128 * (t + 1), :].rearrange("p d -> p () d")),
                        (swa, swb)[t])
                for t in range(2):
                    dsth = woT_h[:, t, :] if wi == 3 else wsp_h[:, t, :]
                    dstl = woT_l[:, t, :] if wi == 3 else wsp_l[:, t, :]
                    sch.wait("scalar", (swa, swb)[t], wload[t])
                    if wi > 0 and wi < 4 and prev_last_tr is not None and wi != 3:
                        sch.wait("scalar", sg, prev_last_tr)
                    him[t] = sch.op("scalar", lambda e, d=dsth, t=t: e.copy(d, wrow[:, t, :]), se)
                    sch.wait("vector", se, him[t])
                    lom[t] = sch.op("vector", lambda e, dl=dstl, dh=dsth, t=t:
                                    e.scalar_tensor_tensor(dl, wrow[:, t, :], 1.0, dh,
                                                           ALU.mult, ALU.subtract), sv)
                    prev_split[t] = (him[t], lom[t])
                if wi == 3:
                    break
                for hl in range(2):
                    srcT = (wsp_h, wsp_l)[hl]
                    dst = (wqkvT_h, wqkvT_l)[hl]
                    for t in range(2):
                        for f in range(16):
                            if n == 0:
                                sch.wait("tensor", sd, m_ident)
                            sch.wait("tensor", (se, sv)[hl], (him, lom)[hl][t])
                            if n >= 2:
                                sch.wait("tensor", sv, evac_mark[n - 2])
                            mtr = sch.op("tensor", lambda e, srcT=srcT, t=t, f=f, n=n:
                                         e.transpose(pT[n % 2][:, 0:128],
                                                     srcT[:, t, 128 * f:128 * (f + 1)],
                                                     ident_sb[:, :]), sg)
                            sch.wait("vector", sg, mtr)
                            evac_mark[n] = sch.op(
                                "vector", lambda e, dst=dst, wi=wi, f=f, t=t, n=n:
                                e.tensor_copy(dst[:, wi, f, 128 * t:128 * (t + 1)],
                                              pT[n % 2][:, 0:128]), sv)
                            n += 1
                    prev_last_tr = sg.v
            m_awT = evac_mark[n - 1]
            marks_b0["awT"] = m_awT

            marks_b0["const"] = m_const
            marks_b0["idx"] = m_idx
            sch.flush(blk)

        with (
            nc.sbuf_tensor("b3_sb", [128, 3, 2], F32) as b3_sb,
            nc.sbuf_tensor("xf", [128, 2, 512], F32) as xf,
            nc.sbuf_tensor("xh", [128, 2, 512], BF16) as xh,
            nc.sbuf_tensor("xl", [128, 2, 512], BF16) as xl,
            nc.sbuf_tensor("qkst", [128, 6, 512], F32) as qkst,
            nc.sbuf_tensor("qkh", [128, 6, 512], BF16) as qkh,
            nc.sbuf_tensor("qkl", [128, 6, 512], BF16) as qkl,
            nc.sbuf_tensor("vev", [128, 4, 128], BF16) as vev,
            nc.Block("b1") as blk,
        ):
            sch = Sched(nc)
            sch.wait("tensor", sv, marks_b0["awT"])
            m_b3 = sch.dma("sync", lambda e: e.dma_start(out=b3_sb[:, :, :], in_=b3), sems["sb3"])
            xh_m, xl_m, mm_fc = {}, {}, {}
            prev = {}
            ntp = 0
            tevac2, tdma2 = {}, {}
            for tc in range(16):
                for fc in range(16):
                    i = tc * 16 + fc
                    if i >= 2:
                        ptc, pfc = (i - 2) // 16, (i - 2) % 16
                        sch.wait("sync", sv, xl_m[(ptc, pfc)])
                    mload = sch.dma("sync", lambda e, tc=tc, fc=fc: e.dma_start(
                        out=xf[:, fc % 2, :],
                        in_=xt[128 * fc:128 * (fc + 1), 512 * tc:512 * (tc + 1)]),
                        (sems["sxa"], sems["sxb"])[fc % 2])
                    sch.wait("scalar", (sems["sxa"], sems["sxb"])[fc % 2], mload)
                    if i >= 2:
                        sch.wait("scalar", sg, mm_fc[i - 2])
                    xh_m[(tc, fc)] = sch.op("scalar", lambda e, fc=fc: e.copy(
                        xh[:, fc % 2, :], xf[:, fc % 2, :]), se)
                    sch.wait("vector", se, xh_m[(tc, fc)])
                    if i >= 2:
                        sch.wait("vector", sg, mm_fc[i - 2])
                    xl_m[(tc, fc)] = sch.op("vector", lambda e, fc=fc: e.scalar_tensor_tensor(
                        xl[:, fc % 2, :], xf[:, fc % 2, :], 1.0, xh[:, fc % 2, :],
                        ALU.mult, ALU.subtract), sv)
                    sch.wait("tensor", se, xh_m[(tc, fc)])
                    sch.wait("tensor", sv, xl_m[(tc, fc)])
                    for oi in range(6):
                        wi, hd = oi // 2, oi % 2
                        if fc == 0 and tc > 0:
                            sch.wait("tensor", se, prev[oi]["stage"])
                        for pi in range(3):
                            lw = (wqkvT_h, wqkvT_h, wqkvT_l)[pi]
                            rx = (xh, xl, xh)[pi]
                            strt = (fc == 0 and pi == 0)
                            stop = (fc == 15 and pi == 2)
                            last_of_fc = (oi == 5 and pi == 2)
                            sch.op("tensor", lambda e, lw=lw, rx=rx, wi=wi, hd=hd, fc=fc, oi=oi,
                                   strt=strt, stop=stop:
                                   e.matmul(ps[oi][:, :], lw[:, wi, fc, 128 * hd:128 * (hd + 1)],
                                            rx[:, fc % 2, :], start=strt, stop=stop),
                                   sg if (stop or last_of_fc) else None)
                            if last_of_fc:
                                mm_fc[i] = sg.v
                cur = {}
                for oi in range(6):
                    wi, hd = oi // 2, oi % 2
                    mm_stop = sg.v - (5 - oi)
                    cur[oi] = {}
                    sch.wait("scalar", sg, mm_stop)
                    if tc == 0:
                        sch.wait("scalar", sems["sb3"], m_b3)
                    if tc > 0:
                        p = prev[oi]
                        sch.wait("scalar", sv, p["lo"])
                        if wi < 2:
                            sk = (sems["ska"], sems["skb"], sems["skc"], sems["skd"])[oi]
                            sch.wait("scalar", sk, sk.v)
                        else:
                            sch.wait("scalar", sg, p["lasttr"])
                    mstg = sch.op("scalar", lambda e, oi=oi, wi=wi, hd=hd: e.activation(
                        qkst[:, oi, :], ps[oi][:, :], AF.Identity,
                        bias=b3_sb[:, wi, hd:hd + 1], scale=1.0), se)
                    cur[oi]["stage"] = mstg
                    sch.wait("scalar", se, mstg)
                    mhi = sch.op("scalar", lambda e, oi=oi: e.copy(
                        qkh[:, oi, :], qkst[:, oi, :]), se)
                    cur[oi]["hi"] = mhi
                    sch.wait("vector", se, mhi)
                    mlo = sch.op("vector", lambda e, oi=oi: e.scalar_tensor_tensor(
                        qkl[:, oi, :], qkst[:, oi, :], 1.0, qkh[:, oi, :],
                        ALU.mult, ALU.subtract), sv)
                    cur[oi]["lo"] = mlo
                    if wi < 2:
                        scr = (q_scr, k_scr)[wi]
                        sk = (sems["ska"], sems["skb"], sems["skc"], sems["skd"])[oi]
                        sch.wait("gpsimd", se, mhi)
                        cur[oi]["hidma"] = sch.dma(
                            "gpsimd", lambda e, scr=scr, hd=hd, tc=tc, oi=oi: e.dma_start(
                                out=scr[hd, 0, :, 512 * tc:512 * (tc + 1)], in_=qkh[:, oi, :]), sk)
                        sch.wait("gpsimd", sv, mlo)
                        cur[oi]["lodma"] = sch.dma(
                            "gpsimd", lambda e, scr=scr, hd=hd, tc=tc, oi=oi: e.dma_start(
                                out=scr[hd, 1, :, 512 * tc:512 * (tc + 1)], in_=qkl[:, oi, :]), sk)
                    else:
                        for hl in range(2):
                            mm = (mhi, mlo)[hl]
                            srcv = (qkh, qkl)[hl]
                            for j in range(4):
                                sch.wait("tensor", (se, sv)[hl], mm)
                                if tc == 0 and ntp < 2:
                                    pass
                                elif ntp >= 2:
                                    sch.wait("tensor", sv, tevac2[ntp - 2])
                                mtr = sch.op("tensor", lambda e, srcv=srcv, oi=oi, j=j, ntp=ntp:
                                             e.transpose(pT[ntp % 2][:, 0:128],
                                                         srcv[:, oi, 128 * j:128 * (j + 1)],
                                                         ident_sb[:, :]), sg)
                                cur[oi]["lasttr"] = mtr
                                sv4 = (sems["sva"], sems["svb"], sems["svc"], sems["svd"])[ntp % 4]
                                sch.wait("vector", sg, mtr)
                                if ntp >= 4:
                                    sch.wait("vector", sv4, sv4.v)
                                tevac2[ntp] = sch.op("vector", lambda e, ntp=ntp: e.tensor_copy(
                                    vev[:, ntp % 4, :], pT[ntp % 2][:, 0:128]), sv)
                                sch.wait("gpsimd", sv, tevac2[ntp])
                                tdma2[ntp] = sch.dma(
                                    "gpsimd", lambda e, hd=hd, hl=hl, tc=tc, j=j, ntp=ntp: e.dma_start(
                                        out=v_scr[hd, hl, 512 * tc + 128 * j:512 * tc + 128 * (j + 1), :],
                                        in_=vev[:, ntp % 4, :]), sv4)
                                ntp += 1
                prev = cur

            tile_rows = [128] * 5 + [64]
            tile_off = [0, 128, 256, 384, 512, 640]
            q8m, rowm, b16m, gatherm, tevac, tdma = {}, {}, {}, {}, {}, {}
            ntp = 0
            for it in range(12):
                gi, ti = it // 6, it % 6
                nr = tile_rows[ti]
                wfull = (wg_in, wu_in)[gi]
                if it == 0:
                    sch.wait("gpsimd", sd, marks_b0["idx"])
                if it >= 2:
                    sch.wait("gpsimd", sv, b16m[it - 2])
                gatherm[it] = sch.dma(
                    "gpsimd", lambda e, nr=nr, it=it, ti=ti, wfull=wfull: e.indirect_dma_start(
                        out=gw[0:nr, it % 2, :], out_offset=None,
                        in_=wfull[:, :],
                        in_offset=IndirectOffsetOnAxis(ap=idx_sb[0:nr, ti:ti + 1], axis=0)),
                    (sga, sgb)[it % 2])
                sch.wait("vector", (sga, sgb)[it % 2], gatherm[it])
                if it >= 2:
                    sch.wait("vector", se, q8m[it - 2])
                rowm[it] = sch.op("vector", lambda e, nr=nr, it=it: e.reduce_max(
                    gm[0:nr, it % 2:it % 2 + 1], gw[0:nr, it % 2, :], axis=AXX,
                    apply_absolute_value=True), sv)
                sch.wait("vector", sv, rowm[it])
                sch.op("vector", lambda e, nr=nr, gi=gi, ti=ti, it=it: e.tensor_copy(
                    wsc_sb[0:nr, gi, ti:ti + 1], gm[0:nr, it % 2:it % 2 + 1]), sv)
                sch.wait("vector", sv, rowm[it])
                m1 = sch.op("vector", lambda e, nr=nr, it=it: e.tensor_scalar_mul(
                    gm[0:nr, 4 + it % 2:5 + it % 2], gm[0:nr, it % 2:it % 2 + 1], IQD), sv)
                sch.wait("vector", sv, m1)
                mrs = sch.op("vector", lambda e, nr=nr, it=it: e.reciprocal(
                    gm[0:nr, 2 + it % 2:3 + it % 2], gm[0:nr, 4 + it % 2:5 + it % 2]), sv)
                if it == 0:
                    sch.wait("scalar", sv, marks_b0["const"])
                sch.wait("scalar", sv, mrs)
                q8m[it] = sch.op("scalar", lambda e, nr=nr, it=it: e.activation(
                    g8[0:nr, it % 2, :], gw[0:nr, it % 2, :], AF.Copy, bias=0.0,
                    scale=gm[0:nr, 2 + it % 2:3 + it % 2]), se)
                sch.wait("vector", se, q8m[it])
                b16m[it] = sch.op("vector", lambda e, nr=nr, it=it: e.tensor_copy(
                    g8b[0:nr, it % 2, :], g8[0:nr, it % 2, :]), sv)
                for f in range(16):
                    sch.wait("tensor", sv, b16m[it])
                    if ntp < 2:
                        sch.wait("tensor", sv, sv.v)
                    else:
                        sch.wait("tensor", se, tevac[ntp - 2])
                    mtr = sch.op("tensor", lambda e, nr=nr, it=it, f=f, ntp=ntp:
                                 e.transpose(pT[ntp % 2][:, 0:nr],
                                             g8b[0:nr, it % 2, 128 * f:128 * (f + 1)],
                                             ident_sb[0:nr, 0:nr]), sg)
                    se4 = (sems["sea"], sems["seb"], sems["sec"], sems["sed"])[ntp % 4]
                    sch.wait("scalar", sg, mtr)
                    if ntp >= 4:
                        sch.wait("scalar", se4, se4.v)
                    tevac[ntp] = sch.op("scalar", lambda e, nr=nr, ntp=ntp: e.copy(
                        gev[:, ntp % 4, 0:nr], pT[ntp % 2][:, 0:nr]), se)
                    sch.wait("gpsimd", se, tevac[ntp])
                    tdma[ntp] = sch.dma("gpsimd", lambda e, gi=gi, f=f, ti=ti, nr=nr, ntp=ntp:
                                        e.dma_start(
                                            out=w8gu_in[gi, f, :, tile_off[ti]:tile_off[ti] + nr],
                                            in_=gev[:, ntp % 4, 0:nr]), se4)
                    ntp += 1

            prev_tr_end = None
            prev_b16 = None
            for t in range(2):
                if prev_b16 is not None:
                    sch.wait("sync", sv, prev_b16)
                    sch.wait("sync", se, se.v)
                mwl = sch.dma("sync", lambda e, t=t: e.dma_start(
                    out=wd_sb[:, 0:1, :],
                    in_=wd_rows[128 * t:128 * (t + 1), :].rearrange("p d -> p () d")), swd)
                sch.wait("vector", swd, mwl)
                mr = sch.op("vector", lambda e, t=t: e.reduce_max(
                    gm[:, 4:5], wd_sb[:, 0, :], axis=AXX, apply_absolute_value=True), sv)
                sch.wait("vector", sv, mr)
                sch.op("vector", lambda e, t=t: e.tensor_copy(
                    wsc_sb[:, 2, t:t + 1], gm[:, 4:5]), sv)
                sch.wait("vector", sv, mr)
                m1 = sch.op("vector", lambda e: e.tensor_scalar_mul(gm[:, 3:4], gm[:, 4:5], IQD), sv)
                sch.wait("vector", sv, m1)
                mrs = sch.op("vector", lambda e: e.reciprocal(gm[:, 5:6], gm[:, 3:4]), sv)
                sch.wait("scalar", sv, mrs)
                if prev_tr_end is not None:
                    sch.wait("scalar", sg, prev_tr_end)
                m8 = sch.op("scalar", lambda e, t=t: e.activation(
                    wd8[:, 0, :], wd_sb[:, 0, :], AF.Copy, bias=0.0, scale=gm[:, 5:6]), se)
                sch.wait("vector", se, m8)
                mb16 = sch.op("vector", lambda e, t=t: e.tensor_copy(wd8b[:, 0, :], wd8[:, 0, :]), sv)
                prev_b16 = mb16
                for ic in range(44):
                    sch.wait("tensor", sv, mb16)
                    if ntp >= 2:
                        sch.wait("tensor", se, tevac[ntp - 2])
                    mtr = sch.op("tensor", lambda e, ic=ic, ntp=ntp:
                                 e.transpose(pT[ntp % 2][:, 0:128],
                                             wd8b[:, 0, 128 * ic:128 * (ic + 1)],
                                             ident_sb[:, :]), sg)
                    se4 = (sems["sea"], sems["seb"], sems["sec"], sems["sed"])[ntp % 4]
                    sch.wait("scalar", sg, mtr)
                    if ntp >= 4:
                        sch.wait("scalar", se4, se4.v)
                    tevac[ntp] = sch.op("scalar", lambda e, ntp=ntp: e.copy(
                        gev[:, ntp % 4, 0:128], pT[ntp % 2][:, 0:128]), se)
                    sch.wait("gpsimd", se, tevac[ntp])
                    tdma[ntp] = sch.dma("gpsimd", lambda e, ic=ic, t=t, ntp=ntp: e.dma_start(
                        out=w8d_in[ic, :, 128 * t:128 * (t + 1)], in_=gev[:, ntp % 4, 0:128]), se4)
                    ntp += 1
                prev_tr_end = sg.v

            sch.wait("gpsimd", sv, sv.v)
            for g in range(2):
                sch.dma_nc("gpsimd", wsc_in[g, 0:640].rearrange("(t p) -> p t", p=128),
                           wsc_sb[:, g, 0:5], sq)
                sch.dma_nc("gpsimd", wsc_in[g, 640:704].rearrange("(p o) -> p o", o=1),
                           wsc_sb[0:64, g, 5:6], sq)
            sch.dma_nc("gpsimd", wsc_in[2, 0:256].rearrange("(t p) -> p t", p=128),
                       wsc_sb[:, 2, 0:2], sq)
            mz = sch.op("vector", lambda e: e.memset(zpad[:, :], 0.0), sv)
            sch.wait("gpsimd", sv, mz)
            sch.dma("gpsimd", lambda e: e.dma_start(
                out=wsc_in[2, 256:704].rearrange("(o t) -> o t", o=1), in_=zpad[:, :]), sq)
            sch.wait("gpsimd", sq, sq.v)
            for _sn in ("sea", "seb", "sec", "sed"):
                sch.wait("gpsimd", sems[_sn], sems[_sn].v)
            rg = [list(range(N_CORES))]
            sch.op("gpsimd", lambda e: e.collective_compute(
                "AllGather", ALU.bypass, replica_groups=rg,
                ins=[w8gu_in.opt()], outs=[w8gu_full.opt()]), cc)
            sch.op("gpsimd", lambda e: e.collective_compute(
                "AllGather", ALU.bypass, replica_groups=rg,
                ins=[w8d_in.opt()], outs=[w8d_full.opt()]), cc)
            sch.op("gpsimd", lambda e: e.collective_compute(
                "AllGather", ALU.bypass, replica_groups=rg,
                ins=[wsc_in.opt()], outs=[wsc_full.opt()]), cc)

            if dbg_phase == 1:
                sch.wait("gpsimd", sq, sq.v)
                sch.wait("gpsimd", cc, 3)
                for a, b in (("d_qsc", q_scr), ("d_ksc", k_scr), ("d_vsc", v_scr),
                             ("d_w8g", w8gu_full), ("d_w8d", w8d_full), ("d_wsc", wsc_full)):
                    sch.dma("gpsimd", lambda e, a=a, b=b: e.dma_start(out=dbg[a], in_=b), sq)
                sch.wait("gpsimd", sq, sq.v)
            sch.flush(blk)

        stA.close()

        if dbg_phase == 1:
            return nc

        sqa, sqb, skv, sms = sems["sqa"], sems["sqb"], sems["skv"], sems["sms"]
        marks_b0["b1_se"] = se.v

        stB = contextlib.ExitStack()
        st.enter_context(stB)
        attn_hi = stB.enter_context(nc.sbuf_tensor("attn_hi", [128, 64, 256], BF16))
        attn_lo = stB.enter_context(nc.sbuf_tensor("attn_lo", [128, 64, 256], BF16))
        with (
            nc.sbuf_tensor("masks_sb", [128, 4, 512], F32) as masks_sb,
            nc.sbuf_tensor("kh_sb", [128, S], BF16) as kh_sb,
            nc.sbuf_tensor("kl_sb", [128, S], BF16) as kl_sb,
            nc.sbuf_tensor("vh_sb", [128, 64, 130], BF16) as vh_sb,
            nc.sbuf_tensor("vl_sb", [128, 64, 130], BF16) as vl_sb,
            nc.sbuf_tensor("qh_sb", [128, 2, 512], BF16) as qh_sb,
            nc.sbuf_tensor("ql_sb", [128, 2, 512], BF16) as ql_sb,
            nc.sbuf_tensor("p32", [128, 2, 512], F32) as p32,
            nc.sbuf_tensor("ph_sb", [128, 2, 512], BF16) as ph_sb,
            nc.sbuf_tensor("pl_sb", [128, 2, 512], BF16) as pl_sb,
            nc.sbuf_tensor("rcp", [128, 4], F32) as rcp,
            nc.Block("b2") as blk,
        ):
            sch = Sched(nc)
            sch.wait("sync", sq, sq.v)
            sch.wait("gpsimd", sq, sq.v)
            for _sn in ("ska", "skb", "skc", "skd", "sva", "svb", "svc", "svd"):
                sch.wait("sync", sems[_sn], sems[_sn].v)
                sch.wait("gpsimd", sems[_sn], sems[_sn].v)
            m_msk = sch.dma("sync", lambda e: e.dma_start(
                out=masks_sb[:, :, :], in_=masks.rearrange("m p q -> p m q")), sms)
            m_ones = sch.op("vector", lambda e: e.memset(vh_sb[:, :, 128:129], 1.0), sv)
            m_zcol = sch.op("vector", lambda e: e.memset(vl_sb[:, :, 128:129], 0.0), sv)

            expm, splitm, av_of_kc = {}, {}, {}
            evacm = {}
            prevmhi = {}
            smark_g = {}
            prev_head_last = None
            prev_av_end = None
            for h in range(2):
                if prev_head_last is not None:
                    sch.wait("sync", sg, prev_head_last)
                    sch.wait("gpsimd", sg, prev_head_last)
                for hl, dstk in ((0, kh_sb), (1, kl_sb)):
                    sch.dma("sync", lambda e, h=h, hl=hl, dstk=dstk: e.dma_start(
                        out=dstk[:, :], in_=k_scr[h, hl, :, :]), skv)
                for hl, dstv in ((0, vh_sb), (1, vl_sb)):
                    for ch in range(64):
                        sch.dma("gpsimd", lambda e, h=h, hl=hl, dstv=dstv, ch=ch: e.dma_start(
                            out=dstv[:, ch, 0:128],
                            in_=v_scr[h, hl, 128 * ch:128 * (ch + 1), :]), skv)
                m_kv = skv.v
                qldm = {}
                for qg in range(16):
                    sqp = (sqa, sqb)[qg % 2]
                    if qg >= 2:
                        sch.wait("sync", sg, smark_g[(h, qg - 2)])
                    elif h == 1:
                        sch.wait("sync", sg, smark_g[(0, 14 + qg)])
                    for hl, dstq in ((0, qh_sb), (1, ql_sb)):
                        qldm[(qg, hl)] = sch.dma(
                            "sync", lambda e, h=h, hl=hl, qg=qg, dstq=dstq: e.dma_start(
                                out=dstq[:, qg % 2, :],
                                in_=q_scr[h, hl, :, 512 * qg:512 * (qg + 1)]), sqp)
                    L = 4 * qg + 4
                    smark = {}

                    def emit_scores(kc, qg=qg, h=h, L=L, qldm=qldm, smark=smark):
                        if kc >= 2:
                            sch.wait("tensor", se, expm[(h, qg, kc - 2)])
                        else:
                            sch.wait("tensor", se, se.v)
                        if kc == 0:
                            sch.wait("tensor", (sqa, sqb)[qg % 2], qldm[(qg, 1)])
                            sch.wait("tensor", skv, m_kv)
                        for pi in range(3):
                            lk = (kh_sb, kh_sb, kl_sb)[pi]
                            rq = (qh_sb, ql_sb, qh_sb)[pi]
                            sch.op("tensor", lambda e, lk=lk, rq=rq, kc=kc, qg=qg, pi=pi:
                                   e.matmul(ps[kc % 2][:, :],
                                            lk[:, 128 * kc:128 * (kc + 1)],
                                            rq[:, qg % 2, :],
                                            start=(pi == 0), stop=(pi == 2)),
                                   sg if pi == 2 else None)
                        mk = ("g", sg.v)
                        if kc >= L - 4:
                            p = kc - (L - 4)
                            sch.wait("vector", sg, sg.v)
                            if qg == 0 and kc == 0 and h == 0:
                                sch.wait("vector", sms, m_msk)
                            mm = sch.op("vector", lambda e, p=p, kc=kc: e.tensor_tensor(
                                ps[kc % 2][:, :], ps[kc % 2][:, :], masks_sb[:, p, :],
                                ALU.add), sv)
                            mk = ("v", mm)
                        smark[kc] = mk

                    def emit_pstage(kc, qg=qg, h=h, smark=smark):
                        kind, mk = smark[kc]
                        sch.wait("scalar", sv if kind == "v" else sg, mk)
                        if kc >= 2:
                            sch.wait("scalar", sp, splitm[(h, qg, kc - 2)])
                        else:
                            sch.wait("scalar", sp, sp.v)
                        expm[(h, qg, kc)] = sch.op("scalar", lambda e, kc=kc: e.activation(
                            p32[:, kc % 2, :], ps[kc % 2][:, :], AF.Exp,
                            bias=EXP_BIAS, scale=SCALE), se)
                        sch.wait("vector", se, expm[(h, qg, kc)])
                        if kc >= 2:
                            sch.wait("vector", sg, av_of_kc[(h, qg, kc - 2)])
                        elif prev_av_end is not None:
                            sch.wait("vector", sg, prev_av_end)
                        mph = sch.op("vector", lambda e, kc=kc: e.tensor_copy(
                            ph_sb[:, kc % 2, :], p32[:, kc % 2, :]), sv)
                        sch.wait("vector", sv, mph)
                        splitm[(h, qg, kc)] = sch.op(
                            "vector", lambda e, kc=kc: e.scalar_tensor_tensor(
                                pl_sb[:, kc % 2, :], p32[:, kc % 2, :], 1.0,
                                ph_sb[:, kc % 2, :], ALU.mult, ALU.subtract), sp)

                    def emit_av(kc, qg=qg, h=h, L=L):
                        sch.wait("tensor", sp, splitm[(h, qg, kc)])
                        if kc == 0:
                            if evacm:
                                sch.wait("tensor", sv, max(evacm.values()))
                            else:
                                sch.wait("tensor", sv, m_zcol)
                        for qt in range(4):
                            for pi in range(3):
                                lp = (ph_sb, ph_sb, pl_sb)[pi]
                                rv = (vh_sb, vl_sb, vh_sb)[pi]
                                sch.op("tensor",
                                       lambda e, lp=lp, rv=rv, kc=kc, qt=qt, pi=pi, L=L:
                                       e.matmul(ps[2 + qt][:, 0:129],
                                                lp[:, kc % 2, 128 * qt:128 * (qt + 1)],
                                                rv[:, kc, 0:129],
                                                start=(kc == 0 and pi == 0),
                                                stop=(kc == L - 1 and pi == 2)),
                                       sg if (qt == 3 and pi == 2) else None)
                        av_of_kc[(h, qg, kc)] = sg.v

                    for kc in range(L + 2):
                        if kc < L:
                            emit_scores(kc)
                        if 1 <= kc <= L:
                            emit_pstage(kc - 1)
                        if kc >= 2:
                            emit_av(kc - 2)
                    m_av_end = av_of_kc[(h, qg, L - 1)]
                    prev_av_end = m_av_end
                    for qt in range(4):
                        ch = 4 * qg + qt
                        sch.wait("vector", sg, m_av_end)
                        if qt in prevmhi:
                            sch.wait("vector", se, prevmhi[qt])
                        mrc = sch.op("vector", lambda e, qt=qt: e.reciprocal(
                            rcp[:, qt:qt + 1], ps[2 + qt][:, 128:129]), sv)
                        sch.wait("scalar", sv, mrc)
                        mhi = sch.op("scalar", lambda e, qt=qt, ch=ch, h=h: e.activation(
                            attn_hi[:, ch, 128 * h:128 * (h + 1)], ps[2 + qt][:, 0:128],
                            AF.Copy, bias=0.0, scale=rcp[:, qt:qt + 1]), se)
                        prevmhi[qt] = mhi
                        sch.wait("vector", se, mhi)
                        evacm[qt] = sch.op(
                            "vector", lambda e, qt=qt, ch=ch, h=h: e.scalar_tensor_tensor(
                                attn_lo[:, ch, 128 * h:128 * (h + 1)], ps[2 + qt][:, 0:128],
                                rcp[:, qt:qt + 1], attn_hi[:, ch, 128 * h:128 * (h + 1)],
                                ALU.mult, ALU.subtract), sv)
                    smark_g[(h, qg)] = sg.v
                prev_head_last = sg.v
            marks_b0["attn_done"] = max(evacm.values())
            sch.flush(blk)

        with (
            nc.sbuf_tensor("aTh", [128, 64, 2, 128], BF16) as aTh,
            nc.sbuf_tensor("aTl", [128, 64, 2, 128], BF16) as aTl,
            nc.sbuf_tensor("ost", [128, 4, 512], F32) as ost,
            nc.Block("b3") as blk,
        ):
            sch = Sched(nc)
            sch.wait("tensor", sv, marks_b0["attn_done"])
            sch.wait("tensor", se, se.v)
            trm = {}
            tmk = {}
            for nn in range(257):
                if nn < 256:
                    ch, r = nn // 4, nn % 4
                    hl, jj = r // 2, r % 2
                    srca = (attn_hi, attn_lo)[hl]
                    if nn >= 2:
                        sch.wait("tensor", se, trm[nn - 2])
                    tmk[nn] = sch.op("tensor", lambda e, srca=srca, ch=ch, jj=jj, nn=nn:
                                     e.transpose(pT[nn % 2][:, 0:128],
                                                 srca[:, ch, 128 * jj:128 * (jj + 1)],
                                                 ident_sb[:, :]), sg)
                else:
                    sch.wait("tensor", se, trm[nn - 2])
                    tmk[nn] = sch.op("tensor", lambda e, nn=nn: e.transpose(
                        pT[nn % 2][:, 0:128], ident_sb[:, :], ident_sb[:, :]), sg)
                if nn >= 1:
                    k = nn - 1
                    ch, r = k // 4, k % 4
                    hl, jj = r // 2, r % 2
                    dsta = (aTh, aTl)[hl]
                    sch.wait("scalar", sg, tmk[nn])
                    trm[k] = sch.op("scalar", lambda e, dsta=dsta, ch=ch, jj=jj, k=k:
                                    e.copy(dsta[:, ch, jj, :], pT[k % 2][:, 0:128]), se)
            m_aT = trm[255]
            stm = {}
            nst = 0
            so4 = (sems["soa"], sems["sob"], sems["soc"], sems["sod"])
            for half in range(2):
                opart = (o_part0, o_part1)[half]
                for ch in range(64):
                    for oci in range(2):
                        oc = 2 * half + oci
                        if ch == 0 and half == 0 and oci == 0:
                            sch.wait("tensor", se, m_aT)
                        if ch >= 1:
                            sch.wait("tensor", se, stm[(ch - 1, oc)])
                        elif half == 1:
                            sch.wait("tensor", se, stm[(63, oc - 2)])
                        for jj in range(2):
                            for pi in range(3):
                                la = (aTh, aTh, aTl)[pi]
                                rw = (woT_h, woT_l, woT_h)[pi]
                                sch.op("tensor",
                                       lambda e, la=la, rw=rw, ch=ch, jj=jj, oc=oc, pi=pi:
                                       e.matmul(ps[oc][:, :], la[:, ch, jj, :],
                                                rw[:, jj, 512 * oc:512 * (oc + 1)],
                                                start=(jj == 0 and pi == 0),
                                                stop=(jj == 1 and pi == 2)),
                                       sg if (jj == 1 and pi == 2) else None)
                        mmk = sg.v
                        sch.wait("scalar", sg, mmk)
                        sl = nst % 4
                        if nst >= 4:
                            sch.wait("scalar", so4[sl], so4[sl].v)
                        stm[(ch, oc)] = sch.op("scalar", lambda e, sl=sl, oc=oc: e.copy(
                            ost[:, sl, :], ps[oc][:, :]), se)
                        sch.wait("gpsimd", se, stm[(ch, oc)])
                        sch.dma("gpsimd", lambda e, ch=ch, oci=oci, opart=opart, sl=sl:
                                e.dma_start(
                                    out=opart[128 * ch:128 * (ch + 1), 512 * oci:512 * (oci + 1)],
                                    in_=ost[:, sl, :]), so4[sl])
                        nst += 1
                for s4 in so4:
                    sch.wait("gpsimd", s4, s4.v)
                ored = (o_red0, o_red1)[half]
                sch.op("gpsimd", lambda e, opart=opart, ored=ored: e.collective_compute(
                    "ReduceScatter", ALU.add, replica_groups=[list(range(N_CORES))],
                    ins=[opart.opt()], outs=[ored.opt()]), cc)
            sch.flush(blk)

        stB.close()

        with (
            nc.sbuf_tensor("xo5", [128, 2, D], F32) as xo5,
            nc.sbuf_tensor("oo5", [128, 2, D], F32) as oo5,
            nc.sbuf_tensor("bob", [128, D], F32) as bob,
            nc.sbuf_tensor("z5", [128, 2, D], F32) as z5,
            nc.sbuf_tensor("sqd", [128, D], F32) as sqd,
            nc.sbuf_tensor("st5", [128, 2, 8], F32) as st5,
            nc.sbuf_tensor("x15", [128, 2, D], F32) as x15,
            nc.sbuf_tensor("qx8", [128, 2, D], F8) as qx8,
            nc.sbuf_tensor("qxb", [128, 2, D], BF16) as qxb,
            nc.Block("b5") as blk,
        ):
            sch = Sched(nc)
            m_bob = sch.dma("sync", lambda e: e.dma_start(out=bob[:, :], in_=bo_b), sms)
            x1m, qxbm, qxtr = {}, {}, {}
            ntp = 0
            tevac5 = {}
            for tt in range(8):
                pp = tt % 2
                sxp = (sems["sxa"], sems["sxb"])[pp]
                if tt >= 2:
                    sch.wait("sync", se, x1m[tt - 2])
                    sch.wait("sync", sv, qxbm[tt - 2])
                ldm = sch.dma("sync", lambda e, tt=tt, pp=pp: e.dma_start(
                    out=xo5[:, pp, :], in_=x_own[128 * tt:128 * (tt + 1), :]), sxp)
                if tt == 0:
                    sch.wait("sync", cc, 5)
                sch.dma("sync", lambda e, tt=tt, pp=pp: e.dma_start(
                    out=oo5[:, pp, 0:1024], in_=o_red0[128 * tt:128 * (tt + 1), :]), sxp)
                ldo = sch.dma("sync", lambda e, tt=tt, pp=pp: e.dma_start(
                    out=oo5[:, pp, 1024:2048], in_=o_red1[128 * tt:128 * (tt + 1), :]), sxp)
                sch.wait("vector", sxp, ldo)
                if tt == 0:
                    sch.wait("vector", sms, m_bob)
                if tt >= 2:
                    sch.wait("vector", se, x1m[tt - 2])
                m1 = sch.op("vector", lambda e, pp=pp: e.tensor_add(
                    z5[:, pp, :], oo5[:, pp, :], bob[:, :]), sv)
                sch.wait("vector", sv, m1)
                m2 = sch.op("vector", lambda e, pp=pp: e.scalar_tensor_tensor(
                    z5[:, pp, :], z5[:, pp, :], 1.0, xo5[:, pp, :], ALU.mult, ALU.add,
                    accum_out=st5[:, pp, 0:1]), sv)
                sch.wait("vector", sv, m2)
                m3 = sch.op("vector", lambda e, pp=pp: e.tensor_scalar_mul(
                    st5[:, pp, 1:2], st5[:, pp, 0:1], -IND), sv)
                sch.wait("scalar", sv, m3)
                m4 = sch.op("scalar", lambda e, pp=pp: e.activation(
                    sqd[:, :], z5[:, pp, :], AF.Square, bias=st5[:, pp, 1:2], scale=1.0,
                    accum_out=st5[:, pp, 2:3]), se)
                sch.wait("scalar", se, m4)
                m5 = sch.op("scalar", lambda e, pp=pp: e.activation(
                    st5[:, pp, 3:4], st5[:, pp, 2:3], AF.Sqrt, bias=EPS, scale=IND), se)
                sch.wait("vector", se, m5)
                m6 = sch.op("vector", lambda e, pp=pp: e.reciprocal(
                    st5[:, pp, 4:5], st5[:, pp, 3:4]), sv)
                sch.wait("vector", sv, m6)
                m7 = sch.op("vector", lambda e, pp=pp: e.tensor_mul(
                    st5[:, pp, 5:6], st5[:, pp, 1:2], st5[:, pp, 4:5]), sv)
                sch.wait("scalar", sv, m7)
                x1m[tt] = sch.op("scalar", lambda e, pp=pp: e.activation(
                    x15[:, pp, :], z5[:, pp, :], AF.Identity,
                    bias=st5[:, pp, 5:6], scale=st5[:, pp, 4:5]), se)
                sch.wait("gpsimd", se, x1m[tt])
                sch.dma("gpsimd", lambda e, tt=tt, pp=pp: e.dma_start(
                    out=x1_scr[128 * tt:128 * (tt + 1), :], in_=x15[:, pp, :]), sq)
                sch.wait("vector", se, x1m[tt])
                m8 = sch.op("vector", lambda e, tt=tt, pp=pp: e.reduce_max(
                    st5[:, pp, 6:7], x15[:, pp, :], axis=AXX, apply_absolute_value=True), sv)
                sch.wait("vector", sv, m8)
                m9 = sch.op("vector", lambda e, tt=tt, pp=pp: e.tensor_scalar_mul(
                    mxd_sb[:, tt:tt + 1], st5[:, pp, 6:7], DQF), sv)
                m10 = sch.op("vector", lambda e, pp=pp: e.tensor_scalar_mul(
                    st5[:, pp, 7:8], st5[:, pp, 6:7], IQD), sv)
                sch.wait("vector", sv, m10)
                m11 = sch.op("vector", lambda e, pp=pp: e.reciprocal(
                    st5[:, pp, 6:7], st5[:, pp, 7:8]), sv)
                sch.wait("scalar", sv, m11)
                m12 = sch.op("scalar", lambda e, pp=pp: e.activation(
                    qx8[:, pp, :], x15[:, pp, :], AF.Copy, bias=0.0,
                    scale=st5[:, pp, 6:7]), se)
                sch.wait("vector", se, m12)
                if tt >= 2:
                    sch.wait("vector", sg, qxtr[tt - 2])
                qxbm[tt] = sch.op("vector", lambda e, pp=pp: e.tensor_copy(
                    qxb[:, pp, :], qx8[:, pp, :]), sv)
                for f in range(16):
                    sch.wait("tensor", sv, qxbm[tt])
                    if ntp >= 2:
                        sch.wait("tensor", se, tevac5[ntp - 2])
                    elif tt == 0 and f == 0:
                        sch.wait("tensor", se, se.v)
                    mtr = sch.op("tensor", lambda e, pp=pp, f=f, ntp=ntp:
                                 e.transpose(pT[ntp % 2][:, 0:128],
                                             qxb[:, pp, 128 * f:128 * (f + 1)],
                                             ident_sb[:, :]), sg)
                    sch.wait("scalar", sg, mtr)
                    tevac5[ntp] = sch.op("scalar", lambda e, f=f, tt=tt, ntp=ntp: e.copy(
                        qxT[:, f, 128 * tt:128 * (tt + 1)], pT[ntp % 2][:, 0:128]), se)
                    ntp += 1
                qxtr[tt] = sg.v
            marks_b0["qxT_done"] = tevac5[ntp - 1]
            sch.flush(blk)

        sw6a, sw6b, st6a, st6b = sems["sw6a"], sems["sw6b"], sems["st6a"], sems["st6b"]
        sto, sout = sems["sto"], sems["sout"]
        sx6a, sx6b = sems["sx6a"], sems["sx6b"]

        def gu_slices(ic):
            lo, hi2 = 512 * ic, 512 * (ic + 1)
            out = []
            r0, r1 = lo // ISH, (hi2 - 1) // ISH
            for r in range(r0, r1 + 1):
                a = max(lo, r * ISH)
                b = min(hi2, (r + 1) * ISH)
                out.append((r, a - r * ISH, a - lo, b - a))
            return out

        mtm = {}
        with (
            nc.sbuf_tensor("wgt", [128, 2, 16, 512], F8) as wgt,
            nc.sbuf_tensor("wut", [128, 2, 16, 512], F8) as wut,
            nc.sbuf_tensor("mwg", [128, 2, 512], F32) as mwg,
            nc.sbuf_tensor("mwu", [128, 2, 512], F32) as mwu,
            nc.sbuf_tensor("gst", [128, 2, 512], F32) as gst,
            nc.sbuf_tensor("ust", [128, 2, 512], F32) as ust,
            nc.sbuf_tensor("sil", [128, 2, 512], F32) as sil,
            nc.sbuf_tensor("tt6", [128, 2, 512], F32) as tt6,
            nc.sbuf_tensor("mt6", [128, 8, 2], F32) as mt6,
            nc.sbuf_tensor("ones_row", [1, 128], F32) as ones_row,
            nc.Block("b6a") as blk,
        ):
            sch = Sched(nc)
            m_or = sch.op("vector", lambda e: e.memset(ones_row[:, :], 1.0), sv)
            sch.wait("sync", cc, 3)
            tm, em6 = {}, {}
            gum = {}
            mwdone = {}
            for ic in range(11):
                swp = (sw6a, sw6b)[ic % 2]
                if ic >= 2:
                    sch.wait("sync", sg, gum[ic - 2])
                    sch.wait("sync", sv, tm[(ic - 2, 7)])
                for f in range(16):
                    for r, roff, coff, ln in gu_slices(ic):
                        sch.dma("sync", lambda e, ic=ic, f=f, r=r, roff=roff, coff=coff, ln=ln:
                                e.dma_start(out=wgt[:, ic % 2, f, coff:coff + ln],
                                            in_=w8gu_full[2 * r + 0, f, :, roff:roff + ln]), swp)
                        sch.dma("sync", lambda e, ic=ic, f=f, r=r, roff=roff, coff=coff, ln=ln:
                                e.dma_start(out=wut[:, ic % 2, f, coff:coff + ln],
                                            in_=w8gu_full[2 * r + 1, f, :, roff:roff + ln]), swp)
                for r, roff, coff, ln in gu_slices(ic):
                    sch.dma_nc("sync", mwg[0:1, ic % 2, coff:coff + ln],
                               wsc_full[3 * r + 0, roff:roff + ln].rearrange("(o d) -> o d", o=1), swp)
                    sch.dma_nc("sync", mwu[0:1, ic % 2, coff:coff + ln],
                               wsc_full[3 * r + 1, roff:roff + ln].rearrange("(o d) -> o d", o=1), swp)
                mload6 = swp.v
                sch.wait("tensor", swp, mload6)
                sch.wait("tensor", sv, m_or)
                if ic > 0:
                    sch.wait("tensor", sv, tm[(ic - 1, 7)])
                    sch.wait("tensor", se, em6[(ic - 1, 7)])
                mb1 = sch.op("tensor", lambda e, ic=ic: e.matmul(
                    ps[4][:, :], ones_row[:, :], mwg[0:1, ic % 2, :],
                    start=True, stop=True), sg)
                sch.wait("scalar", sg, mb1)
                mb1e = sch.op("scalar", lambda e, ic=ic: e.copy(mwg[:, ic % 2, :], ps[4][:, :]), se)
                sch.wait("tensor", se, mb1e)
                mb2 = sch.op("tensor", lambda e, ic=ic: e.matmul(
                    ps[4][:, :], ones_row[:, :], mwu[0:1, ic % 2, :],
                    start=True, stop=True), sg)
                sch.wait("scalar", sg, mb2)
                mb2e = sch.op("scalar", lambda e, ic=ic: e.copy(mwu[:, ic % 2, :], ps[4][:, :]), se)
                for tt in range(8):
                    bk = tt % 2
                    sch.wait("tensor", swp, mload6)
                    if ic == 0 and tt == 0:
                        sch.wait("tensor", se, marks_b0["qxT_done"])
                    key = (ic, tt - 2) if tt >= 2 else ((ic - 1, tt + 6) if ic > 0 else None)
                    if key is not None:
                        sch.wait("tensor", se, em6[key])
                    for f in range(16):
                        sch.op("tensor", lambda e, ic=ic, tt=tt, f=f, bk=bk: e.matmul(
                            ps[bk][:, :], qxT[:, f, 128 * tt:128 * (tt + 1)],
                            wgt[:, ic % 2, f, :], start=(f == 0), stop=(f == 15)),
                            sg if f == 15 else None)
                    for f in range(16):
                        sch.op("tensor", lambda e, ic=ic, tt=tt, f=f, bk=bk: e.matmul(
                            ps[2 + bk][:, :], qxT[:, f, 128 * tt:128 * (tt + 1)],
                            wut[:, ic % 2, f, :], start=(f == 0), stop=(f == 15)),
                            sg if f == 15 else None)
                    mgu = sg.v
                    sch.wait("scalar", sg, mgu)
                    sch.wait("scalar", se, mb2e)
                    if ic > 0 or tt >= 2:
                        sch.wait("scalar", sv, tm.get((ic, tt - 2), tm.get((ic - 1, tt + 6), 0)) or 0)
                    mg1 = sch.op("scalar", lambda e, tt=tt, bk=bk: e.activation(
                        gst[:, bk, :], ps[bk][:, :], AF.Copy, bias=0.0,
                        scale=mxd_sb[:, tt:tt + 1]), se)
                    mu1 = sch.op("scalar", lambda e, tt=tt, bk=bk: e.activation(
                        ust[:, bk, :], ps[2 + bk][:, :], AF.Copy, bias=0.0,
                        scale=mxd_sb[:, tt:tt + 1]), se)
                    em6[(ic, tt)] = mu1
                    sch.wait("vector", se, mu1)
                    mg2 = sch.op("vector", lambda e, ic=ic, bk=bk: e.tensor_mul(
                        gst[:, bk, :], gst[:, bk, :], mwg[:, ic % 2, :]), sv)
                    mu2 = sch.op("vector", lambda e, ic=ic, bk=bk: e.tensor_mul(
                        ust[:, bk, :], ust[:, bk, :], mwu[:, ic % 2, :]), sv)
                    sch.wait("scalar", sv, mg2)
                    ms = sch.op("scalar", lambda e, bk=bk: e.activation(
                        sil[:, bk, :], gst[:, bk, :], AF.Silu, bias=0.0, scale=1.0), se)
                    sch.wait("vector", se, ms)
                    sch.wait("vector", sv, mu2)
                    stp6 = (sems["sta"], sems["stb"])[bk]
                    if ic > 0 or tt >= 2:
                        sch.wait("vector", stp6, stp6.v)
                    mt_ = sch.op("vector", lambda e, bk=bk: e.tensor_mul(
                        tt6[:, bk, :], sil[:, bk, :], ust[:, bk, :]), sv)
                    tm[(ic, tt)] = mt_
                    sch.wait("vector", sv, mt_)
                    mr = sch.op("vector", lambda e, tt=tt, bk=bk, ic=ic: e.reduce_max(
                        mt6[:, tt, ic % 2:ic % 2 + 1], tt6[:, bk, :], axis=AXX,
                        apply_absolute_value=True), sv)
                    sch.wait("vector", sv, mr)
                    if ic == 0:
                        mm_ = sch.op("vector", lambda e, tt=tt: e.tensor_copy(
                            mt_sb[:, tt:tt + 1], mt6[:, tt, 0:1]), sv)
                    else:
                        mm_ = sch.op("vector", lambda e, tt=tt, ic=ic: e.tensor_max(
                            mt_sb[:, tt:tt + 1], mt_sb[:, tt:tt + 1],
                            mt6[:, tt, ic % 2:ic % 2 + 1]), sv)
                    mtm[(ic, tt)] = mm_
                    sch.wait("gpsimd", sv, mt_)
                    sch.dma("gpsimd", lambda e, ic=ic, tt=tt, bk=bk: e.dma_start(
                        out=t_scr[128 * tt:128 * (tt + 1), 512 * ic:512 * (ic + 1)],
                        in_=tt6[:, bk, :]), stp6)
                gum[ic] = sg.v
            marks_b0["t_done"] = (sems["sta"].v, sems["stb"].v)
            marks_b0["mt_done"] = {tt: mtm[(10, tt)] for tt in range(8)}
            sch.flush(blk)

        with (
            nc.sbuf_tensor("wdt6", [128, 44, D], F8) as wdt6,
            nc.sbuf_tensor("mwd6", [128, 4, 512], F32) as mwd6,
            nc.sbuf_tensor("mwr", [1, 4, 512], F32) as mwr,
            nc.sbuf_tensor("ones_row2", [1, 128], F32) as ones_row2,
            nc.sbuf_tensor("tld", [128, 2, 512], F32) as tld,
            nc.sbuf_tensor("qt8", [128, 2, 512], F8) as qt8,
            nc.sbuf_tensor("qtb", [128, 2, 512], BF16) as qtb,
            nc.sbuf_tensor("qtT", [128, 44, 128], F8) as qtT,
            nc.sbuf_tensor("st6", [128, 2, 8], F32) as st6,
            nc.sbuf_tensor("zo6", [128, 4, 512], F32) as zo6,
            nc.sbuf_tensor("x16", [128, 2, D], F32) as x16,
            nc.sbuf_tensor("sq6", [128, D], F32) as sq6,
            nc.sbuf_tensor("fo6", [128, 2, D], F32) as fo6,
            nc.Block("b6b") as blk,
        ):
            sch = Sched(nc)
            sch.wait("sync", sq, sq.v)
            m_or2 = sch.op("vector", lambda e: e.memset(ones_row2[:, :], 1.0), sv)
            for r in range(8):
                for ic in range(44):
                    sch.dma("sync", lambda e, r=r, ic=ic: e.dma_start(
                        out=wdt6[:, ic, 256 * r:256 * (r + 1)],
                        in_=w8d_full[44 * r + ic, :, :]), swd)
                sch.dma_nc("sync", mwr[0:1, r // 2, 256 * (r % 2):256 * (r % 2) + 256],
                           wsc_full[3 * r + 2, 0:256].rearrange("(o d) -> o d", o=1), swd)
            m_wd = swd.v
            sch.wait("tensor", swd, m_wd)
            sch.wait("tensor", sv, m_or2)
            mwde = {}
            for oc in range(4):
                sch.wait("tensor", se, se.v)
                mb = sch.op("tensor", lambda e, oc=oc: e.matmul(
                    ps[4][:, :], ones_row2[:, :], mwr[0:1, oc, :], start=True, stop=True), sg)
                sch.wait("scalar", sg, mb)
                mwde[oc] = sch.op("scalar", lambda e, oc=oc: e.copy(
                    mwd6[:, oc, :], ps[4][:, :]), se)
            qtev, dnm, zevm, zmulm, n3m, tldm = {}, {}, {}, {}, {}, {}
            qtbm_d, qttr_d = {}, {}
            sttm, fom = {}, {}
            x16m = {}
            for tt in range(8):
                pp = tt % 2
                sch.wait("vector", sv, mtm[(10, tt)])
                if tt >= 2:
                    sch.wait("vector", se, qtev[(tt - 2, 43)])
                n1 = sch.op("vector", lambda e, tt=tt, pp=pp: e.tensor_scalar_mul(
                    st6[:, pp, 0:1], mt_sb[:, tt:tt + 1], IQD), sv)
                sch.wait("vector", sv, n1)
                n2 = sch.op("vector", lambda e, pp=pp: e.reciprocal(
                    st6[:, pp, 1:2], st6[:, pp, 0:1]), sv)
                n3 = sch.op("vector", lambda e, tt=tt, pp=pp: e.tensor_scalar_mul(
                    st6[:, pp, 2:3], mt_sb[:, tt:tt + 1], DQF), sv)
                n3m[tt] = n3
                for ic in range(11):
                    gic = tt * 11 + ic
                    gp = gic % 2
                    stp = (st6a, st6b)[gp]
                    if gic >= 2:
                        sch.wait("sync", se, tldm[gic - 2])
                    if tt == 0 and ic == 0:
                        sch.wait("sync", sems["sta"], marks_b0["t_done"][0])
                        sch.wait("sync", sems["stb"], marks_b0["t_done"][1])
                    mld = sch.dma("sync", lambda e, tt=tt, ic=ic, gp=gp: e.dma_start(
                        out=tld[:, gp, :],
                        in_=t_scr[128 * tt:128 * (tt + 1), 512 * ic:512 * (ic + 1)]), stp)
                    sch.wait("scalar", stp, mld)
                    sch.wait("scalar", sv, n2)
                    if gic >= 2:
                        sch.wait("scalar", sv, qtbm_d.get(gic - 2, 0) or 0)
                    mq8 = sch.op("scalar", lambda e, gp=gp, pp=pp: e.activation(
                        qt8[:, gp, :], tld[:, gp, :], AF.Copy, bias=0.0,
                        scale=st6[:, pp, 1:2]), se)
                    tldm[gic] = mq8
                    sch.wait("vector", se, mq8)
                    if gic >= 2:
                        sch.wait("vector", sg, qttr_d.get(gic - 2, 0) or 0)
                    qtbm = sch.op("vector", lambda e, gp=gp: e.tensor_copy(
                        qtb[:, gp, :], qt8[:, gp, :]), sv)
                    qtbm_d[gic] = qtbm
                    for j in range(4):
                        icc = 4 * ic + j
                        sch.wait("tensor", sv, qtbm)
                        if (tt, icc) != (0, 0):
                            pk = icc - 2 if icc >= 2 else 42 + icc
                            ptt = tt if icc >= 2 else tt - 1
                            sch.wait("tensor", se, qtev.get((ptt, pk), 0) or 0)
                        if tt > 0:
                            sch.wait("tensor", sg, dnm.get((tt - 1, 3), 0) or 0)
                        mtr = sch.op("tensor", lambda e, gp=gp, icc=icc, j=j: e.transpose(
                            pT[icc % 2][:, 0:128],
                            qtb[:, gp, 128 * j:128 * (j + 1)], ident_sb[:, :]), sg)
                        sch.wait("scalar", sg, mtr)
                        qtev[(tt, icc)] = sch.op("scalar", lambda e, ic=ic, j=j, icc=icc:
                                                 e.copy(qtT[:, icc, :],
                                                        pT[icc % 2][:, 0:128]), se)
                    qttr_d[gic] = sg.v
                for oc in range(4):
                    sch.wait("tensor", se, qtev[(tt, 43)])
                    if tt == 0 and oc == 0:
                        sch.wait("tensor", swd, m_wd)
                        sch.wait("tensor", se, se.v)
                    if tt > 0:
                        sch.wait("tensor", se, zevm.get((tt - 1, oc), 0) or 0)
                    for ic in range(44):
                        sch.op("tensor", lambda e, tt=tt, oc=oc, ic=ic: e.matmul(
                            ps[oc][:, :], qtT[:, ic, :],
                            wdt6[:, ic, 512 * oc:512 * (oc + 1)],
                            start=(ic == 0), stop=(ic == 43)),
                            sg if ic == 43 else None)
                    dnm[(tt, oc)] = sg.v
                    sch.wait("scalar", sg, dnm[(tt, oc)])
                    sch.wait("scalar", se, mwde[3])
                    if tt >= 2:
                        sch.wait("scalar", sv, sttm[tt - 2])
                    sch.wait("scalar", sv, n3m[tt])
                    mze = sch.op("scalar", lambda e, tt=tt, oc=oc, pp=pp: e.activation(
                        zo6[:, oc, :], ps[oc][:, :], AF.Copy, bias=0.0,
                        scale=st6[:, pp, 2:3]), se)
                    zevm[(tt, oc)] = mze
                    sch.wait("vector", se, mze)
                    zmulm[(tt, oc)] = sch.op("vector", lambda e, oc=oc: e.tensor_mul(
                        zo6[:, oc, :], zo6[:, oc, :], mwd6[:, oc, :]), sv)
                sx6p = (sx6a, sx6b)[pp]
                if tt >= 2:
                    sch.wait("sync", sv, sttm[tt - 2])
                x16m[tt] = sch.dma("sync", lambda e, tt=tt, pp=pp: e.dma_start(
                    out=x16[:, pp, :], in_=x1_scr[128 * tt:128 * (tt + 1), :]), sx6p)
                sch.wait("vector", sx6p, x16m[tt])
                sch.wait("vector", sv, zmulm[(tt, 3)])
                if tt >= 2:
                    sch.wait("vector", se, fom[tt - 2])
                    sch.wait("vector", sout, sout.v)
                mstt = sch.op("vector", lambda e, tt=tt, pp=pp: e.scalar_tensor_tensor(
                    fo6[:, pp, :], zo6[:, :, :].rearrange("p a b -> p (a b)"), 1.0,
                    x16[:, pp, :], ALU.mult, ALU.add, accum_out=st6[:, pp, 3:4]), sv)
                sttm[tt] = mstt
                sch.wait("vector", sv, mstt)
                k3 = sch.op("vector", lambda e, pp=pp: e.tensor_scalar_mul(
                    st6[:, pp, 4:5], st6[:, pp, 3:4], -IND), sv)
                sch.wait("scalar", sv, k3)
                k4 = sch.op("scalar", lambda e, pp=pp: e.activation(
                    sq6[:, :], fo6[:, pp, :], AF.Square, bias=st6[:, pp, 4:5], scale=1.0,
                    accum_out=st6[:, pp, 5:6]), se)
                sch.wait("scalar", se, k4)
                k5 = sch.op("scalar", lambda e, pp=pp: e.activation(
                    st6[:, pp, 6:7], st6[:, pp, 5:6], AF.Sqrt, bias=EPS, scale=IND), se)
                sch.wait("vector", se, k5)
                k6 = sch.op("vector", lambda e, pp=pp: e.reciprocal(
                    st6[:, pp, 7:8], st6[:, pp, 6:7]), sv)
                sch.wait("vector", sv, k6)
                k7 = sch.op("vector", lambda e, pp=pp: e.tensor_mul(
                    st6[:, pp, 4:5], st6[:, pp, 4:5], st6[:, pp, 7:8]), sv)
                sch.wait("scalar", sv, k7)
                fom[tt] = sch.op("scalar", lambda e, tt=tt, pp=pp: e.activation(
                    fo6[:, pp, :], fo6[:, pp, :], AF.Identity,
                    bias=st6[:, pp, 4:5], scale=st6[:, pp, 7:8]), se)
                sch.wait("gpsimd", se, fom[tt])
                sch.dma("gpsimd", lambda e, tt=tt, pp=pp: e.dma_start(
                    out=out_own[128 * tt:128 * (tt + 1), :], in_=fo6[:, pp, :]), sout)
            if dbg_phase == 2:
                sch.wait("gpsimd", sq, sq.v)
                sch.wait("gpsimd", sto, sto.v)
                sch.wait("gpsimd", sout, sout.v)
                for a, b in (("d_x1", x1_scr), ("d_t", t_scr)):
                    sch.dma("gpsimd", lambda e, a=a, b=b: e.dma_start(out=dbg[a], in_=b), sq)
                sch.wait("gpsimd", sv, sv.v)
                sch.dma("gpsimd", lambda e: e.dma_start(out=dbg["d_mt"], in_=mt_sb[:, :]), sq)
                sch.dma("gpsimd", lambda e: e.dma_start(out=dbg["d_qxT"], in_=qxT[:, :, :]), sq)
            sch.wait("gpsimd", sout, sout.v)
            sch.wait("gpsimd", sq, sq.v)
            sch.flush(blk)
    return nc


def make_inmaps(inputs):
    import ml_dtypes
    x = np.ascontiguousarray(np.asarray(inputs["L0_attn_x"], dtype=np.float32).reshape(S, D))
    xt = np.ascontiguousarray(x.T)
    WQ = np.asarray(inputs["L0_attn_WQ_w"], dtype=np.float32)
    WK = np.asarray(inputs["L0_attn_WK_w"], dtype=np.float32)
    WV = np.asarray(inputs["L0_attn_WV_w"], dtype=np.float32)
    WO = np.asarray(inputs["L0_attn_WO_w"], dtype=np.float32)
    bq = np.asarray(inputs["L0_attn_WQ_b"], dtype=np.float32)
    bk = np.asarray(inputs["L0_attn_WK_b"], dtype=np.float32)
    bv = np.asarray(inputs["L0_attn_WV_b"], dtype=np.float32)
    bo = np.asarray(inputs["L0_attn_WO_b"], dtype=np.float32)
    wg = np.ascontiguousarray(np.asarray(inputs["L0_ffn_w_gate"], dtype=np.float32))
    wu = np.ascontiguousarray(np.asarray(inputs["L0_ffn_w_up"], dtype=np.float32))
    wd = np.ascontiguousarray(np.asarray(inputs["L0_ffn_w_down"], dtype=np.float32))
    dn = np.asarray(inputs["L0_ffn_down_reorder_index"], dtype=np.int32)
    WOT = np.ascontiguousarray(WO.T)
    bo_b = np.ascontiguousarray(np.broadcast_to(bo, (128, D)))

    mask = np.zeros((4, 128, 512), dtype=np.float32)
    for j in range(4):
        for k in range(128):
            for q in range(512):
                if 128 * j + k > q:
                    mask[j, k, q] = -1e9
    import ml_dtypes as md
    ident = np.eye(128, dtype=md.bfloat16)

    in_maps = []
    for c in range(N_CORES):
        fsl = slice(OSH * c, OSH * (c + 1))
        tsl = slice(TOK * c, TOK * (c + 1))
        b3c = np.stack([bq[fsl], bk[fsl], bv[fsl]])
        b3c = np.ascontiguousarray(b3c.reshape(3, 2, 128).transpose(2, 0, 1))
        in_maps.append({
            "xt": xt,
            "x_own": np.ascontiguousarray(x[tsl]),
            "wq": np.ascontiguousarray(WQ[fsl]),
            "wk": np.ascontiguousarray(WK[fsl]),
            "wv": np.ascontiguousarray(WV[fsl]),
            "b3": b3c,
            "wo_t": np.ascontiguousarray(WOT[fsl]),
            "bo_b": bo_b,
            "wg_in": wg,
            "wu_in": wu,
            "wd_rows": np.ascontiguousarray(wd[fsl]),
            "dn_shard": np.ascontiguousarray(dn[ISH * c:ISH * (c + 1)]),
            "masks": mask,
            "identb": ident,
        })
    return in_maps


def kernel(**inputs):
    nc = build_nc(dbg_phase=0)
    in_maps = make_inmaps(inputs)
    res = run_bass_kernel_spmd(nc, in_maps, core_ids=list(range(N_CORES)))
    out = np.concatenate([r["out_own"] for r in res.results], axis=0)
    return out.reshape(1, S, D)
